# version 12
# baseline (speedup 1.0000x reference)
import math
import os
from contextlib import ExitStack

import numpy as np
import ml_dtypes

import concourse.bass as bass
import concourse.mybir as mybir
from concourse.ap import AP
from concourse.bass_core_rust import IndirectOffsetOnAxis
from concourse.bass_utils import run_bass_kernel_spmd

F32 = mybir.dt.float32
BF16 = mybir.dt.bfloat16
I16 = mybir.dt.int16
I32 = mybir.dt.int32
AF = mybir.ActivationFunctionType
ALU = mybir.AluOpType
AX = mybir.AxisListType.X
bfloat16 = ml_dtypes.bfloat16

NCORE = 8
N = 4096
D = 1024
H = 16
DH = 64
II = 4096
E = 8
T = NCORE * N
CAP = 10240
YROWS = CAP + 128
FBROW = 81920
NB = 320
NREL = 2 * N - 1
BRP = NREL + 257
EPS = 1e-5
NT = N // 128
FAR_ABS = 789
NDELTA = 10
DBG = [x for x in os.environ.get("MK_DEBUG", "").split(",") if x]
DRAIN = os.environ.get("MK_DRAIN", "1") != "0"


def _near_pair(jt, it):
    d = abs(jt - it)
    return d == 0 or (d - 1) * 128 + 1 < FAR_ABS


def _tile_region(jt, ic):
    if any(_near_pair(jt, ic * 4 + s) for s in range(4)):
        return 0
    return 1 if jt > ic * 4 else 2


def _bucket_table():
    r = np.arange(-(N - 1), N, dtype=np.int64)
    nb = NB // 2
    b = (r > 0).astype(np.int64) * nb
    ra = np.abs(r)
    mx = nb // 2
    small = ra < mx
    lg = np.log(np.maximum(ra, 1).astype(np.float64) / mx) \
        / math.log(800.0 / mx) * (nb - mx)
    lg = np.minimum((mx + lg).astype(np.int64), nb - 1)
    return (b + np.where(small, ra, lg)).astype(np.int64)


class Ctr:
    def __init__(self, sem, step):
        self.sem, self.step, self.val = sem, step, 0


EMITTED = {}


class Bld:
    ENG = ("sync", "gpsimd", "scalar", "vector", "tensor")

    def __init__(self, nc, es):
        self.nc, self.es = nc, es
        self.q = {e: [] for e in self.ENG}
        self.ctr = {}
        self.tag = ""

    def counter(self, name, step):
        sem = self.es.enter_context(self.nc.semaphore(name))
        self.ctr[name] = Ctr(sem, step)
        return self.ctr[name]

    def E(self, eng, meth, *args, inc=None, drain=None, **kw):
        pairs = []
        if inc is not None:
            for c in (inc if isinstance(inc, (list, tuple)) else [inc]):
                c.val += c.step
                pairs.append((c.sem, c.step))
        tag = f"{self.tag}|{eng}.{meth}#{len(self.q[eng])}"
        if drain is None:
            drain = (DRAIN and eng in ("vector", "scalar", "gpsimd")
                     and meth not in ("dma_start", "indirect_dma_start",
                                      "collective_compute",
                                      "dma_start_transpose"))
        else:
            drain = drain and DRAIN

        def f(h, meth=meth, args=args, kw=kw, pairs=pairs, tag=tag,
              drain=drain):
            inst = getattr(h, meth)(*args, **kw)
            try:
                EMITTED[inst.ins.name] = tag
            except Exception:
                pass
            for sem, step in pairs:
                inst = inst.then_inc(sem, step)
            if drain:
                h.drain()
        self.q[eng].append(f)

    def W(self, eng, c, val=None):
        v = c.val if val is None else val
        if v <= 0:
            return

        def f(h, sem=c.sem, v=v):
            h.wait_ge(sem, v)
        self.q[eng].append(f)

    def WT(self, eng, tok):
        if tok is not None:
            self.W(eng, tok[0], tok[1])

    def raw(self, eng, fn):
        self.q[eng].append(fn)


ARENA_BYTES = 168 * 1024
OF_VAUG = 0
OF_GCOL = OF_VAUG + NT * H * 65 * 2
OF_GCOLR = OF_GCOL + NT * H * 4
OF_WPM = OF_GCOLR + NT * H * 4
PERS_END = OF_WPM + NT * H * 2 * 4
OF_STG = PERS_END
OF_STGB = OF_STG + 2 * D * 4
OF_XA = OF_STGB + 2 * D * 2
OF_XTC = OF_XA + 2 * D * 4
OF_WSTR = OF_XTC + 8 * 512 * 2
OF_G1H = OF_WSTR + 2 * 8 * 128 * 2
P01_END = OF_G1H + 8192 * 2
OF_KTB = OF_XTC
OF_QPAD = OF_KTB + 2 * 4096 * 2
OF_A1T = OF_QPAD + 4 * 512 * 2
OF_DIAG = OF_A1T + 2 * 21 * 128 * 2
OF_EXPS = OF_DIAG + 2 * 4 * 128 * 2
OF_RCMB = OF_EXPS + 4 * 512 * 2
OF_ATTS = OF_RCMB + 1024 * 4
ATT_END = OF_ATTS + 2 * 512 * 2
assert ATT_END <= ARENA_BYTES and P01_END <= ARENA_BYTES
OF_ATTLD = OF_KTB
OF_RTR = OF_ATTLD + 2 * 8 * 128 * 2
OF_X1T32 = OF_RTR + 256 * 4
OF_WMST = OF_X1T32 + D * 4
OF_MSB = OF_VAUG
OF_POSS = OF_MSB + 8 * 256 * 2
OF_KEPT = OF_POSS + 8 * 256 * 4
OF_SCR = OF_KEPT + 8 * 256 * 4
OF_IDXA = OF_SCR + 8 * 256 * 4
OF_ELIGB = OF_IDXA + 16 * 256 * 4
OF_BBF = OF_ELIGB + 256 * 2
OF_ONESR = OF_BBF + 256 * 2
OF_CSROW = OF_ONESR + 256 * 4
SCAN_END = OF_CSROW + 512 * 4
assert SCAN_END <= OF_STG
OF_WA = 0
OF_WB = OF_WA + 8 * 4096 * 2
OF_HTS = OF_WB + 32 * 1024 * 2
OF_XET = OF_HTS + 32 * 256 * 2
OF_FSB = OF_XET + 2 * 8 * 256 * 2
OF_SCST = OF_FSB + 2 * 1024 * 2
FFN_END = OF_SCST + 6 * 1024 * 2
assert FFN_END <= ARENA_BYTES
OF_LOC = 0
OF_LI32 = OF_LOC + 8 * 32 * 4
OF_G0 = OF_LI32 + 64 * 4
OF_G1R = OF_G0 + 2 * D * 2
OF_YFB = OF_G1R + 2 * D * 2
OF_CX1 = OF_YFB + 2 * D * 2
OF_CST = OF_CX1 + 2 * D * 4
CMB_END = OF_CST + 2 * D * 4


def build_program():
    nc = bass.Bass("TRN2", target_bir_lowering=False, debug=False,
                   detect_race_conditions=False)

    def din(name, shape, dt):
        return nc.dram_tensor(name, list(shape), dt, kind="ExternalInput").ap()

    x_in = din("x", (N, D), F32)
    w_in = {n: din(n, (D, D), F32) for n in ("wq", "wk", "wv", "wo")}
    bq_i = din("bq", (D,), F32)
    bk_i = din("bk", (D,), F32)
    bv_i = din("bv", (D,), F32)
    bo_i = din("bo", (D,), F32)
    gru_w = din("gru_w", (8, DH), F32)
    gru_b = din("gru_b", (1, 8), F32)
    gru_c = din("gru_c", (48, 1), F32)
    rel_e = din("rel_embed", (NB, H), F32)
    wr_i = din("wr", (E, D), F32)
    w1e_i = din("w1e", (D, II), F32)
    b1e_i = din("b1e", (II,), F32)
    w2e_i = din("w2e", (II, D), F32)
    b2e_i = din("b2e", (D,), F32)
    fw1_i = din("fw1", (D, II), F32)
    fb1_i = din("fb1", (II,), F32)
    fw2_i = din("fw2", (II, D), F32)
    fb2_i = din("fb2", (D,), F32)
    idn_i = din("idn", (128, 128), BF16)
    idn32_i = din("idn32", (128, 128), F32)
    ltx_i = din("ltx", (128, 128), BF16)
    anti_i = din("anti", (128, 128), BF16)
    onez_i = din("onez", (1, 128), BF16)
    onez32_i = din("onez32", (1, 128), F32)
    g1h_i = din("g1h", (384, NREL), BF16)
    eidx_i = din("eidx", (128, 8), F32)
    coreoh_i = din("coreoh", (1, 8), F32)

    out = nc.dram_tensor("out", [N, D], F32, kind="ExternalOutput").ap()

    dbg_specs = {
        "qt": ((D, N), BF16), "kt": ((D, N), BF16),
        "vaug": ((128, NT * H * 65), BF16), "gcol": ((128, NT * H), F32),
        "gcolr": ((128, NT * H), F32), "wpm": ((128, NT * H * 2), F32),
        "biasrel": ((H, NREL), BF16), "x1": ((N, D), F32),
        "attnT": ((D, N), BF16), "wm": ((E, N), BF16),
        "pos": ((128, 8 * 256), F32), "kept": ((128, 8 * 256), F32),
        "rso": ((160, 128), F32), "slotc": ((128, 256), F32),
        "yfb": ((N, D), BF16), "ye": ((YROWS, D), BF16),
        "xe": ((CAP + 1, D), BF16), "idxa": ((128, 16 * 256), F32),
    }
    dbg = {}
    for name in DBG:
        shp, dt = dbg_specs[name]
        dbg[name] = nc.dram_tensor("dbg_" + name, list(shp), dt,
                                   kind="ExternalOutput").ap()

    def hbm(name, shape, dt):
        return nc.dram_tensor(name, list(shape), dt, kind="Internal").ap()

    biasrel_h = hbm("biasrel", (H, BRP), BF16)
    wT_h = {n: hbm(n + "T", (D, D), BF16) for n in ("wq", "wk", "wv", "wo")}
    qt_h = hbm("qt", (D, N), BF16)
    kt_h = hbm("kt", (D, N), BF16)
    attnT_h = hbm("attnT", (D, N), BF16)
    x1_h = hbm("x1f", (N, D), F32)
    x1bf_h = hbm("x1bf", (N, D), BF16)
    x1g_h = hbm("x1g", (T, D), BF16)
    x1t_h = hbm("x1t", (D, N), BF16)
    wmT_h = hbm("wmT", (E, N), BF16)
    wall_h = hbm("wall", (NCORE * E, N), BF16)
    rsi_h = hbm("rs_in", (NCORE * 160, 128), F32)
    rso_h = hbm("rs_out", (160, 128), F32)
    xe_h = hbm("xe", (CAP + 1, D), BF16)
    ht_h = hbm("ht", (II, CAP), BF16)
    hft_h = hbm("hft", (II, N), BF16)
    ye_h = hbm("ye", (YROWS, D), BF16)
    yall_h = hbm("yall", (NCORE * YROWS, D), BF16)
    yfb_h = hbm("yfb", (N, D), BF16)

    es = ExitStack()

    def S(name, shape, dt):
        return es.enter_context(nc.sbuf_tensor(name, list(shape), dt))

    bq_s = S("bq_s", (128, 8), F32)
    bk_s = S("bk_s", (128, 8), F32)
    bv_bc = S("bv_bc", (128, D), F32)
    bo_bc = S("bo_bc", (128, D), F32)
    b2_bc = S("b2_bc", (128, D), F32)
    fb2_bc = S("fb2_bc", (128, D), F32)
    b1_s = S("b1_s", (128, II // 128), F32)
    fb1_s = S("fb1_s", (128, II // 128), F32)
    gwa = S("gwa", (128, 8 * 48), BF16)
    gba = S("gba", (48, 1), F32)
    gruc_s = S("gruc_s", (48, 1), F32)
    bpm_bc = S("bpm_bc", (128, 2 * H), F32)
    idn_s = S("idn_s", (128, 128), BF16)
    idn32_s = S("idn32_s", (128, 128), F32)
    ltx_s = S("ltx_s", (128, 128), BF16)
    anti_s = S("anti_s", (128, 128), BF16)
    onez_s = S("onez_s", (1, 128), BF16)
    onez32_s = S("onez32_s", (1, 128), F32)
    onecol_s = S("onecol_s", (128, 8), BF16)
    eidx_s = S("eidx_s", (128, 8), F32)
    coreoh_bc = S("coreoh_bc", (128, 8), F32)
    ones8_s = S("ones8_s", (128, 8), F32)
    zrow_s = S("zrow_s", (1, D), BF16)
    wrT_s = S("wrT_s", (128, 8 * E), F32)
    slotc_s = S("slotc_s", (128, 256), F32)
    slotc32_s = S("slotc32_s", (128, 256), I32)
    rel_s = S("rel_s", (128, 16), BF16)
    b2row_s = S("b2row_s", (1, D), BF16)
    fb2row_s = S("fb2row_s", (1, D), BF16)
    small_s = S("small_s", (128, 64), F32)

    arena = S("arena", (128, ARENA_BYTES), mybir.dt.uint8)

    def A(off, dt, cols, rows=128, row0=0):
        esz = {F32: 4, BF16: 2, I32: 4, I16: 2}[dt]
        v = arena.bitcast(dt)
        return v[row0:row0 + rows, off // esz:off // esz + cols]

    pp = [es.enter_context(nc.psum_tensor(f"pp{i}", [128, 512], F32))
          for i in range(8)]

    def ppb(i):
        return pp[i].bitcast(BF16)

    b = Bld(nc, es)
    qd = b.counter("qd", 16)
    gd = b.counter("gd", 16)
    sd = b.counter("sd", 16)
    pe = b.counter("pe", 1)
    ac = b.counter("ac", 1)
    dv = b.counter("dv", 1)
    cc = b.counter("cc", 1)
    od = b.counter("od", 16)
    E_, W_ = b.E, b.W

    def stg(slot):
        return A(OF_STG + slot * D * 4, F32, D)

    def stgb(slot):
        return A(OF_STGB + slot * D * 2, BF16, D)

    def xa(slot):
        return A(OF_XA + slot * D * 4, F32, D)

    b.tag = "P0"
    CH = {}

    CHAL = {}
    for _s in range(2):
        for _q in range(4):
            for _fam in ("fx", "hl", "ex"):
                CHAL["%s%d_%d" % (_fam, _s, _q)] = "xq%d_%d" % (_s, _q)
    CHAL.update({"hs2": "kt0", "hs3": "kt1"})

    def ch(name):
        name = CHAL.get(name, name)
        if name not in CH:
            CH[name] = b.counter("c_" + name, 16)
        return CH[name]

    def dma(eng, name, out, in_, **kw):
        c = ch(name)
        E_(eng, "dma_start", out=out, in_=in_, inc=c, **kw)
        return c.val

    for dst, srcp in ((idn_s, idn_i), (idn32_s, idn32_i), (ltx_s, ltx_i),
                      (anti_s, anti_i), (onez_s, onez_i),
                      (onez32_s, onez32_i), (eidx_s, eidx_i),
                      (gruc_s, gru_c)):
        dma("sync", "k0", dst[:, :], srcp[:, :])
    dma("sync", "k0", stgb(1).bitcast(F32)[0:8, 0:128],
        AP(bq_i.tensor, 0, [[128, 8], [1, 128]]))
    dma("sync", "k0", stgb(1).bitcast(F32)[0:8, 128:256],
        AP(bk_i.tensor, 0, [[128, 8], [1, 128]]))
    dma("sync", "k0", A(OF_XTC, F32, 256)[0:32, 0:128],
        AP(b1e_i.tensor, 0, [[128, 32], [1, 128]]))
    dma("sync", "k0", A(OF_XTC, F32, 256)[0:32, 128:256],
        AP(fb1_i.tensor, 0, [[128, 32], [1, 128]]))
    dma("sync", "k0", small_s[0:1, 0:16],
        AP(rel_e.tensor, 159 * H, [[H, 1], [1, H]]))
    dma("sync", "k0", small_s[0:1, 16:32],
        AP(rel_e.tensor, 319 * H, [[H, 1], [1, H]]))
    dma("sync", "k0", small_s[0:1, 32:40], coreoh_i[:, :])
    dma("sync", "k0", small_s[0:1, 40:48], gru_b[:, :])
    for g in range(4):
        dma("sync", "k0", stg(0)[0:2, g * DH:(g + 1) * DH],
            AP(gru_w.tensor, g * DH, [[4 * DH, 2], [1, DH]]))
    dma("sync", "k0", stg(1)[0:1, 0:512],
        AP(bv_i.tensor, 0, [[D, 1], [1, 512]]))
    dma("sync", "k0", xa(0)[0:1, 0:512],
        AP(bv_i.tensor, 512, [[D, 1], [1, 512]]))
    dma("sync", "k0", stg(1)[0:1, 512:1024],
        AP(bo_i.tensor, 0, [[D, 1], [1, 512]]))
    dma("sync", "k0", xa(0)[0:1, 512:1024],
        AP(bo_i.tensor, 512, [[D, 1], [1, 512]]))
    dma("gpsimd", "b2r", b2row_s[0:1, :],
        AP(b2e_i.tensor, 0, [[D, 1], [1, 1024]]))
    dma("gpsimd", "b2r", fb2row_s[0:1, :],
        AP(fb2_i.tensor, 0, [[D, 1], [1, 1024]]))
    dma("sync", "k0", xa(1)[0:1, 0:512],
        AP(b2e_i.tensor, 0, [[D, 1], [1, 512]]))
    dma("sync", "k0", xa(1)[0:1, 512:1024],
        AP(b2e_i.tensor, 512, [[D, 1], [1, 512]]))
    dma("sync", "k0", stg(0)[0:1, 512:1024],
        AP(fb2_i.tensor, 0, [[D, 1], [1, 512]]))
    dma("sync", "k0", stg(0)[0:1, 256 + 768:1024 + 256],
        AP(fb2_i.tensor, 512, [[D, 1], [1, 512]])) if False else None
    dma("sync", "k0", xa(0).bitcast(F32)[0:1, 1024 + 0:1024 + 0], None) \
        if False else None
    fb2hi_row = A(OF_XTC + 1024, F32, 512)
    dma("sync", "k0", fb2hi_row[0:1, :],
        AP(fb2_i.tensor, 512, [[D, 1], [1, 512]]))
    k0 = ch("k0")
    E_("vector", "memset", zrow_s[:, :], 0.0)
    E_("vector", "memset", onecol_s[:, :], 1.0)
    E_("vector", "memset", ones8_s[:, :], 1.0)
    E_("vector", "memset", gba[:, :], 0.0, inc=dv)
    dv_gba0 = dv.val

    W_("tensor", k0)
    E_("tensor", "matmul", pp[0][:, :], onez32_s[:, :], stg(1)[0:1, 0:512],
       start=True, stop=True)
    E_("tensor", "matmul", pp[1][:, :], onez32_s[:, :], xa(0)[0:1, 0:512],
       start=True, stop=True)
    E_("tensor", "matmul", pp[2][:, :], onez32_s[:, :], stg(1)[0:1, 512:1024],
       start=True, stop=True)
    E_("tensor", "matmul", pp[3][:, :], onez32_s[:, :], xa(0)[0:1, 512:1024],
       start=True, stop=True)
    E_("tensor", "matmul", pp[4][:, :], onez32_s[:, :], xa(1)[0:1, 0:512],
       start=True, stop=True)
    E_("tensor", "matmul", pp[5][:, :], onez32_s[:, :], xa(1)[0:1, 512:1024],
       start=True, stop=True)
    E_("tensor", "matmul", pp[6][:, :], onez32_s[:, :], stg(0)[0:1, 512:1024],
       start=True, stop=True)
    E_("tensor", "matmul", pp[7][:, :], onez32_s[:, :], fb2hi_row[0:1, :],
       start=True, stop=True)
    E_("tensor", "matmul", pp[0][:, :], onez32_s[:, :], stg(1)[0:1, 0:512],
       start=False, stop=True) if False else None
    E_("tensor", "matmul", pp[7][:, 256:288], onez32_s[:, :],
       small_s[0:1, 0:32], start=True, stop=True)
    E_("tensor", "matmul", pp[7][:, 288:296], onez32_s[:, :],
       small_s[0:1, 32:40], start=True, stop=True, inc=pe)
    W_("scalar", pe, pe.val)
    E_("scalar", "copy", bv_bc[:, 0:512], pp[0][:, :])
    E_("scalar", "copy", bv_bc[:, 512:1024], pp[1][:, :])
    E_("scalar", "copy", bo_bc[:, 0:512], pp[2][:, :])
    E_("scalar", "copy", bo_bc[:, 512:1024], pp[3][:, :])
    E_("scalar", "copy", b2_bc[:, 0:512], pp[4][:, :])
    E_("scalar", "copy", b2_bc[:, 512:1024], pp[5][:, :])
    E_("scalar", "copy", fb2_bc[:, 0:512], pp[6][:, :])
    E_("scalar", "copy", fb2_bc[:, 512:1024], pp[7][:, 0:512])
    E_("scalar", "copy", bpm_bc[:, :], pp[7][:, 256:288])
    E_("scalar", "copy", coreoh_bc[:, :], pp[7][:, 288:296], inc=ac)
    ac_bc = ac.val

    W_("tensor", ac, ac_bc)
    E_("tensor", "matmul", pp[0][:, :], onez32_s[:, :], fb2hi_row[0:1, :],
       start=True, stop=True, inc=pe)
    W_("scalar", pe, pe.val)
    E_("scalar", "copy", fb2_bc[:, 512:1024], pp[0][:, :], inc=ac)
    ac_bc = ac.val

    W_("tensor", ac, ac_bc)
    for (srcap, rows, dstap, bscale) in (
            (stgb(1).bitcast(F32)[0:8, 0:128], 8, bq_s[:, 0:8], 0.125),
            (stgb(1).bitcast(F32)[0:8, 128:256], 8, bk_s[:, 0:8], 1.0),
            (A(OF_XTC, F32, 256)[0:32, 0:128], 32, b1_s[:, 0:32], 1.0),
            (A(OF_XTC, F32, 256)[0:32, 128:256], 32, fb1_s[:, 0:32], 1.0)):
        E_("tensor", "transpose", pp[3][:, 0:rows], srcap,
           idn32_s[0:rows, 0:rows], inc=pe)
        W_("scalar", pe, pe.val)
        E_("scalar", "activation", dstap, pp[3][:, 0:rows], AF.Identity,
           scale=bscale, inc=ac)
        W_("tensor", ac, ac.val)

    W_("vector", k0)
    E_("vector", "reduce_sum", small_s[0:1, 56:57], small_s[0:1, 40:44],
       axis=AX)
    E_("vector", "reduce_sum", small_s[0:1, 57:58], small_s[0:1, 44:48],
       axis=AX, inc=dv)
    W_("tensor", dv, dv.val)
    E_("tensor", "matmul", pp[2][:, 0:2], onez32_s[:, :],
       small_s[0:1, 56:58], start=True, stop=True, inc=pe)
    W_("scalar", pe, pe.val)
    W_("scalar", dv, dv_gba0)
    E_("scalar", "copy", gba[0:16, 0:1], pp[2][0:16, 0:1])
    E_("scalar", "copy", gba[32:48, 0:1], pp[2][32:48, 1:2], inc=ac)

    E_("vector", "tensor_add", stg(0)[0:2, 256:320], stg(0)[0:2, 0:64],
       stg(0)[0:2, 64:128])
    E_("vector", "tensor_add", stg(0)[0:2, 320:384], stg(0)[0:2, 128:192],
       stg(0)[0:2, 192:256])
    E_("vector", "tensor_add", stg(0)[0:2, 384:448], stg(0)[0:2, 256:320],
       stg(0)[0:2, 320:384])
    E_("vector", "tensor_copy", stgb(0)[0:2, 0:64], stg(0)[0:2, 384:448],
       inc=dv)
    W_("tensor", dv, dv.val)
    W_("tensor", ac, ac.val)
    E_("tensor", "transpose", ppb(3)[0:64, 0:2], stgb(0)[0:2, 0:64],
       idn_s[0:2, 0:2], inc=pe)
    W_("scalar", pe, pe.val)
    E_("scalar", "copy", stgb(0)[0:64, 64:66], ppb(3)[0:64, 0:2], inc=ac)
    ac_g2 = ac.val
    E_("vector", "memset", gwa[:, :], 0.0)
    W_("vector", ac, ac_g2)
    for h in range(H):
        kt_i, r0 = h // 2, (h % 2) * 64
        E_("vector", "tensor_copy",
           gwa[r0:r0 + 64, kt_i * 48 + h:kt_i * 48 + h + 1],
           stgb(0)[0:64, 64:65])
        E_("vector", "tensor_copy",
           gwa[r0:r0 + 64, kt_i * 48 + 32 + h:kt_i * 48 + 32 + h + 1],
           stgb(0)[0:64, 65:66], inc=(dv if h == H - 1 else None))

    conv_done = [dv.val, dv.val]
    tpose_done = [ac.val, ac.val]
    bank_free = [pe.val] * 8
    for _bi in range(8):
        bank_free[_bi] = ac.val
    xa_guard_pe = pe.val
    wst_ch = [0, 0]
    for wi, (wn, scale) in enumerate(
            (("wq", 0.125), ("wk", 1.0), ("wv", 1.0), ("wo", 1.0))):
        for rc in range(8):
            slot = (wi * 8 + rc) % 2
            W_("sync", dv, conv_done[slot])
            if wi == 0 and rc < 2:
                W_("sync", pe, xa_guard_pe)
            ldv = dma("sync", "wx%d" % slot, xa(slot)[:, :],
                      w_in[wn][rc * 128:(rc + 1) * 128, :])
            W_("vector", ch("wx%d" % slot), ldv)
            W_("vector", ac, tpose_done[slot])
            E_("vector", "tensor_scalar", stgb(slot)[:, :], xa(slot)[:, :],
               scale, None, ALU.mult, inc=dv)
            conv_done[slot] = dv.val
            W_("tensor", dv, conv_done[slot])
            for ccc in range(8):
                bk2 = 4 + (ccc % 2)
                W_("tensor", ac, bank_free[bk2])
                E_("tensor", "transpose", ppb(bk2)[:, 0:128],
                   stgb(slot)[:, ccc * 128:(ccc + 1) * 128], idn_s[:, :],
                   inc=pe)
                dstc = A(OF_WSTR + (ccc % 2) * 8 * 128 * 2, BF16, 128)
                W_("scalar", pe, pe.val)
                W_("scalar", ch("ws%d" % (ccc % 2)), wst_ch[ccc % 2])
                E_("scalar", "copy", dstc[:, :], ppb(bk2)[:, 0:128], inc=ac)
                bank_free[bk2] = ac.val
                if ccc == 7:
                    tpose_done[slot] = ac.val
                W_("sync", ac, ac.val)
                wst_ch[ccc % 2] = dma(
                    "sync", "ws%d" % (ccc % 2),
                    wT_h[wn][ccc * 128:(ccc + 1) * 128,
                             rc * 128:(rc + 1) * 128], dstc[:, :])
    W_("sync", dv, dv.val)
    W_("sync", pe, pe.val)
    wrld = dma("sync", "k0", stg(0)[0:8, 0:D], wr_i[:, :])
    W_("tensor", ch("k0"), wrld)
    for ccc in range(8):
        bk2 = 4 + (ccc % 2)
        W_("tensor", ac, bank_free[bk2])
        E_("tensor", "transpose", pp[bk2][:, 0:8],
           stg(0)[0:8, ccc * 128:(ccc + 1) * 128], idn32_s[0:8, 0:8], inc=pe)
        W_("scalar", pe, pe.val)
        E_("scalar", "copy", wrT_s[:, ccc * 8:(ccc + 1) * 8], pp[bk2][:, 0:8],
           inc=ac)
        bank_free[bk2] = ac.val

    g1hv = A(OF_G1H, BF16, NREL)
    for half in range(2):
        for kc3 in range(3):
            W_("sync", pe, pe.val)
            g1ld = dma("sync", "g1", g1hv[:, :],
                       g1h_i[kc3 * 128:(kc3 + 1) * 128, :])
            rows = 128 if kc3 < 2 else 64
            W_("sync", dv, dv.val)
            rld = dma("sync", "rl", stg(1)[0:rows, 0:16],
                      rel_e[kc3 * 128:kc3 * 128 + rows, :])
            W_("vector", ch("rl"), rld)
            W_("vector", pe, pe.val)
            if kc3 == 2:
                E_("vector", "memset", rel_s[:, :], 0.0)
            E_("vector", "tensor_copy", rel_s[0:rows, 0:16],
               stg(1)[0:rows, 0:16], inc=dv)
            W_("tensor", dv, dv.val)
            W_("tensor", ch("g1"), g1ld)
            if kc3 == 0:
                for _b2 in range(8):
                    W_("tensor", ac, bank_free[_b2])
            for nch in range(8):
                ncol = half * 8 + nch
                n0 = ncol * 512
                nn = min(512, NREL - n0)
                if nn <= 0:
                    continue
                E_("tensor", "matmul", pp[nch][0:16, 0:nn], rel_s[:, 0:16],
                   g1hv[:, n0:n0 + nn], start=(kc3 == 0), stop=(kc3 == 2),
                   inc=(pe if nch == 7 else None))
        W_("scalar", pe, pe.val)
        for nch in range(8):
            ncol = half * 8 + nch
            n0 = ncol * 512
            nn = min(512, NREL - n0)
            if nn <= 0:
                continue
            W_("scalar", ch("br%d" % (nch % 2)), CH.get("br%d" % (nch % 2))
               and CH["br%d" % (nch % 2)].val or 0)
            E_("scalar", "copy", stgb(nch % 2)[0:16, 0:nn],
               pp[nch][0:16, 0:nn], inc=ac)
            bank_free[nch] = ac.val
            W_("sync", ac, ac.val)
            dma("sync", "br%d" % (nch % 2),
                AP(biasrel_h.tensor, n0, [[BRP, 16], [1, nn]]),
                stgb(nch % 2)[0:16, 0:nn])
    if "biasrel" in dbg:
        W_("gpsimd", ch("br0"))
        W_("gpsimd", ch("br1"))
        dma("gpsimd", "dbg", dbg["biasrel"][:, :], biasrel_h[:, 0:NREL])

    b.tag = "P1"
    vaugv = A(OF_VAUG, BF16, NT * H * 65)
    gcolv = A(OF_GCOL, F32, NT * H)
    gcolrv = A(OF_GCOLR, F32, NT * H)
    wpmv = A(OF_WPM, F32, NT * H * 2)
    xtcv = A(OF_XTC, BF16, 8 * 512)

    def wstr(slot):
        return A(OF_WSTR + slot * 8 * 128 * 2, BF16, 8 * 128)

    OF_PSTG = OF_G1H + 16 * 512 * 2 + 8 * 512 * 2 + 2 * 16 * 1024

    def pstg(slot):
        return A(OF_PSTG + slot * 512 * 2, BF16, 512)

    vstr = A(OF_G1H + 4 * 512 * 2, BF16, 8 * 512)

    E_("vector", "memset", vaugv, 1.0)
    W_("vector", ch("br0"))
    W_("vector", ch("br1"))
    W_("sync", ch("ws0"))
    W_("sync", ch("ws1"))
    W_("sync", ch("g1"))
    OF_XTC2 = P01_END
    OF_WQK = OF_XTC2 + 8 * 512 * 2
    OF_WKK = OF_WQK + 8 * 1024 * 2
    assert OF_WKK + 8 * 1024 * 2 <= ARENA_BYTES
    xtc2v = A(OF_XTC2, BF16, 8 * 512)
    wqk = A(OF_WQK, BF16, 8 * 1024)
    wkk = A(OF_WKK, BF16, 8 * 1024)
    vstr2 = A(OF_G1H, BF16, 16 * 512)
    W_("sync", pe, pe.val)
    wq_ld = dma("sync", "pw0", wqk[:, :],
                AP(wT_h["wq"].tensor, 0, [[D, 128], [128 * D, 8], [1, 1024]]))
    wk_ld = dma("sync", "pw1", wkk[:, :],
                AP(wT_h["wk"].tensor, 0, [[D, 128], [128 * D, 8], [1, 1024]]))
    vld = 0
    for oc in range(2):
        vld = dma("sync", "pv", vstr2[:, oc * 4096:(oc + 1) * 4096],
                  AP(wT_h["wv"].tensor, oc * 512,
                     [[D, 128], [128 * D, 8], [1, 512]]))
    x_conv = [(dv, dv.val), (dv, dv.val)]
    stgb_free = [(ac, ac.val), (ac, ac.val)]
    pstg_free = [None, None, None, None]
    stg_free = [(dv, dv.val), (dv, dv.val)]
    xt_ready = [0, 0]
    xtc_free = [(pe, pe.val), (pe, pe.val)]
    WT = b.WT

    def xtc(p):
        return xtcv if p == 0 else xtc2v

    def emit_xt(tc):
        p = tc % 2
        first_copy = True
        for sub in range(4):
            slot = sub % 2
            WT("sync", x_conv[slot])
            ldv = dma("sync", "px%d" % slot, xa(slot)[:, :],
                      x_in[(tc * 4 + sub) * 128:(tc * 4 + sub + 1) * 128, :])
            W_("vector", ch("px%d" % slot), ldv)
            WT("vector", stgb_free[slot])
            E_("vector", "tensor_copy", stgb(slot)[:, :], xa(slot)[:, :],
               inc=dv)
            x_conv[slot] = (dv, dv.val)
            W_("tensor", dv, dv.val)
            for dt_ in range(8):
                bk2 = 4 + (dt_ % 2)
                W_("tensor", ac, bank_free[bk2])
                E_("tensor", "transpose", ppb(bk2)[:, 0:128],
                   stgb(slot)[:, dt_ * 128:(dt_ + 1) * 128], idn_s[:, :],
                   inc=pe)
                W_("scalar", pe, pe.val)
                if first_copy:
                    WT("scalar", xtc_free[p])
                    first_copy = False
                E_("scalar", "copy",
                   xtc(p)[:, dt_ * 512 + sub * 128:dt_ * 512 + sub * 128 + 128],
                   ppb(bk2)[:, 0:128], drain=False, inc=ac)
                bank_free[bk2] = ac.val
            stgb_free[slot] = (ac, ac.val)
        xt_ready[p] = ac.val

    emit_xt(0)
    for tc in range(8):
        p = tc % 2
        for pi, (wres, bias_col, dsth) in enumerate(
                ((wqk, bq_s, qt_h), (wkk, bk_s, kt_h))):
            for ot in range(8):
                bk0 = ot % 2
                if tc == 0 and pi == 0 and ot == 0:
                    W_("tensor", ch("pw0"), wq_ld)
                    W_("tensor", ch("pw1"), wk_ld)
                    W_("tensor", ch("pv"), vld)
                W_("tensor", ac, max(xt_ready[p], bank_free[bk0]))
                for kc in range(8):
                    E_("tensor", "matmul", pp[bk0][:, :],
                       wres[:, kc * 1024 + ot * 128:kc * 1024 + (ot + 1) * 128],
                       xtc(p)[:, kc * 512:(kc + 1) * 512],
                       start=(kc == 0), stop=(kc == 7),
                       inc=(pe if kc == 7 else None))
                pslot = pi * 2 + (ot % 2)
                W_("scalar", pe, pe.val)
                WT("scalar", pstg_free[pslot])
                E_("scalar", "activation", pstg(pslot)[:, :], pp[bk0][:, :],
                   AF.Identity, bias=bias_col[:, ot:ot + 1], drain=False,
                   inc=ac)
                bank_free[bk0] = ac.val
                W_("sync", ac, ac.val)
                _v = dma(
                    "sync", "ps%d" % pslot,
                    dsth[ot * 128:(ot + 1) * 128, tc * 512:(tc + 1) * 512],
                    pstg(pslot)[:, :])
                pstg_free[pslot] = (ch("ps%d" % pslot), _v)
        if tc + 1 < 8:
            emit_xt(tc + 1)
        for oc in range(2):
            for sub in range(4):
                bkv = 2 + (sub % 2)
                W_("tensor", ac, bank_free[bkv])
                for kc in range(8):
                    E_("tensor", "matmul", pp[bkv][:, :],
                       xtc(p)[:, kc * 512 + sub * 128:
                              kc * 512 + sub * 128 + 128],
                       vstr2[:, oc * 4096 + kc * 512:
                             oc * 4096 + (kc + 1) * 512],
                       start=(kc == 0), stop=(kc == 7),
                       inc=(pe if kc == 7 else None))
                sslot = sub % 2
                W_("scalar", pe, pe.val)
                WT("scalar", stg_free[sslot])
                E_("scalar", "copy", stg(sslot)[:, 0:512], pp[bkv][:, :],
                   drain=False, inc=ac)
                bank_free[bkv] = ac.val
                W_("vector", ac, ac.val)
                it = tc * 4 + sub
                for hh in range(8):
                    h = oc * 8 + hh
                    E_("vector", "tensor_tensor",
                       vaugv[:, (it * H + h) * 65:(it * H + h) * 65 + 64],
                       stg(sslot)[:, hh * 64:hh * 64 + 64],
                       bv_bc[:, oc * 512 + hh * 64:oc * 512 + hh * 64 + 64],
                       ALU.add, drain=False, inc=(dv if hh == 7 else None))
                stg_free[sslot] = (dv, dv.val)
        W_("tensor", ac, max(xt_ready[p], bank_free[6]))
        for kc in range(8):
            E_("tensor", "matmul", pp[6][0:48, :],
               gwa[:, kc * 48:(kc + 1) * 48],
               xtc(p)[:, kc * 512:(kc + 1) * 512],
               start=(kc == 0), stop=(kc == 7), inc=(pe if kc == 7 else None))
        W_("scalar", pe, pe.val)
        WT("scalar", stg_free[0])
        E_("scalar", "activation", stg(0)[0:48, 0:512], pp[6][0:48, :],
           AF.Sigmoid, bias=gba[:, 0:1], inc=ac)
        bank_free[6] = ac.val
        W_("vector", ac, ac.val)
        WT("vector", stg_free[1])
        E_("vector", "tensor_scalar", stg(1)[0:16, 512:1024],
           stg(0)[32:48, 0:512], gruc_s[32:48, 0:1], -1.0, ALU.mult, ALU.add)
        E_("vector", "tensor_mul", stg(1)[64:80, 0:512],
           stg(1)[0:16, 512:1024], stg(0)[0:16, 0:512])
        E_("vector", "tensor_scalar", stg(1)[0:16, 0:512],
           stg(1)[64:80, 0:512], 2.0, None, ALU.add)
        E_("vector", "tensor_copy", stg(1)[32:48, 0:512],
           stg(1)[0:16, 511::-1], inc=dv)
        g_ready = dv.val
        stg_free[0] = (dv, dv.val)
        W_("tensor", dv, g_ready)
        for sub in range(4):
            bk2 = 4 + (sub % 2)
            W_("tensor", ac, bank_free[bk2])
            E_("tensor", "transpose", pp[bk2][:, 0:16],
               stg(1)[0:16, sub * 128:(sub + 1) * 128],
               idn32_s[0:16, 0:16], inc=pe)
            W_("scalar", pe, pe.val)
            it = tc * 4 + sub
            E_("scalar", "copy", gcolv[:, it * H:(it + 1) * H],
               pp[bk2][:, 0:16], drain=False, inc=ac)
            bank_free[bk2] = ac.val
            E_("tensor", "transpose", pp[bk2][:, 16:32],
               stg(1)[32:48, sub * 128:(sub + 1) * 128],
               idn32_s[32:48, 32:48], inc=pe)
            W_("scalar", pe, pe.val)
            itr = tc * 4 + (3 - sub)
            E_("scalar", "copy", gcolrv[:, itr * H:(itr + 1) * H],
               pp[bk2][:, 16:32], drain=False, inc=ac)
            bank_free[bk2] = ac.val
        gcol_ready = ac.val
        W_("vector", ac, gcol_ready)
        for sub in range(4):
            it = tc * 4 + sub
            E_("vector", "tensor_tensor", stg(1)[:, sub * 32:sub * 32 + 16],
               gcolv[:, it * H:(it + 1) * H], bpm_bc[:, 0:16], ALU.mult,
               drain=False)
            E_("vector", "tensor_tensor",
               stg(1)[:, sub * 32 + 16:sub * 32 + 32],
               gcolv[:, it * H:(it + 1) * H], bpm_bc[:, 16:32], ALU.mult,
               inc=(dv if sub == 3 else None))
        W_("scalar", dv, dv.val)
        for sub in range(4):
            it = tc * 4 + sub
            E_("scalar", "activation", wpmv[:, it * 32:(it + 1) * 32],
               stg(1)[:, sub * 32:(sub + 1) * 32], AF.Exp,
               drain=False, inc=(ac if sub == 3 else None))
        stg_free[1] = (ac, ac.val)
        xtc_free[p] = (pe, pe.val)

    p1_ac = ac.val
    p1_dv = dv.val
    if "qt" in dbg:
        W_("gpsimd", ch("ps0"))
        W_("gpsimd", ch("ps1"))
        dma("gpsimd", "dbg", dbg["qt"][:, :], qt_h[:, :])
    if "kt" in dbg:
        W_("gpsimd", ch("ps2"))
        W_("gpsimd", ch("ps3"))
        dma("gpsimd", "dbg", dbg["kt"][:, :], kt_h[:, :])
    if "vaug" in dbg:
        W_("gpsimd", dv, p1_dv)
        dma("gpsimd", "dbg", dbg["vaug"][:, :], vaugv)
    if "gcol" in dbg:
        W_("gpsimd", ac, p1_ac)
        dma("gpsimd", "dbg", dbg["gcol"][:, :], gcolv)
    if "gcolr" in dbg:
        W_("gpsimd", ac, p1_ac)
        dma("gpsimd", "dbg", dbg["gcolr"][:, :], gcolrv)
    if "wpm" in dbg:
        W_("gpsimd", ac, p1_ac)
        dma("gpsimd", "dbg", dbg["wpm"][:, :], wpmv)

    b.tag = "ATT"
    OF_ATTB = OF_ATTS + 2 * 512 * 2
    assert OF_ATTB + 256 * 2 <= ARENA_BYTES

    def ktb(slot):
        return A(OF_KTB + slot * 4096 * 2, BF16, 4096)

    def qpadv(v):
        return A(OF_QPAD + v * 512 * 2, BF16, 512)

    def a1t(hp):
        return A(OF_A1T + hp * 21 * 128 * 2, BF16, 21 * 128)

    def diagv(s):
        return A(OF_DIAG + s * 4 * 128 * 2, BF16, 4 * 128)

    def expsv(e4):
        return A(OF_EXPS + e4 * 512 * 2, BF16, 512)

    rcmb = A(OF_RCMB, F32, 1024)
    attbf = A(OF_ATTB, BF16, 256)
    OF_RC2 = OF_ATTB + 256 * 2
    OF_RCMB2 = OF_RC2 + 1024 * 4
    assert OF_RCMB2 + 1024 * 4 <= ARENA_BYTES
    rc2 = A(OF_RC2, F32, 1024)
    rcmbs = [rcmb, A(OF_RCMB2, F32, 1024)]

    def attsv(s):
        return A(OF_ATTS + s * 512 * 2, BF16, 512)

    for cn in ("ps0", "ps1", "ps2", "ps3"):
        W_("sync", ch(cn))
    W_("sync", pe, pe.val)
    W_("sync", dv, dv.val)
    W_("sync", ac, ac.val)
    W_("vector", pe, pe.val)
    W_("vector", ac, ac.val)
    for v4 in range(4):
        E_("vector", "memset", qpadv(v4)[:, :], 0.0)
    E_("vector", "memset", rcmb[:, :], 0.0, inc=dv)
    qpad_zero = dv.val
    W_("sync", dv, qpad_zero)

    regcol = {}
    for ic in range(8):
        present = [0]
        if any(_tile_region(jt, ic) == 1 for jt in range(32)):
            present.append(1)
        if any(_tile_region(jt, ic) == 2 for jt in range(32)):
            present.append(2)
        regcol[ic] = {r: i * 65 for i, r in enumerate(present)}

    kt_ld = [None, None]
    kt_free = [None, None]
    a1_ld = [None, None]
    a1_free = [None, None]
    q_ld = [None] * 4
    q_free = [None] * 4
    diag_rdy = [None, None]
    diag_free = [None, None]
    scores_free = [None, None]
    exps_free = [None] * 4
    atts_free = [None, None]
    WT = b.WT

    def load_kt(h):
        slot = h % 2
        WT("sync", kt_free[slot])
        v = dma("sync", "kt%d" % slot, ktb(slot)[:, :],
                kt_h[(h // 2) * 128:(h // 2) * 128 + 128, :])
        kt_ld[slot] = (CH["kt%d" % slot], v)

    def load_a1(h):
        slot = h % 2
        WT("sync", a1_free[slot])
        for di in range(21):
            off = h * BRP + 128 * (di - 10) + 3968
            v = dma("sync", "a1%d" % slot,
                    a1t(slot)[:, di * 128:(di + 1) * 128],
                    AP(biasrel_h.tensor, off, [[1, 128], [1, 128]]))
        a1_ld[slot] = (CH["a1%d" % slot], v)

    def load_q(h, ic):
        v4 = (h % 2) * 2 + (ic % 2)
        WT("sync", q_free[v4])
        r0 = (h % 2) * 64
        v = dma("sync", "q%d" % v4,
                qpadv(v4)[r0:r0 + 64, :] if r0 else qpadv(v4)[0:64, :],
                qt_h[h * 64:(h + 1) * 64, ic * 512:(ic + 1) * 512])
        q_ld[v4] = (CH["q%d" % v4], v)

    load_kt(0)
    load_a1(0)
    load_q(0, 0)
    pending = [None]
    attbf_free = [None]
    rcmb_free = [None, None]
    for h in range(H):
        hp = h % 2
        for ic in range(8):
            it0 = ic * 4
            ds = (h * 8 + ic) % 2
            v4 = hp * 2 + (ic % 2)
            if ic + 1 < 8:
                load_q(h, ic + 1)
            elif h + 1 < H:
                load_q(h + 1, 0)
            if ic == 0 and h + 1 < H:
                load_kt(h + 1)
                load_a1(h + 1)
            WT("vector", diag_free[ds])
            for sub in range(4):
                it = it0 + sub
                E_("vector", "tensor_scalar",
                   diagv(ds)[:, sub * 128:(sub + 1) * 128], anti_s[:, :],
                   gcolrv[:, it * H + h:it * H + h + 1], None, ALU.mult,
                   drain=False, inc=(dv if sub == 3 else None))
            diag_rdy[ds] = (dv, dv.val)
            WT("tensor", kt_ld[hp])
            WT("tensor", a1_ld[hp])
            WT("tensor", q_ld[v4])
            WT("tensor", diag_rdy[ds])
            av_first = {0: (0, 0), 1: (0, 2)}
            av_last = {0: (31, 1), 1: (31, 3)}
            qk_done = [None] * 32

            def emit_qk(jt):
                sb = jt % 2
                WT("tensor", scores_free[sb])
                nearj = _near_pair(jt, it0) or _near_pair(jt, it0 + 1) or \
                    _near_pair(jt, it0 + 2) or _near_pair(jt, it0 + 3)
                nbias = [sub for sub in range(4)
                         if abs(jt - (it0 + sub)) <= 7] if nearj else []
                E_("tensor", "matmul", pp[sb][:, :],
                   ktb(hp)[:, jt * 128:(jt + 1) * 128], qpadv(v4)[:, :],
                   start=True, stop=(not nbias),
                   inc=(pe if not nbias else None))
                for bi, sub in enumerate(nbias):
                    dd = jt - (it0 + sub)
                    E_("tensor", "matmul",
                       pp[sb][:, sub * 128:(sub + 1) * 128],
                       a1t(hp)[:, (dd + 10) * 128:(dd + 11) * 128],
                       diagv(ds)[:, sub * 128:(sub + 1) * 128],
                       start=False, stop=(bi == len(nbias) - 1),
                       inc=(pe if bi == len(nbias) - 1 else None))
                qk_done[jt] = pe.val

            emit_qk(0)
            for jt in range(32):
                sb = jt % 2
                if jt + 1 < 32:
                    emit_qk(jt + 1)
                e4 = jt % 4
                W_("scalar", pe, qk_done[jt])
                WT("scalar", exps_free[e4])
                E_("scalar", "activation", expsv(e4)[:, :], pp[sb][:, :],
                   AF.Exp, drain=False, inc=ac)
                exp_done = ac.val
                scores_free[sb] = (ac, exp_done)
                W_("tensor", ac, exp_done)
                for sub in range(4):
                    r = 0 if abs(jt - (it0 + sub)) <= 7 else \
                        (1 if jt > it0 + sub else 2)
                    bank = pp[2 + sub // 2]
                    coff = (sub % 2) * 256 + regcol[ic][r]
                    E_("tensor", "matmul", bank[:, coff:coff + 65],
                       expsv(e4)[:, sub * 128:(sub + 1) * 128],
                       vaugv[:, ((jt * H) + h) * 65:((jt * H) + h) * 65 + 65],
                       start=(av_first[sub // 2] == (jt, sub)),
                       stop=(av_last[sub // 2] == (jt, sub)),
                       inc=(pe if (jt == 31 and sub == 3) else None))
                exps_free[e4] = (pe, pe.val)
                if jt == 5 and pending[0] is not None:
                    pending[0]()
                    pending[0] = None
            kt_free[hp] = (pe, pe.val)
            a1_free[hp] = (pe, pe.val)
            q_free[v4] = (pe, pe.val)
            diag_free[ds] = (pe, pe.val)
            av_done = pe.val
            ncols = 65 * len(regcol[ic])
            ip = (h * 8 + ic) % 2
            rcmb_c = rcmbs[ip]
            W_("scalar", pe, av_done)
            WT("scalar", rcmb_free[ip])
            for sub in range(4):
                E_("scalar", "copy",
                   rcmb_c[:, sub * 200:sub * 200 + ncols],
                   pp[2 + sub // 2][:, (sub % 2) * 256:
                                    (sub % 2) * 256 + ncols],
                   drain=False, inc=(ac if sub == 3 else None))
            rc_done = ac.val

            def make_epilogue(h=h, ic=ic, it0=it0, rc_done=rc_done,
                              rcmb_c=rcmb_c, ip=ip):
                def emit():
                    W_("vector", ac, rc_done)
                    WT("vector", attbf_free[0])
                    has1 = 1 in regcol[ic]
                    has2 = 2 in regcol[ic]

                    def _cf(sub):
                        if has2:
                            return rc2, sub * 130 + 65
                        if has1:
                            return rc2, sub * 130
                        return rcmb_c, sub * 200
                    if has1:
                        for sub in range(4):
                            it = it0 + sub
                            base = sub * 200
                            E_("vector", "scalar_tensor_tensor",
                               rc2[:, sub * 130:sub * 130 + 65],
                               rcmb_c[:, base + regcol[ic][1]:
                                      base + regcol[ic][1] + 65],
                               wpmv[:, it * 32 + 16 + h:it * 32 + 16 + h + 1],
                               rcmb_c[:, base:base + 65], ALU.mult, ALU.add,
                               drain=(sub == 3))
                    if has2:
                        for sub in range(4):
                            it = it0 + sub
                            base = sub * 200
                            src1 = rc2[:, sub * 130:sub * 130 + 65] \
                                if has1 else rcmb_c[:, base:base + 65]
                            E_("vector", "scalar_tensor_tensor",
                               rc2[:, sub * 130 + 65:sub * 130 + 130],
                               rcmb_c[:, base + regcol[ic][2]:
                                      base + regcol[ic][2] + 65],
                               wpmv[:, it * 32 + h:it * 32 + h + 1],
                               src1, ALU.mult, ALU.add, drain=(sub == 3))
                    for sub in range(4):
                        bufc, c0 = _cf(sub)
                        E_("vector", "reciprocal",
                           rc2[:, 520 + sub:521 + sub],
                           bufc[:, c0 + 64:c0 + 65], drain=(sub == 3))
                    for sub in range(4):
                        bufc, c0 = _cf(sub)
                        E_("vector", "tensor_scalar",
                           attbf[:, sub * 64:sub * 64 + 64],
                           bufc[:, c0:c0 + 64], rc2[:, 520 + sub:521 + sub],
                           None, ALU.mult, drain=False,
                           inc=(dv if sub == 3 else None))
                    cmb_done = dv.val
                    rcmb_free[ip] = (dv, cmb_done)
                    W_("tensor", dv, cmb_done)
                    asl = ic % 2
                    for sub in range(4):
                        bk2 = 4 + (sub % 2)
                        W_("tensor", ac, bank_free[bk2])
                        E_("tensor", "transpose", ppb(bk2)[0:64, 0:128],
                           attbf[:, sub * 64:(sub + 1) * 64], idn_s[:, :],
                           inc=pe)
                        W_("scalar", pe, pe.val)
                        if sub < 2:
                            WT("scalar", atts_free[asl])
                        E_("scalar", "copy", attsv(asl)[0:64, sub * 128:
                                                        (sub + 1) * 128],
                           ppb(bk2)[0:64, 0:128], drain=False, inc=ac)
                        bank_free[bk2] = ac.val
                    attbf_free[0] = (pe, pe.val)
                    W_("sync", ac, ac.val)
                    v = dma("sync", "at%d" % asl,
                            attnT_h[h * 64:(h + 1) * 64,
                                    ic * 512:(ic + 1) * 512],
                            attsv(asl)[0:64, :])
                    atts_free[asl] = (CH["at%d" % asl], v)
                return emit
            pending[0] = make_epilogue()
    if pending[0] is not None:
        pending[0]()
        pending[0] = None
    if "attnT" in dbg:
        W_("gpsimd", ch("at0"))
        W_("gpsimd", ch("at1"))
        dma("gpsimd", "dbg", dbg["attnT"][:, :], attnT_h[:, :])

    b.tag = "LN1"

    def attld(slot):
        return A(OF_ATTLD + slot * 8 * 128 * 2, BF16, 8 * 128)

    x1t32 = A(OF_X1T32, F32, D)
    OF_X1T32B = 137 * 1024

    def x1t32p(p):
        return x1t32 if p == 0 else A(OF_X1T32B, F32, D)
    wmst = A(OF_WMST, BF16, 512)
    rtr = A(OF_RTR, F32, 256)
    W_("sync", ch("at0"))
    W_("sync", ch("at1"))
    W_("sync", pe, pe.val)
    W_("sync", dv, dv.val)
    W_("sync", ac, ac.val)
    W_("vector", pe, pe.val)
    W_("scalar", dv, dv.val)
    attld_free = [None, None]
    wo_free = [None, None]
    lnst_free = [None, None]
    x1st_free = [None, None]
    x1b_free = [None, None]
    x1t_free = [None, None]
    wm_free = [None]
    xin_free = [None, None]
    lgbank_free = [(ac, ac.val)]
    x1t32_free = [[], []]
    pending_rt = [None]
    for it in range(NT):
        sl = it % 2
        WT("sync", attld_free[sl])
        av_ = dma("sync", "al%d" % sl, attld(sl)[:, :],
                  AP(attnT_h.tensor, it * 128, [[N, 128], [128 * N, 8],
                                                [1, 128]]))
        WT("sync", wo_free[sl])
        wv_ = dma("sync", "wl%d" % sl, wstr(sl)[:, :],
                  AP(wT_h["wo"].tensor, it * 0, [[D, 128], [128 * D, 8],
                                                 [1, 128]])) if False else None
        if it == 0:
            wo_ld = dma("sync", "wo", vstr[:, :],
                        AP(wT_h["wo"].tensor, 0,
                           [[D, 128], [128 * D, 8], [1, 512]]))
            wo_ld2 = dma("sync", "wo2", A(OF_G1H + 12 * 1024, BF16, 8 * 512),
                         AP(wT_h["wo"].tensor, 512,
                            [[D, 128], [128 * D, 8], [1, 512]]))
        WT("sync", xin_free[sl])
        xv_ = dma("sync", "xi%d" % sl, xa(sl)[:, :],
                  x_in[it * 128:(it + 1) * 128, :])
        W_("tensor", ch("al%d" % sl), av_)
        if it == 0:
            W_("tensor", ch("wo"), wo_ld)
            W_("tensor", ch("wo2"), wo_ld2)
        W_("tensor", ac, bank_free[6])
        W_("tensor", ac, bank_free[7])
        for ot in range(8):
            E_("tensor", "matmul", pp[6][:, :],
               attld(sl)[:, ot * 128:(ot + 1) * 128],
               vstr[:, ot * 512:(ot + 1) * 512],
               start=(ot == 0), stop=(ot == 7))
            E_("tensor", "matmul", pp[7][:, :],
               attld(sl)[:, ot * 128:(ot + 1) * 128],
               A(OF_G1H + 12 * 1024, BF16, 8 * 512)[:, ot * 512:
                                                    (ot + 1) * 512],
               start=(ot == 0), stop=(ot == 7),
               inc=(pe if ot == 7 else None))
        attld_free[sl] = (pe, pe.val)
        oproj_done = pe.val
        W_("scalar", pe, oproj_done)
        WT("scalar", lnst_free[sl])
        E_("scalar", "copy", stg(sl)[:, 0:512], pp[6][:, :], drain=False)
        E_("scalar", "copy", stg(sl)[:, 512:1024], pp[7][:, :], drain=False,
           inc=ac)
        bank_free[6] = ac.val
        bank_free[7] = ac.val
        ev_done = ac.val
        W_("vector", ac, ev_done)
        W_("vector", ch("xi%d" % sl), xv_)
        E_("vector", "tensor_add", stgb(sl).bitcast(F32)[:, 0:1024],
           stg(sl)[:, :], bo_bc[:, :]) if False else None
        E_("vector", "tensor_add", stg(sl)[:, :], stg(sl)[:, :],
           bo_bc[:, :]) if False else None
        s2 = A(OF_EXPS, F32, 1024)
        WT("vector", x1st_free[0])
        WT("vector", x1st_free[1])
        E_("vector", "tensor_add", s2[:, 0:1024], stg(sl)[:, :], bo_bc[:, :])
        E_("vector", "tensor_add", stg(sl)[:, :], s2[:, 0:1024], xa(sl)[:, :],
           inc=dv)
        xin_free[sl] = (dv, dv.val)
        E_("vector", "reduce_sum", rcmb[:, 0:1], stg(sl)[:, :], axis=AX,
           inc=dv)
        sum_done = dv.val
        W_("scalar", dv, sum_done)
        E_("scalar", "activation", s2[:, 0:1024], stg(sl)[:, :], AF.Square,
           accum_out=rcmb[:, 1:2], drain=False, inc=ac)
        sq_done = ac.val
        W_("vector", ac, sq_done)
        E_("vector", "tensor_scalar", rcmb[:, 2:3], rcmb[:, 0:1],
           1.0 / D, None, ALU.mult)
        E_("vector", "tensor_scalar", rcmb[:, 3:4], rcmb[:, 1:2],
           1.0 / D, None, ALU.mult)
        E_("vector", "tensor_mul", rcmb[:, 4:5], rcmb[:, 2:3], rcmb[:, 2:3])
        E_("vector", "tensor_sub", rcmb[:, 5:6], rcmb[:, 3:4], rcmb[:, 4:5])
        E_("vector", "tensor_scalar", rcmb[:, 6:7], rcmb[:, 5:6],
           EPS, None, ALU.add, inc=dv)
        var_done = dv.val
        W_("scalar", dv, var_done)
        E_("scalar", "sqrt", rcmb[:, 7:8], rcmb[:, 6:7], drain=False, inc=ac)
        W_("vector", ac, ac.val)
        E_("vector", "reciprocal", rcmb[:, 8:9], rcmb[:, 7:8], inc=dv)
        W_("vector", dv, dv.val)
        E_("vector", "tensor_scalar", s2[:, 0:1024], stg(sl)[:, :],
           rcmb[:, 2:3], rcmb[:, 8:9], ALU.subtract, ALU.mult, inc=dv)
        x1_done = dv.val
        lnst_free[sl] = (dv, x1_done)
        W_("sync", dv, x1_done)
        WT("sync", x1st_free[sl])
        v1 = dma("sync", "x1f%d" % sl, x1_h[it * 128:(it + 1) * 128, :],
                 s2[:, 0:1024])
        x1st_free[sl] = (CH["x1f%d" % sl], v1)
        WT("vector", x1t_free[sl])
        E_("vector", "tensor_copy", stgb(sl)[:, :], s2[:, 0:1024], inc=dv)
        x1b_done = dv.val
        W_("sync", dv, x1b_done)
        WT("sync", x1b_free[sl])
        v2 = dma("sync", "x1b%d" % sl, x1bf_h[it * 128:(it + 1) * 128, :],
                 stgb(sl)[:, :])
        x1b_free[sl] = (CH["x1b%d" % sl], v2)
        W_("tensor", dv, x1_done)
        p = it % 2
        x1t32c = x1t32p(p)
        for tok in x1t32_free[p]:
            WT("scalar", tok)
        for dt_ in range(8):
            bk2 = 4 + (dt_ % 2)
            W_("tensor", ac, bank_free[bk2])
            E_("tensor", "transpose", pp[bk2][:, 0:128],
               s2[:, dt_ * 128:(dt_ + 1) * 128], idn32_s[:, :], inc=pe)
            W_("scalar", pe, pe.val)
            E_("scalar", "copy", x1t32c[:, dt_ * 128:(dt_ + 1) * 128],
               pp[bk2][:, 0:128], drain=False, inc=ac)
            bank_free[bk2] = ac.val
        x1t_done = ac.val
        W_("vector", ac, x1t_done)
        WT("vector", x1b_free[sl])
        E_("vector", "tensor_copy", stgb(sl)[:, :], x1t32c[:, :], inc=dv)
        x1tb_done = dv.val
        W_("sync", dv, x1tb_done)
        WT("sync", x1t_free[sl])
        v3 = dma("sync", "x1t%d" % sl,
                 AP(x1t_h.tensor, it * 128, [[N, 128], [128 * N, 8],
                                             [1, 128]]),
                 stgb(sl)[:, :])
        x1t_free[sl] = (CH["x1t%d" % sl], v3)

        def make_rt(it=it, p=p, x1t32c=x1t32c, x1t_done=x1t_done,
                    x1tb_done=x1tb_done):
            def emit():
                W_("tensor", ac, max(x1t_done, lgbank_free[0][1]))
                for kc in range(8):
                    E_("tensor", "matmul", pp[2][:, 0:8],
                       x1t32c[:, kc * 128:(kc + 1) * 128], wrT_s[:, kc * 8:kc * 8 + 8],
                       start=(kc == 0), stop=(kc == 7),
                       inc=(pe if kc == 7 else None))
                lg_done = pe.val
                W_("scalar", pe, lg_done)
                E_("scalar", "copy", rtr[:, 0:8], pp[2][:, 0:8], drain=False, inc=ac)
                lgbank_free[0] = (ac, ac.val)
                lgc_done = ac.val
                W_("vector", ac, lgc_done)
                E_("vector", "reduce_max", rtr[:, 8:9], rtr[:, 0:8], axis=AX)
                E_("vector", "tensor_scalar", rtr[:, 16:24], rtr[:, 0:8],
                   rtr[:, 8:9], None, ALU.is_equal)
                E_("vector", "tensor_tensor_scan", rtr[:, 24:32], ones8_s[:, :],
                   rtr[:, 16:24], 0.0, ALU.mult, ALU.add)
                E_("vector", "tensor_scalar", rtr[:, 32:40], rtr[:, 24:32],
                   1.0, None, ALU.is_equal)
                E_("vector", "tensor_mul", rtr[:, 40:48], rtr[:, 16:24],
                   rtr[:, 32:40])
                E_("vector", "tensor_scalar", rtr[:, 48:56], rtr[:, 40:48],
                   1e30, None, ALU.mult)
                E_("vector", "tensor_sub", rtr[:, 56:64], rtr[:, 0:8], rtr[:, 48:56])
                E_("vector", "reduce_max", rtr[:, 9:10], rtr[:, 56:64], axis=AX)
                E_("vector", "tensor_scalar", rtr[:, 64:72], rtr[:, 56:64],
                   rtr[:, 9:10], None, ALU.is_equal)
                E_("vector", "tensor_tensor_scan", rtr[:, 72:80], ones8_s[:, :],
                   rtr[:, 64:72], 0.0, ALU.mult, ALU.add)
                E_("vector", "tensor_scalar", rtr[:, 80:88], rtr[:, 72:80],
                   1.0, None, ALU.is_equal)
                E_("vector", "tensor_mul", rtr[:, 88:96], rtr[:, 64:72],
                   rtr[:, 80:88])
                E_("vector", "tensor_scalar", rtr[:, 10:11], rtr[:, 8:9],
                   -1.0, None, ALU.mult, inc=dv)
                t2a = dv.val
                W_("scalar", dv, t2a)
                E_("scalar", "activation", rtr[:, 96:104], rtr[:, 0:8], AF.Exp,
                   bias=rtr[:, 10:11], drain=False, inc=ac)
                e8_done = ac.val
                W_("vector", ac, e8_done)
                E_("vector", "tensor_mul", rtr[:, 104:112], rtr[:, 96:104],
                   rtr[:, 40:48])
                E_("vector", "reduce_sum", rtr[:, 11:12], rtr[:, 104:112], axis=AX)
                E_("vector", "tensor_mul", rtr[:, 112:120], rtr[:, 96:104],
                   rtr[:, 88:96])
                E_("vector", "reduce_sum", rtr[:, 12:13], rtr[:, 112:120], axis=AX)
                E_("vector", "tensor_add", rtr[:, 13:14], rtr[:, 11:12],
                   rtr[:, 12:13])
                E_("vector", "reciprocal", rtr[:, 14:15], rtr[:, 13:14])
                E_("vector", "tensor_mul", rtr[:, 15:16], rtr[:, 11:12],
                   rtr[:, 14:15])
                E_("vector", "tensor_mul", rtr[:, 120:121], rtr[:, 12:13],
                   rtr[:, 14:15])
                E_("vector", "tensor_scalar", rtr[:, 128:136], rtr[:, 40:48],
                   rtr[:, 15:16], None, ALU.mult)
                E_("vector", "scalar_tensor_tensor", rtr[:, 136:144],
                   rtr[:, 88:96], rtr[:, 120:121], rtr[:, 128:136],
                   ALU.mult, ALU.add)
                E_("vector", "tensor_copy", A(OF_ATTB, BF16, 256)[:, 0:8],
                   rtr[:, 136:144], inc=dv)
                wm8_done = dv.val
                W_("tensor", dv, wm8_done)
                bk2 = 4
                W_("tensor", ac, bank_free[bk2])
                E_("tensor", "transpose", ppb(bk2)[0:8, 0:128],
                   A(OF_ATTB, BF16, 256)[:, 0:8], idn_s[:, :], inc=pe)
                W_("scalar", pe, pe.val)
                if it % 4 == 0 and wm_free[0] is not None:
                    WT("scalar", wm_free[0])
                E_("scalar", "copy", wmst[0:8, (it % 4) * 128:(it % 4) * 128 + 128],
                   ppb(bk2)[0:8, 0:128], drain=False, inc=ac)
                bank_free[bk2] = ac.val
                if it % 4 == 3:
                    W_("sync", ac, ac.val)
                    if wm_free[0] is not None:
                        WT("sync", wm_free[0])
                    vw = dma("sync", "wm", wmT_h[:, (it - 3) * 128:(it + 1) * 128],
                             wmst[0:8, :])
                    wm_free[0] = (CH["wm"], vw)
                x1t32_free[p] = [(pe, lg_done),
                                 (dv, x1tb_done)]
            return emit
        if pending_rt[0] is not None:
            pending_rt[0]()
        pending_rt[0] = make_rt()
    if pending_rt[0] is not None:
        pending_rt[0]()
        pending_rt[0] = None
    if "x1" in dbg:
        W_("gpsimd", ch("x1f0"))
        W_("gpsimd", ch("x1f1"))
        dma("gpsimd", "dbg", dbg["x1"][:, :], x1_h[:, :])
    if "wm" in dbg:
        W_("gpsimd", ch("wm"))
        dma("gpsimd", "dbg", dbg["wm"][:, :], wmT_h[:, :])

    b.tag = "SCAN"
    W_("gpsimd", ch("wm"))
    E_("gpsimd", "collective_compute", "AllGather", ALU.bypass,
       [list(range(NCORE))], [wmT_h[:, :].opt()], [wall_h[:, :].opt()],
       inc=cc)
    cc_ag1 = cc.val
    W_("gpsimd", ch("x1b0"))
    W_("gpsimd", ch("x1b1"))
    E_("gpsimd", "collective_compute", "AllGather", ALU.bypass,
       [list(range(NCORE))], [x1bf_h[:, :].opt()], [x1g_h[:, :].opt()],
       inc=cc)
    cc_ag2 = cc.val
    msbv = A(OF_MSB, BF16, 8 * 256)
    posv = A(OF_POSS, F32, 8 * 256)
    keptv = A(OF_KEPT, F32, 8 * 256)
    scr = A(OF_SCR, F32, 8 * 256)
    idxa = A(OF_IDXA, F32, 16 * 256)
    eligb = A(OF_ELIGB, BF16, 256)
    bbf = A(OF_BBF, BF16, 256)
    onesr = A(OF_ONESR, F32, 256)
    csrow = A(OF_CSROW, F32, 512)

    W_("sync", pe, pe.val)
    W_("vector", pe, pe.val)
    W_("vector", ac, ac.val)
    W_("sync", cc, cc_ag1)
    W_("vector", cc, cc_ag1)
    xbv = 0
    for e in range(E):
        for r in range(NCORE):
            xbv = dma("sync", "xb", msbv[:, e * 256 + r * 32:
                                         e * 256 + (r + 1) * 32],
                      AP(wall_h.tensor, (8 * r + e) * N, [[128, 32], [1, 128]]),
                      transpose=True)
    E_("vector", "memset", onesr[0:1, :], 1.0)
    E_("vector", "memset", scr[:, 0:256], 0.0, inc=dv)
    W_("vector", ch("xb"), xbv)
    fb_cur = 0
    for e in range(E):
        mat = idxa[:, (8 + e) * 256:(9 + e) * 256]
        E_("vector", "tensor_copy", mat, msbv[:, e * 256:(e + 1) * 256])
        E_("vector", "tensor_scalar", scr[:, 768:1024], mat, 0.0, None,
           ALU.is_gt)
        E_("vector", "tensor_scalar", scr[:, 1280:1536],
           scr[:, fb_cur:fb_cur + 256], -1.0, 1.0, ALU.mult, ALU.add)
        E_("vector", "tensor_mul", scr[:, 1024:1280], scr[:, 768:1024],
           scr[:, 1280:1536])
        E_("vector", "tensor_copy", eligb[:, :], scr[:, 1024:1280], inc=dv)
        el_done = dv.val
        W_("tensor", dv, el_done)
        W_("tensor", ac, max(bank_free[0], bank_free[1]))
        E_("tensor", "matmul", pp[0][:, 0:256], ltx_s[:, :], eligb[:, :],
           start=True, stop=False)
        E_("tensor", "matmul", pp[1][0:1, 0:256], onecol_s[:, 0:1],
           eligb[:, :], start=True, stop=True, inc=pe)
        cs_done = pe.val
        W_("scalar", pe, cs_done)
        E_("scalar", "copy", csrow[0:1, 0:256], pp[1][0:1, 0:256], inc=ac)
        W_("vector", ac, ac.val)
        E_("vector", "tensor_tensor_scan", csrow[0:1, 256:512],
           onesr[0:1, :], csrow[0:1, 0:256], 0.0, ALU.mult, ALU.add)
        E_("vector", "memset", scr[0:1, 1536:1537], 0.0)
        E_("vector", "tensor_copy", scr[0:1, 1537:1792],
           csrow[0:1, 256:511])
        E_("vector", "tensor_copy", eligb[0:1, :], scr[0:1, 1536:1792])
        E_("vector", "tensor_copy", csrow[0:1, 0:256], eligb[0:1, :])
        E_("vector", "tensor_sub", scr[0:1, 1792:2048],
           scr[0:1, 1536:1792], csrow[0:1, 0:256])
        E_("vector", "tensor_copy", bbf[0:1, :], scr[0:1, 1792:2048],
           inc=dv)
        ex_done = dv.val
        W_("tensor", dv, ex_done)
        E_("tensor", "matmul", pp[0][:, 0:256], onez_s[:, :], eligb[0:1, :],
           start=False, stop=False)
        E_("tensor", "matmul", pp[0][:, 0:256], onez_s[:, :], bbf[0:1, :],
           start=False, stop=True, inc=pe)
        pos_done = pe.val
        W_("scalar", pe, pos_done)
        E_("scalar", "copy", posv[:, e * 256:(e + 1) * 256],
           pp[0][:, 0:256], inc=ac)
        bank_free[0] = ac.val
        bank_free[1] = ac.val
        posc_done = ac.val
        W_("vector", ac, posc_done)
        E_("vector", "tensor_scalar", scr[:, 1792:2048],
           posv[:, e * 256:(e + 1) * 256], float(CAP), None, ALU.is_lt)
        E_("vector", "tensor_mul", keptv[:, e * 256:(e + 1) * 256],
           scr[:, 1792:2048], scr[:, 1024:1280])
        E_("vector", "tensor_sub", scr[:, 1280:1536], scr[:, 1024:1280],
           keptv[:, e * 256:(e + 1) * 256])
        nfb = 256 if fb_cur == 0 else 0
        E_("vector", "tensor_add", scr[:, nfb:nfb + 256],
           scr[:, fb_cur:fb_cur + 256], scr[:, 1280:1536], inc=dv)
        fb_cur = nfb
    for col in range(7):
        E_("vector", "memset", idxa[:, col * 256:(col + 1) * 256], 0.0)
    E_("vector", "memset", idxa[:, 7 * 256:8 * 256], 0.0, inc=dv)
    for e in range(E):
        mat = idxa[:, (8 + e) * 256:(9 + e) * 256]
        E_("vector", "tensor_scalar", scr[:, 768:1024], mat, 0.0, None,
           ALU.is_gt)
        E_("vector", "tensor_scalar", scr[:, 1024:1280],
           idxa[:, 0:256], 0.0, None, ALU.is_equal)
        E_("vector", "tensor_mul", scr[:, 1280:1536], scr[:, 768:1024],
           scr[:, 1024:1280])
        E_("vector", "tensor_scalar", scr[:, 1024:1280],
           idxa[:, 0:256], 1.0, None, ALU.is_equal)
        E_("vector", "tensor_mul", scr[:, 1536:1792], scr[:, 768:1024],
           scr[:, 1024:1280])
        for (facc, base) in ((scr[:, 1280:1536], 1), (scr[:, 1536:1792], 4)):
            E_("vector", "scalar_tensor_tensor",
               idxa[:, base * 256:(base + 1) * 256], facc, float(e),
               idxa[:, base * 256:(base + 1) * 256], ALU.mult, ALU.add) \
                if False else None
            E_("vector", "scalar_tensor_tensor", scr[:, 1024:1280],
               facc, float(e), idxa[:, base * 256:(base + 1) * 256],
               ALU.mult, ALU.add)
            E_("vector", "tensor_copy", idxa[:, base * 256:(base + 1) * 256],
               scr[:, 1024:1280])
            E_("vector", "tensor_mul", scr[:, 1024:1280], facc,
               posv[:, e * 256:(e + 1) * 256])
            E_("vector", "tensor_add", scr[:, 1792:2048], scr[:, 1024:1280],
               idxa[:, (base + 1) * 256:(base + 2) * 256])
            E_("vector", "tensor_copy",
               idxa[:, (base + 1) * 256:(base + 2) * 256],
               scr[:, 1792:2048])
            E_("vector", "tensor_mul", scr[:, 1024:1280], facc, mat)
            E_("vector", "tensor_add", scr[:, 1792:2048], scr[:, 1024:1280],
               idxa[:, (base + 2) * 256:(base + 3) * 256])
            E_("vector", "tensor_copy",
               idxa[:, (base + 2) * 256:(base + 3) * 256],
               scr[:, 1792:2048])
        E_("vector", "tensor_add", scr[:, 1024:1280], idxa[:, 0:256],
           scr[:, 768:1024])
        E_("vector", "tensor_copy", idxa[:, 0:256], scr[:, 1024:1280],
           inc=dv)
    E_("vector", "memset", scr[:, 1792:2048], 0.0)
    for e in range(E):
        E_("vector", "tensor_scalar", scr[:, 1024:1280],
           posv[:, e * 256:(e + 1) * 256], -float(CAP), None, ALU.add)
        E_("vector", "tensor_mul", scr[:, 1280:1536],
           keptv[:, e * 256:(e + 1) * 256], scr[:, 1024:1280])
        E_("vector", "tensor_scalar", scr[:, 1024:1280], scr[:, 1280:1536],
           float(CAP), None, ALU.add)
        E_("vector", "tensor_scalar", scr[:, 1280:1536], scr[:, 1024:1280],
           coreoh_bc[:, e:e + 1], None, ALU.mult)
        E_("vector", "tensor_add", scr[:, 1536:1792], scr[:, 1792:2048],
           scr[:, 1280:1536])
        E_("vector", "tensor_copy", scr[:, 1792:2048], scr[:, 1536:1792])
    E_("vector", "tensor_copy", slotc_s[:, :], scr[:, 1792:2048])
    E_("vector", "tensor_copy", slotc32_s[:, :], slotc_s[:, :],
       drain=True, inc=dv)
    slotc_done = dv.val
    sl32_done = dv.val
    if "slotc" in dbg:
        W_("gpsimd", dv, slotc_done)
        dma("gpsimd", "dbg", dbg["slotc"][:, :], slotc_s[:, :])
    if "pos" in dbg:
        W_("gpsimd", ac, ac.val)
        dma("gpsimd", "dbg", dbg["pos"][:, :], posv)
    if "kept" in dbg:
        W_("gpsimd", dv, dv.val)
        dma("gpsimd", "dbg", dbg["kept"][:, :], keptv)
    iscr = A(OF_SCR + 1536 * 4, I32, 256)
    ihi = A(OF_SCR + 256 * 4, I32, 256)
    ilo = A(OF_SCR + 512 * 4, I32, 256)
    E_("vector", "tensor_scalar", scr[:, 1024:1280],
       scr[:, fb_cur:fb_cur + 256], -1.0, 1.0, ALU.mult, ALU.add)
    E_("vector", "tensor_scalar", scr[:, 1280:1536],
       scr[:, fb_cur:fb_cur + 256], float(FBROW), None, ALU.mult)
    for r_i, (ecol, pcol) in enumerate(((1, 2), (4, 5))):
        E_("vector", "tensor_copy", iscr[:, :],
           idxa[:, pcol * 256:(pcol + 1) * 256])
        E_("vector", "tensor_scalar", ihi[:, :], iscr[:, :], 10, None,
           ALU.arith_shift_right)
        E_("vector", "tensor_scalar", ilo[:, :], iscr[:, :], 1023, None,
           ALU.bitwise_and)
        E_("vector", "tensor_copy", scr[:, 768:1024], ihi[:, :])
        E_("vector", "tensor_copy", scr[:, 1792:2048], ilo[:, :])
        E_("vector", "scalar_tensor_tensor", scr[:, 1536:1792],
           idxa[:, ecol * 256:(ecol + 1) * 256], 1024.0,
           scr[:, 1792:2048], ALU.mult, ALU.add)
        E_("vector", "scalar_tensor_tensor", scr[:, 512:768],
           scr[:, 768:1024], 8192.0, scr[:, 1536:1792],
           ALU.mult, ALU.add)
        E_("vector", "tensor_mul", idxa[:, 7 * 256:8 * 256],
           scr[:, 512:768], scr[:, 1024:1280])
        E_("vector", "tensor_add", idxa[:, r_i * 256:(r_i + 1) * 256],
           idxa[:, 7 * 256:8 * 256], scr[:, 1280:1536], inc=dv)
    rs_arrays = [idxa[:, 0:256], idxa[:, 256:512], idxa[:, 3 * 256:4 * 256],
                 idxa[:, 6 * 256:7 * 256], scr[:, fb_cur:fb_cur + 256]]
    rs_done = dv.val
    W_("tensor", dv, rs_done)
    rsst = 0
    for arr_i, arr in enumerate(rs_arrays):
        for hhf in range(2):
            bk2 = 4 + (hhf % 2)
            W_("tensor", ac, bank_free[bk2])
            E_("tensor", "transpose", pp[bk2][:, 0:128],
               arr[:, hhf * 128:(hhf + 1) * 128], idn32_s[:, :], inc=pe)
            W_("scalar", pe, pe.val)
            W_("scalar", ch("rs"), rsst)
            E_("scalar", "activation", csrow[0:128, 0:128],
               pp[bk2][:, 0:128], AF.Identity, scale=0.125, inc=ac)
            bank_free[bk2] = ac.val
            W_("sync", ac, ac.val)
            rsst = dma("sync", "rs",
                       AP(rsi_h.tensor,
                          ((4 * hhf) * 160 + arr_i * 32) * 128,
                          [[160 * 128, 4], [128, 32], [1, 128]]),
                       csrow[0:128, 0:128])
    W_("gpsimd", ch("rs"))
    E_("gpsimd", "collective_compute", "ReduceScatter", ALU.add,
       [list(range(NCORE))], [rsi_h[:, :].opt()], [rso_h[:, :].opt()],
       inc=cc)
    cc_rs = cc.val
    if "rso" in dbg:
        W_("gpsimd", cc, cc_rs)
        dma("gpsimd", "dbg", dbg["rso"][:, :], rso_h[:, :])

    b.tag = "FB"

    fwa = A(OF_WA, BF16, 8 * 4096)
    fwb = A(OF_WB, BF16, 32 * 1024)
    hts = A(OF_HTS, BF16, 32 * 256)

    def xet(slot):
        return A(OF_XET + slot * 8 * 256 * 2, BF16, 8 * 256)

    def fsb(slot):
        return A(OF_FSB + slot * 1024 * 2, BF16, 1024)

    W_("sync", pe, pe.val)
    W_("sync", dv, dv.val)
    W_("sync", ac, ac.val)
    W_("gpsimd", pe, pe.val)
    W_("gpsimd", dv, dv.val)
    W_("gpsimd", ac, ac.val)
    for cn in ("x1t0", "x1t1", "x1f0", "x1f1", "x1b0", "x1b1", "wm"):
        W_("sync", ch(cn))
        W_("gpsimd", ch(cn))
    W_("gpsimd", ch("rs"))

    def ffn_block(wname1, wname2, w1ap, w2ap, b1col, b2row, xt_src, nchunks,
                  ydst, ych, aux=None, agchunk=None):
        w1ch = ch(wname1)
        w2ch = ch(wname2)
        for cn in ("yf0", "yf1", "ye0", "ye1", "yz",
                   "xg0", "xg1", "xg2", "xg3", "xg4", "xg5"):
            if cn in CH:
                W_("scalar", ch(cn))
                W_("vector", ch(cn))
                W_("sync", ch(cn))
        W_("gpsimd", pe, pe.val)
        W_("gpsimd", ac, ac.val)
        W_("tensor", ch("b2r"))
        for kc in range(8):
            for hf in range(2):
                E_("gpsimd", "dma_start",
                   out=fwa[:, kc * 4096 + hf * 2048:kc * 4096 + (hf + 1) * 2048],
                   in_=w1ap[kc * 128:(kc + 1) * 128,
                            hf * 2048:(hf + 1) * 2048], inc=w1ch)
        w1_ld = w1ch.val
        for iik in range(32):
            E_("gpsimd", "dma_start",
               out=fwb[:, iik * 1024:(iik + 1) * 1024],
               in_=w2ap[iik * 128:(iik + 1) * 128, :], inc=w2ch)
        w2_ld = w2ch.val
        xet_free = [None, None]
        ybank_free = {4: (ac, ac.val), 5: (ac, ac.val),
                      6: (ac, ac.val), 7: (ac, ac.val)}
        yst_free = [None, None]
        w2_done_prev = (pe, pe.val)
        for tc in range(nchunks):
            xsl = tc % 2
            WT("sync", xet_free[xsl])
            _, xtoks = xt_src(tc, xsl)
            if tc == 0:
                W_("tensor", w1ch, w1_ld)
            for xtok in xtoks:
                W_("tensor", xtok[0], xtok[1])
            for iic in range(32):
                if aux is not None:
                    aux()
                bk0 = iic % 4
                W_("tensor", ac, bank_free[bk0])
                for kc in range(8):
                    E_("tensor", "matmul", pp[bk0][:, 0:256],
                       fwa[:, kc * 4096 + iic * 128:kc * 4096 + (iic + 1) * 128],
                       xet(xsl)[:, kc * 256:(kc + 1) * 256],
                       start=(kc == 0), stop=(kc == 7),
                       inc=(pe if kc == 7 else None))
                if iic == 31:
                    xet_free[xsl] = (pe, pe.val)
                W_("scalar", pe, pe.val)
                if iic == 0:
                    WT("scalar", w2_done_prev)
                E_("scalar", "activation", hts[:, iic * 256:(iic + 1) * 256],
                   pp[bk0][:, 0:256], AF.Gelu, bias=b1col[:, iic:iic + 1],
                   drain=False, inc=ac)
                bank_free[bk0] = ac.val
            hts_done = ac.val
            if tc == 0:
                W_("tensor", w2ch, w2_ld)
            for tt4 in range(2):
                tt = tc * 2 + tt4
                ysl = tt % 2
                for oc in range(2):
                    bky = 4 + 2 * tt4 + oc
                    WT("tensor", ybank_free[bky])
                    if tt4 == 0 and oc == 0:
                        W_("tensor", ac, hts_done)
                    for iik in range(32):
                        E_("tensor", "matmul", pp[bky][:, :],
                           hts[:, iik * 256 + tt4 * 128:
                               iik * 256 + tt4 * 128 + 128],
                           fwb[:, iik * 1024 + oc * 512:
                               iik * 1024 + (oc + 1) * 512],
                           start=(iik == 0), stop=False)
                    E_("tensor", "matmul", pp[bky][:, :], onez_s[:, :],
                       b2row[0:1, oc * 512:(oc + 1) * 512],
                       start=False, stop=True, inc=pe)
                    if tt4 == 1 and oc == 1:
                        w2_done_prev = (pe, pe.val)
                    W_("scalar", pe, pe.val)
                    if oc == 0:
                        WT("scalar", yst_free[ysl])
                    E_("scalar", "copy", fsb(ysl)[:, oc * 512:(oc + 1) * 512],
                       pp[bky][:, :], drain=False, inc=ac)
                    ybank_free[bky] = (ac, ac.val)
                W_("sync", ac, ac.val)
                v = dma("sync", ych + str(ysl),
                        ydst[tt * 128:(tt + 1) * 128, :], fsb(ysl)[:, :])
                yst_free[ysl] = (CH[ych + str(ysl)], v)
                if agchunk is not None:
                    agchunk(tt)

    def fb_xt(tc, xsl):
        toks = []
        for q in range(2):
            chn = "xt%d_%d" % (xsl, q)
            v = dma("sync", chn, xet(xsl)[:, q * 1024:(q + 1) * 1024],
                    AP(x1t_h.tensor, tc * 256 + q * 4 * 128 * N,
                       [[N, 128], [128 * N, 4], [1, 256]]))
            toks.append((ch(chn), v))
        return None, toks

    def scst(gsl):
        return A(OF_SCST + gsl * 1024 * 2, BF16, 1024)

    sc_state = {"n": 0, "c": 0}
    xg_free = [None] * 6

    def scatter_aux():
        sc_state["n"] += 1
        if sc_state["n"] <= 128:
            return
        c = sc_state["c"]
        if c >= 256:
            return
        gsl = c % 6
        if c == 0:
            W_("sync", cc, cc_ag2)
            W_("gpsimd", dv, sl32_done)
        WT("sync", xg_free[gsl])
        v = dma("sync", "xg%d" % gsl, scst(gsl)[:, :],
                x1g_h[c * 128:(c + 1) * 128, :])
        W_("gpsimd", ch("xg%d" % gsl), v)
        E_("gpsimd", "indirect_dma_start",
           out=xe_h[:, :],
           out_offset=IndirectOffsetOnAxis(slotc32_s[:, c:c + 1], 0),
           in_=scst(gsl)[:, :], in_offset=None, inc=ch("sc%d" % gsl))
        xg_free[gsl] = (ch("sc%d" % gsl), ch("sc%d" % gsl).val)
        sc_state["c"] = c + 1

    ffn_block("fw1", "fw2", fw1_i, fw2_i, fb1_s, fb2row_s, fb_xt, 16,
              yfb_h, "yf", aux=scatter_aux)
    assert sc_state["c"] == 256, sc_state
    if "yfb" in dbg:
        W_("gpsimd", ch("yf0"))
        W_("gpsimd", ch("yf1"))
        dma("gpsimd", "dbg", dbg["yfb"][:, :], yfb_h[:, :])

    b.tag = "EXP"
    W_("vector", dv, dv.val)
    for cn in ("yf0", "yf1", "hs0", "hs1"):
        if cn in CH:
            W_("vector", ch(cn))
    E_("vector", "memset", fsb(0)[:, :], 0.0, inc=dv)
    zb_done = dv.val
    W_("sync", dv, zb_done)
    for blk in range(8):
        dma("sync", "yz", yall_h[FBROW + blk * 128:FBROW + (blk + 1) * 128, :],
            fsb(0)[:, :])
    for cn in ("sc0", "sc1", "sc2", "sc3", "sc4", "sc5"):
        W_("sync", ch(cn))
    if "xe" in dbg:
        for cn in ("sc0", "sc1", "sc2", "sc3", "sc4", "sc5"):
            W_("gpsimd", ch(cn))
        dma("gpsimd", "dbg", dbg["xe"][:, :], xe_h[:, :])

    def ex_xt(tc, xsl):
        vq = {}
        for dt_ in range(8):
            chn = "xt%d_%d" % (xsl, dt_ % 2)
            vq[chn] = dma("sync", chn,
                          xet(xsl)[:, dt_ * 256:(dt_ + 1) * 256],
                          AP(xe_h.tensor, tc * 256 * D + dt_ * 128,
                             [[D, 256], [1, 128]]),
                          transpose=True)
        return None, [(ch(c), v) for c, v in vq.items()]

    agst = {"k": 0}

    def ag3_chunk(tt):
        if (tt + 1) % 8:
            return
        k = tt // 8
        W_("gpsimd", CH["ye0"])
        W_("gpsimd", CH["ye1"])
        E_("gpsimd", "collective_compute", "AllGather", ALU.bypass,
           [list(range(NCORE))],
           [AP(ye_h.tensor, k * 1024 * D, [[1, 1024 * D]])],
           [AP(yall_h.tensor, k * 8192 * D, [[1, 8192 * D]])],
           inc=cc)
        agst["k"] = k + 1

    ffn_block("w1e", "w2e", w1e_i, w2e_i, b1_s, b2row_s, ex_xt, 40,
              ye_h, "ye", agchunk=ag3_chunk)
    assert agst["k"] == 10, agst
    cc_ag3 = cc.val
    if "ye" in dbg:
        for cn in ("ye0", "ye1"):
            W_("gpsimd", ch(cn))
        dma("gpsimd", "dbg", dbg["ye"][:, :], ye_h[:, :])
        W_("gpsimd", ch("dbg"))

    b.tag = "CMB"
    OF_CHT = 32 * 1024
    locv = A(OF_LOC, F32, 8 * 32)
    li32 = A(OF_LI32, I32, 64)

    def g0b(slot):
        return A(OF_G0 + slot * D * 2, BF16, D)

    def g1b(slot):
        return A(OF_G1R + slot * D * 2, BF16, D)

    def yfbb(slot):
        return A(OF_YFB + slot * D * 2, BF16, D)

    def cx1(slot):
        return A(OF_CX1 + slot * D * 4, F32, D)

    def cst(slot):
        return A(OF_CST + slot * D * 4, F32, D)

    W_("sync", cc, cc_rs)
    W_("sync", pe, pe.val)
    W_("sync", ac, ac.val)
    W_("sync", dv, dv.val)
    W_("vector", pe, pe.val)
    W_("vector", ac, ac.val)
    W_("scalar", pe, pe.val)
    W_("scalar", dv, dv.val)
    for cn in ("ye0", "ye1", "xt0_0", "xt0_1", "xt1_0", "xt1_1"):
        if cn in CH:
            W_("vector", ch(cn))
            W_("sync", ch(cn))
    W_("gpsimd", cc, cc_ag3)
    W_("gpsimd", ch("yz"))
    rlv = 0
    for arr_i in range(5):
        rlv = dma("sync", "rl2", csrow[0:32, 0:128] if False else
                  A(OF_SCR, F32, 128, rows=32)[0:32, :],
                  rso_h[arr_i * 32:(arr_i + 1) * 32, :])
        W_("tensor", ch("rl2"), rlv)
        W_("tensor", ac, bank_free[4])
        E_("tensor", "transpose", pp[4][:, 0:32],
           A(OF_SCR, F32, 128, rows=32)[0:32, :], idn32_s[0:32, 0:32],
           inc=pe)
        W_("scalar", pe, pe.val)
        E_("scalar", "activation", locv[:, arr_i * 32:(arr_i + 1) * 32],
           pp[4][:, 0:32], AF.Identity, scale=1.0, inc=ac)
        bank_free[4] = ac.val
        W_("sync", pe, pe.val)
    loc_done = ac.val
    W_("vector", ac, loc_done)
    E_("vector", "tensor_copy", li32[:, 0:32], locv[:, 0:32])
    E_("vector", "tensor_copy", li32[:, 32:64], locv[:, 32:64], inc=dv)
    li_done = dv.val
    g_free = [None, None, None, None]
    yf_free = [None, None]
    cx_free = [None, None]
    cs_free = [None, None]
    ost_free = [None, None]
    W_("gpsimd", dv, li_done)
    for lt in range(NT):
        sl = lt % 2
        WT("gpsimd", g_free[sl])
        E_("gpsimd", "indirect_dma_start",
           out=g0b(sl)[:, :], out_offset=None, in_=yall_h[:, :],
           in_offset=IndirectOffsetOnAxis(li32[:, lt:lt + 1], 0),
           inc=ch("cg0%d" % sl))
        WT("gpsimd", g_free[2 + sl])
        E_("gpsimd", "indirect_dma_start",
           out=g1b(sl)[:, :], out_offset=None, in_=yall_h[:, :],
           in_offset=IndirectOffsetOnAxis(li32[:, 32 + lt:33 + lt], 0),
           inc=ch("cg1%d" % sl))
        WT("sync", yf_free[sl])
        yfv = dma("sync", "cy%d" % sl, yfbb(sl)[:, :],
                  yfb_h[lt * 128:(lt + 1) * 128, :])
        WT("sync", cx_free[sl])
        cxv = dma("sync", "cx%d" % sl, cx1(sl)[:, :],
                  x1_h[lt * 128:(lt + 1) * 128, :])
        W_("vector", ch("cg0%d" % sl), CH["cg0%d" % sl].val)
        W_("vector", ch("cg1%d" % sl), CH["cg1%d" % sl].val)
        W_("vector", ch("cy%d" % sl), yfv)
        W_("vector", ch("cx%d" % sl), cxv)
        WT("vector", cs_free[sl])
        s2c = cst(sl)
        E_("vector", "tensor_scalar", s2c[:, :], g0b(sl)[:, :],
           locv[:, 64 + lt:65 + lt], None, ALU.mult)
        E_("vector", "scalar_tensor_tensor",
           A(OF_CHT, F32, 1024)[:, :], g1b(sl)[:, :],
           locv[:, 96 + lt:97 + lt], s2c[:, :], ALU.mult, ALU.add)
        E_("vector", "tensor_sub", s2c[:, :], yfbb(sl)[:, :],
           A(OF_CHT, F32, 1024)[:, :], inc=dv)
        g_free[sl] = (dv, dv.val)
        g_free[2 + sl] = (dv, dv.val)
        yf_free[sl] = (dv, dv.val)
        E_("vector", "scalar_tensor_tensor",
           A(OF_CHT + 4096, F32, 1024)[:, :], s2c[:, :],
           locv[:, 128 + lt:129 + lt], A(OF_CHT, F32, 1024)[:, :],
           ALU.mult, ALU.add)
        E_("vector", "tensor_add", s2c[:, :],
           A(OF_CHT + 4096, F32, 1024)[:, :], cx1(sl)[:, :], inc=dv)
        cx_free[sl] = (dv, dv.val)
        E_("vector", "reduce_sum", rcmb[:, 0:1], s2c[:, :], axis=AX, inc=dv)
        W_("scalar", dv, dv.val)
        E_("scalar", "activation", A(OF_CHT, F32, 1024)[:, :], s2c[:, :],
           AF.Square, accum_out=rcmb[:, 1:2], inc=ac)
        W_("vector", ac, ac.val)
        E_("vector", "tensor_scalar", rcmb[:, 2:3], rcmb[:, 0:1],
           1.0 / D, None, ALU.mult)
        E_("vector", "tensor_scalar", rcmb[:, 3:4], rcmb[:, 1:2],
           1.0 / D, None, ALU.mult)
        E_("vector", "tensor_mul", rcmb[:, 4:5], rcmb[:, 2:3], rcmb[:, 2:3])
        E_("vector", "tensor_sub", rcmb[:, 5:6], rcmb[:, 3:4], rcmb[:, 4:5])
        E_("vector", "tensor_scalar", rcmb[:, 6:7], rcmb[:, 5:6],
           EPS, None, ALU.add, inc=dv)
        W_("scalar", dv, dv.val)
        E_("scalar", "sqrt", rcmb[:, 7:8], rcmb[:, 6:7], drain=False, inc=ac)
        W_("vector", ac, ac.val)
        E_("vector", "reciprocal", rcmb[:, 8:9], rcmb[:, 7:8], inc=dv)
        W_("vector", dv, dv.val)
        E_("vector", "tensor_scalar", A(OF_CHT, F32, 1024)[:, :], s2c[:, :],
           rcmb[:, 2:3], rcmb[:, 8:9], ALU.subtract, ALU.mult, inc=dv)
        out_done = dv.val
        cs_free[sl] = (dv, out_done)
        W_("sync", dv, out_done)
        WT("sync", ost_free[sl])
        ov = dma("sync", "o%d" % sl, out[lt * 128:(lt + 1) * 128, :],
                 A(OF_CHT, F32, 1024)[:, :], )
        ost_free[sl] = (CH["o%d" % sl], ov)
        WT("vector", ost_free[sl])
    W_("gpsimd", ch("o0"))
    W_("gpsimd", ch("o1"))


    for name, c in CH.items():
        if name == "dbg":
            W_("gpsimd", c)
    W_("gpsimd", od, od.val)
    with es, nc.Block() as block:
        @block.sync
        def _(h):
            for f in b.q["sync"]:
                f(h)

        @block.gpsimd
        def _(h):
            for f in b.q["gpsimd"]:
                f(h)

        @block.scalar
        def _(h):
            for f in b.q["scalar"]:
                f(h)

        @block.vector
        def _(h):
            for f in b.q["vector"]:
                f(h)

        @block.tensor
        def _(h):
            for f in b.q["tensor"]:
                f(h)

    return nc, dbg



def _host_constants():
    eye = np.eye(128)
    cons = {
        "idn": eye.astype(bfloat16),
        "idn32": eye.astype(np.float32),
        "ltx": np.triu(np.ones((128, 128)), 1).astype(bfloat16),
        "anti": eye[::-1].copy().astype(bfloat16),
        "onez": np.ones((1, 128), dtype=bfloat16),
        "onez32": np.ones((1, 128), dtype=np.float32),
        "eidx": np.broadcast_to(np.arange(8, dtype=np.float32),
                                (128, 8)).copy(),
    }
    bt = _bucket_table()
    g1 = np.zeros((384, NREL), dtype=bfloat16)
    g1[bt, np.arange(NREL)] = bfloat16(1.0)
    cons["g1h"] = g1
    return cons


def make_in_maps(inputs):
    cons = _host_constants()
    gc = np.asarray(inputs["L0_attn_gru_const"], np.float32).reshape(H)
    gru_c = np.zeros((48, 1), np.float32)
    gru_c[0:16, 0] = gc
    gru_c[32:48, 0] = gc
    in_maps = []
    for c in range(NCORE):
        m = {
            "x": np.ascontiguousarray(
                np.asarray(inputs["L0_attn_hidden_states"],
                           np.float32)[c]),
            "wq": np.asarray(inputs["L0_attn_Wq"], np.float32),
            "wk": np.asarray(inputs["L0_attn_Wk"], np.float32),
            "wv": np.asarray(inputs["L0_attn_Wv"], np.float32),
            "wo": np.asarray(inputs["L0_attn_Wo"], np.float32),
            "bq": np.asarray(inputs["L0_attn_bq"], np.float32),
            "bk": np.asarray(inputs["L0_attn_bk"], np.float32),
            "bv": np.asarray(inputs["L0_attn_bv"], np.float32),
            "bo": np.asarray(inputs["L0_attn_bo"], np.float32),
            "gru_w": np.asarray(inputs["L0_attn_gru_W"], np.float32),
            "gru_b": np.asarray(inputs["L0_attn_gru_b"],
                                np.float32).reshape(1, 8),
            "gru_c": gru_c,
            "rel_embed": np.asarray(inputs["L0_attn_rel_embed"], np.float32),
            "wr": np.asarray(inputs["L0_ffn_Wr"], np.float32),
            "w1e": np.ascontiguousarray(
                np.asarray(inputs["L0_ffn_W1"], np.float32)[c]),
            "b1e": np.ascontiguousarray(
                np.asarray(inputs["L0_ffn_b1"], np.float32)[c]),
            "w2e": np.ascontiguousarray(
                np.asarray(inputs["L0_ffn_W2"], np.float32)[c]),
            "b2e": np.ascontiguousarray(
                np.asarray(inputs["L0_ffn_b2"], np.float32)[c]),
            "fw1": np.asarray(inputs["L0_ffn_fw1"], np.float32),
            "fb1": np.asarray(inputs["L0_ffn_fb1"], np.float32),
            "fw2": np.asarray(inputs["L0_ffn_fw2"], np.float32),
            "fb2": np.asarray(inputs["L0_ffn_fb2"], np.float32),
            "coreoh": np.eye(8, dtype=np.float32)[c:c + 1],
        }
        m.update(cons)
        in_maps.append(m)
    return in_maps


def kernel(**inputs):
    nc, dbg = build_program()
    in_maps = make_in_maps(inputs)
    res = run_bass_kernel_spmd(nc, in_maps, core_ids=list(range(NCORE)))
    outs = np.stack([np.asarray(r["out"]) for r in res.results], axis=0)
    if DBG:
        kernel.dbg_results = [
            {k: np.asarray(r["dbg_" + k]) for k in DBG}
            for r in res.results]
    return outs


# version 13
# speedup vs baseline: 1.0851x; 1.0851x over previous
import math
import os
from contextlib import ExitStack

import numpy as np
import ml_dtypes

import concourse.bass as bass
import concourse.mybir as mybir
from concourse.ap import AP
from concourse.bass_core_rust import IndirectOffsetOnAxis
from concourse.bass_utils import run_bass_kernel_spmd

F32 = mybir.dt.float32
BF16 = mybir.dt.bfloat16
I16 = mybir.dt.int16
I32 = mybir.dt.int32
AF = mybir.ActivationFunctionType
ALU = mybir.AluOpType
AX = mybir.AxisListType.X
bfloat16 = ml_dtypes.bfloat16

NCORE = 8
N = 4096
D = 1024
H = 16
DH = 64
II = 4096
E = 8
T = NCORE * N
CAP = 10240
YROWS = CAP + 128
FBROW = 81920
NB = 320
NREL = 2 * N - 1
BRP = NREL + 257
EPS = 1e-5
NT = N // 128
FAR_ABS = 789
NDELTA = 10
DBG = [x for x in os.environ.get("MK_DEBUG", "").split(",") if x]
DRAIN = os.environ.get("MK_DRAIN", "1") != "0"


def _near_pair(jt, it):
    d = abs(jt - it)
    return d == 0 or (d - 1) * 128 + 1 < FAR_ABS


def _tile_region(jt, ic):
    if any(_near_pair(jt, ic * 4 + s) for s in range(4)):
        return 0
    return 1 if jt > ic * 4 else 2


def _bucket_table():
    r = np.arange(-(N - 1), N, dtype=np.int64)
    nb = NB // 2
    b = (r > 0).astype(np.int64) * nb
    ra = np.abs(r)
    mx = nb // 2
    small = ra < mx
    lg = np.log(np.maximum(ra, 1).astype(np.float64) / mx) \
        / math.log(800.0 / mx) * (nb - mx)
    lg = np.minimum((mx + lg).astype(np.int64), nb - 1)
    return (b + np.where(small, ra, lg)).astype(np.int64)


class Ctr:
    def __init__(self, sem, step):
        self.sem, self.step, self.val = sem, step, 0


EMITTED = {}


class Bld:
    ENG = ("sync", "gpsimd", "scalar", "vector", "tensor")

    def __init__(self, nc, es):
        self.nc, self.es = nc, es
        self.q = {e: [] for e in self.ENG}
        self.ctr = {}
        self.tag = ""

    def counter(self, name, step):
        sem = self.es.enter_context(self.nc.semaphore(name))
        self.ctr[name] = Ctr(sem, step)
        return self.ctr[name]

    def E(self, eng, meth, *args, inc=None, drain=None, **kw):
        pairs = []
        if inc is not None:
            for c in (inc if isinstance(inc, (list, tuple)) else [inc]):
                c.val += c.step
                pairs.append((c.sem, c.step))
        tag = f"{self.tag}|{eng}.{meth}#{len(self.q[eng])}"
        if drain is None:
            drain = (DRAIN and eng in ("vector", "scalar", "gpsimd")
                     and meth not in ("dma_start", "indirect_dma_start",
                                      "collective_compute",
                                      "dma_start_transpose"))
        else:
            drain = drain and DRAIN

        def f(h, meth=meth, args=args, kw=kw, pairs=pairs, tag=tag,
              drain=drain):
            inst = getattr(h, meth)(*args, **kw)
            try:
                EMITTED[inst.ins.name] = tag
            except Exception:
                pass
            for sem, step in pairs:
                inst = inst.then_inc(sem, step)
            if drain:
                h.drain()
        self.q[eng].append(f)

    def W(self, eng, c, val=None):
        v = c.val if val is None else val
        if v <= 0:
            return

        def f(h, sem=c.sem, v=v):
            h.wait_ge(sem, v)
        self.q[eng].append(f)

    def WT(self, eng, tok):
        if tok is not None:
            self.W(eng, tok[0], tok[1])

    def raw(self, eng, fn):
        self.q[eng].append(fn)


ARENA_BYTES = 168 * 1024
OF_VAUG = 0
OF_GCOL = OF_VAUG + NT * H * 65 * 2
OF_GCOLR = OF_GCOL + NT * H * 4
OF_WPM = OF_GCOLR + NT * H * 4
PERS_END = OF_WPM + NT * H * 2 * 4
OF_STG = PERS_END
OF_STGB = OF_STG + 2 * D * 4
OF_XA = OF_STGB + 2 * D * 2
OF_XTC = OF_XA + 2 * D * 4
OF_WSTR = OF_XTC + 8 * 512 * 2
OF_G1H = OF_WSTR + 2 * 8 * 128 * 2
P01_END = OF_G1H + 8192 * 2
OF_KTB = OF_XTC
OF_QPAD = OF_KTB + 2 * 4096 * 2
OF_A1T = OF_QPAD + 4 * 512 * 2
OF_DIAG = OF_A1T + 2 * 21 * 128 * 2
OF_EXPS = OF_DIAG + 2 * 4 * 128 * 2
OF_RCMB = OF_EXPS + 4 * 512 * 2
OF_ATTS = OF_RCMB + 1024 * 4
ATT_END = OF_ATTS + 2 * 512 * 2
assert ATT_END <= ARENA_BYTES and P01_END <= ARENA_BYTES
OF_ATTLD = OF_KTB
OF_RTR = OF_ATTLD + 2 * 8 * 128 * 2
OF_X1T32 = OF_RTR + 256 * 4
OF_WMST = OF_X1T32 + D * 4
OF_MSB = OF_VAUG
OF_POSS = OF_MSB + 8 * 256 * 2
OF_KEPT = OF_POSS + 8 * 256 * 4
OF_SCR = OF_KEPT + 8 * 256 * 4
OF_IDXA = OF_SCR + 8 * 256 * 4
OF_ELIGB = OF_IDXA + 16 * 256 * 4
OF_BBF = OF_ELIGB + 256 * 2
OF_ONESR = OF_BBF + 256 * 2
OF_CSROW = OF_ONESR + 256 * 4
SCAN_END = OF_CSROW + 512 * 4
assert SCAN_END <= OF_STG
OF_WA = 0
OF_WB = OF_WA + 8 * 4096 * 2
OF_HTS = OF_WB + 32 * 1024 * 2
OF_XET = OF_HTS + 32 * 256 * 2
OF_FSB = OF_XET + 2 * 8 * 256 * 2
OF_SCST = OF_FSB + 2 * 1024 * 2
FFN_END = OF_SCST + 6 * 1024 * 2
assert FFN_END <= ARENA_BYTES
OF_LOC = 0
OF_LI32 = OF_LOC + 8 * 32 * 4
OF_G0 = OF_LI32 + 64 * 4
OF_G1R = OF_G0 + 2 * D * 2
OF_YFB = OF_G1R + 2 * D * 2
OF_CX1 = OF_YFB + 2 * D * 2
OF_CST = OF_CX1 + 2 * D * 4
CMB_END = OF_CST + 2 * D * 4


def build_program():
    nc = bass.Bass("TRN2", target_bir_lowering=False, debug=False,
                   detect_race_conditions=False)

    def din(name, shape, dt):
        return nc.dram_tensor(name, list(shape), dt, kind="ExternalInput").ap()

    x_in = din("x", (N, D), F32)
    w_in = {n: din(n, (D, D), F32) for n in ("wq", "wk", "wv", "wo")}
    bq_i = din("bq", (D,), F32)
    bk_i = din("bk", (D,), F32)
    bv_i = din("bv", (D,), F32)
    bo_i = din("bo", (D,), F32)
    gru_w = din("gru_w", (8, DH), F32)
    gru_b = din("gru_b", (1, 8), F32)
    gru_c = din("gru_c", (48, 1), F32)
    rel_e = din("rel_embed", (NB, H), F32)
    wr_i = din("wr", (E, D), F32)
    w1e_i = din("w1e", (D, II), F32)
    b1e_i = din("b1e", (II,), F32)
    w2e_i = din("w2e", (II, D), F32)
    b2e_i = din("b2e", (D,), F32)
    fw1_i = din("fw1", (D, II), F32)
    fb1_i = din("fb1", (II,), F32)
    fw2_i = din("fw2", (II, D), F32)
    fb2_i = din("fb2", (D,), F32)
    idn_i = din("idn", (128, 128), BF16)
    idn32_i = din("idn32", (128, 128), F32)
    ltx_i = din("ltx", (128, 128), BF16)
    anti_i = din("anti", (128, 128), BF16)
    onez_i = din("onez", (1, 128), BF16)
    onez32_i = din("onez32", (1, 128), F32)
    g1h_i = din("g1h", (384, NREL), BF16)
    eidx_i = din("eidx", (128, 8), F32)
    coreoh_i = din("coreoh", (1, 8), F32)

    out = nc.dram_tensor("out", [N, D], F32, kind="ExternalOutput").ap()

    dbg_specs = {
        "qt": ((D, N), BF16), "kt": ((D, N), BF16),
        "vaug": ((128, NT * H * 65), BF16), "gcol": ((128, NT * H), F32),
        "gcolr": ((128, NT * H), F32), "wpm": ((128, NT * H * 2), F32),
        "biasrel": ((H, NREL), BF16), "x1": ((N, D), F32),
        "attnT": ((D, N), BF16), "wm": ((E, N), BF16),
        "pos": ((128, 8 * 256), F32), "kept": ((128, 8 * 256), F32),
        "rso": ((160, 128), F32), "slotc": ((128, 256), F32),
        "yfb": ((N, D), BF16), "ye": ((YROWS, D), BF16),
        "xe": ((CAP + 1, D), BF16), "idxa": ((128, 16 * 256), F32),
    }
    dbg = {}
    for name in DBG:
        shp, dt = dbg_specs[name]
        dbg[name] = nc.dram_tensor("dbg_" + name, list(shp), dt,
                                   kind="ExternalOutput").ap()

    def hbm(name, shape, dt):
        return nc.dram_tensor(name, list(shape), dt, kind="Internal").ap()

    biasrel_h = hbm("biasrel", (H, BRP), BF16)
    wT_h = {n: hbm(n + "T", (D, D), BF16) for n in ("wq", "wk", "wv", "wo")}
    qt_h = hbm("qt", (D, N), BF16)
    kt_h = hbm("kt", (D, N), BF16)
    attnT_h = hbm("attnT", (D, N), BF16)
    x1_h = hbm("x1f", (N, D), F32)
    x1bf_h = hbm("x1bf", (N, D), BF16)
    x1g_h = hbm("x1g", (T, D), BF16)
    x1t_h = hbm("x1t", (D, N), BF16)
    wmT_h = hbm("wmT", (E, N), BF16)
    wall_h = hbm("wall", (NCORE * E, N), BF16)
    rsi_h = hbm("rs_in", (NCORE * 160, 128), F32)
    rso_h = hbm("rs_out", (160, 128), F32)
    xe_h = hbm("xe", (CAP + 1, D), BF16)
    ht_h = hbm("ht", (II, CAP), BF16)
    hft_h = hbm("hft", (II, N), BF16)
    ye_h = hbm("ye", (YROWS, D), BF16)
    yall_h = hbm("yall", (NCORE * YROWS, D), BF16)
    yfb_h = hbm("yfb", (N, D), BF16)

    es = ExitStack()

    def S(name, shape, dt):
        return es.enter_context(nc.sbuf_tensor(name, list(shape), dt))

    bq_s = S("bq_s", (128, 8), F32)
    bk_s = S("bk_s", (128, 8), F32)
    bv_bc = S("bv_bc", (128, D), F32)
    bo_bc = S("bo_bc", (128, D), F32)
    b2_bc = S("b2_bc", (128, D), F32)
    fb2_bc = S("fb2_bc", (128, D), F32)
    b1_s = S("b1_s", (128, II // 128), F32)
    fb1_s = S("fb1_s", (128, II // 128), F32)
    gwa = S("gwa", (128, 8 * 48), BF16)
    gba = S("gba", (48, 1), F32)
    gruc_s = S("gruc_s", (48, 1), F32)
    bpm_bc = S("bpm_bc", (128, 2 * H), F32)
    idn_s = S("idn_s", (128, 128), BF16)
    idn32_s = S("idn32_s", (128, 128), F32)
    ltx_s = S("ltx_s", (128, 128), BF16)
    anti_s = S("anti_s", (128, 128), BF16)
    onez_s = S("onez_s", (1, 128), BF16)
    onez32_s = S("onez32_s", (1, 128), F32)
    onecol_s = S("onecol_s", (128, 8), BF16)
    eidx_s = S("eidx_s", (128, 8), F32)
    coreoh_bc = S("coreoh_bc", (128, 8), F32)
    ones8_s = S("ones8_s", (128, 8), F32)
    zrow_s = S("zrow_s", (1, D), BF16)
    wrT_s = S("wrT_s", (128, 8 * E), F32)
    slotc_s = S("slotc_s", (128, 256), F32)
    slotc32_s = S("slotc32_s", (128, 256), I32)
    rel_s = S("rel_s", (128, 16), BF16)
    b2row_s = S("b2row_s", (1, D), BF16)
    fb2row_s = S("fb2row_s", (1, D), BF16)
    small_s = S("small_s", (128, 64), F32)

    arena = S("arena", (128, ARENA_BYTES), mybir.dt.uint8)

    def A(off, dt, cols, rows=128, row0=0):
        esz = {F32: 4, BF16: 2, I32: 4, I16: 2}[dt]
        v = arena.bitcast(dt)
        return v[row0:row0 + rows, off // esz:off // esz + cols]

    pp = [es.enter_context(nc.psum_tensor(f"pp{i}", [128, 512], F32))
          for i in range(8)]

    def ppb(i):
        return pp[i].bitcast(BF16)

    b = Bld(nc, es)
    qd = b.counter("qd", 16)
    gd = b.counter("gd", 16)
    sd = b.counter("sd", 16)
    pe = b.counter("pe", 1)
    ac = b.counter("ac", 1)
    dv = b.counter("dv", 1)
    cc = b.counter("cc", 1)
    od = b.counter("od", 16)
    E_, W_ = b.E, b.W

    def stg(slot):
        return A(OF_STG + slot * D * 4, F32, D)

    def stgb(slot):
        return A(OF_STGB + slot * D * 2, BF16, D)

    def xa(slot):
        return A(OF_XA + slot * D * 4, F32, D)

    b.tag = "P0"
    CH = {}

    CHAL = {}
    for _s in range(2):
        for _q in range(4):
            for _fam in ("fx", "hl", "ex"):
                CHAL["%s%d_%d" % (_fam, _s, _q)] = "xq%d_%d" % (_s, _q)
    CHAL.update({"hs2": "kt0", "hs3": "kt1"})

    def ch(name):
        name = CHAL.get(name, name)
        if name not in CH:
            CH[name] = b.counter("c_" + name, 16)
        return CH[name]

    def dma(eng, name, out, in_, **kw):
        c = ch(name)
        E_(eng, "dma_start", out=out, in_=in_, inc=c, **kw)
        return c.val

    for dst, srcp in ((idn_s, idn_i), (idn32_s, idn32_i), (ltx_s, ltx_i),
                      (anti_s, anti_i), (onez_s, onez_i),
                      (onez32_s, onez32_i), (eidx_s, eidx_i),
                      (gruc_s, gru_c)):
        dma("sync", "k0", dst[:, :], srcp[:, :])
    dma("sync", "k0", stgb(1).bitcast(F32)[0:8, 0:128],
        AP(bq_i.tensor, 0, [[128, 8], [1, 128]]))
    dma("sync", "k0", stgb(1).bitcast(F32)[0:8, 128:256],
        AP(bk_i.tensor, 0, [[128, 8], [1, 128]]))
    dma("sync", "k0", A(OF_XTC, F32, 256)[0:32, 0:128],
        AP(b1e_i.tensor, 0, [[128, 32], [1, 128]]))
    dma("sync", "k0", A(OF_XTC, F32, 256)[0:32, 128:256],
        AP(fb1_i.tensor, 0, [[128, 32], [1, 128]]))
    dma("sync", "k0", small_s[0:1, 0:16],
        AP(rel_e.tensor, 159 * H, [[H, 1], [1, H]]))
    dma("sync", "k0", small_s[0:1, 16:32],
        AP(rel_e.tensor, 319 * H, [[H, 1], [1, H]]))
    dma("sync", "k0", small_s[0:1, 32:40], coreoh_i[:, :])
    dma("sync", "k0", small_s[0:1, 40:48], gru_b[:, :])
    for g in range(4):
        dma("sync", "k0", stg(0)[0:2, g * DH:(g + 1) * DH],
            AP(gru_w.tensor, g * DH, [[4 * DH, 2], [1, DH]]))
    dma("sync", "k0", stg(1)[0:1, 0:512],
        AP(bv_i.tensor, 0, [[D, 1], [1, 512]]))
    dma("sync", "k0", xa(0)[0:1, 0:512],
        AP(bv_i.tensor, 512, [[D, 1], [1, 512]]))
    dma("sync", "k0", stg(1)[0:1, 512:1024],
        AP(bo_i.tensor, 0, [[D, 1], [1, 512]]))
    dma("sync", "k0", xa(0)[0:1, 512:1024],
        AP(bo_i.tensor, 512, [[D, 1], [1, 512]]))
    dma("gpsimd", "b2r", b2row_s[0:1, :],
        AP(b2e_i.tensor, 0, [[D, 1], [1, 1024]]))
    dma("gpsimd", "b2r", fb2row_s[0:1, :],
        AP(fb2_i.tensor, 0, [[D, 1], [1, 1024]]))
    dma("sync", "k0", xa(1)[0:1, 0:512],
        AP(b2e_i.tensor, 0, [[D, 1], [1, 512]]))
    dma("sync", "k0", xa(1)[0:1, 512:1024],
        AP(b2e_i.tensor, 512, [[D, 1], [1, 512]]))
    dma("sync", "k0", stg(0)[0:1, 512:1024],
        AP(fb2_i.tensor, 0, [[D, 1], [1, 512]]))
    dma("sync", "k0", stg(0)[0:1, 256 + 768:1024 + 256],
        AP(fb2_i.tensor, 512, [[D, 1], [1, 512]])) if False else None
    dma("sync", "k0", xa(0).bitcast(F32)[0:1, 1024 + 0:1024 + 0], None) \
        if False else None
    fb2hi_row = A(OF_XTC + 1024, F32, 512)
    dma("sync", "k0", fb2hi_row[0:1, :],
        AP(fb2_i.tensor, 512, [[D, 1], [1, 512]]))
    k0 = ch("k0")
    E_("vector", "memset", zrow_s[:, :], 0.0)
    E_("vector", "memset", onecol_s[:, :], 1.0)
    E_("vector", "memset", ones8_s[:, :], 1.0)
    E_("vector", "memset", gba[:, :], 0.0, inc=dv)
    dv_gba0 = dv.val

    W_("tensor", k0)
    E_("tensor", "matmul", pp[0][:, :], onez32_s[:, :], stg(1)[0:1, 0:512],
       start=True, stop=True)
    E_("tensor", "matmul", pp[1][:, :], onez32_s[:, :], xa(0)[0:1, 0:512],
       start=True, stop=True)
    E_("tensor", "matmul", pp[2][:, :], onez32_s[:, :], stg(1)[0:1, 512:1024],
       start=True, stop=True)
    E_("tensor", "matmul", pp[3][:, :], onez32_s[:, :], xa(0)[0:1, 512:1024],
       start=True, stop=True)
    E_("tensor", "matmul", pp[4][:, :], onez32_s[:, :], xa(1)[0:1, 0:512],
       start=True, stop=True)
    E_("tensor", "matmul", pp[5][:, :], onez32_s[:, :], xa(1)[0:1, 512:1024],
       start=True, stop=True)
    E_("tensor", "matmul", pp[6][:, :], onez32_s[:, :], stg(0)[0:1, 512:1024],
       start=True, stop=True)
    E_("tensor", "matmul", pp[7][:, :], onez32_s[:, :], fb2hi_row[0:1, :],
       start=True, stop=True)
    E_("tensor", "matmul", pp[0][:, :], onez32_s[:, :], stg(1)[0:1, 0:512],
       start=False, stop=True) if False else None
    E_("tensor", "matmul", pp[7][:, 256:288], onez32_s[:, :],
       small_s[0:1, 0:32], start=True, stop=True)
    E_("tensor", "matmul", pp[7][:, 288:296], onez32_s[:, :],
       small_s[0:1, 32:40], start=True, stop=True, inc=pe)
    W_("scalar", pe, pe.val)
    E_("scalar", "copy", bv_bc[:, 0:512], pp[0][:, :])
    E_("scalar", "copy", bv_bc[:, 512:1024], pp[1][:, :])
    E_("scalar", "copy", bo_bc[:, 0:512], pp[2][:, :])
    E_("scalar", "copy", bo_bc[:, 512:1024], pp[3][:, :])
    E_("scalar", "copy", b2_bc[:, 0:512], pp[4][:, :])
    E_("scalar", "copy", b2_bc[:, 512:1024], pp[5][:, :])
    E_("scalar", "copy", fb2_bc[:, 0:512], pp[6][:, :])
    E_("scalar", "copy", fb2_bc[:, 512:1024], pp[7][:, 0:512])
    E_("scalar", "copy", bpm_bc[:, :], pp[7][:, 256:288])
    E_("scalar", "copy", coreoh_bc[:, :], pp[7][:, 288:296], inc=ac)
    ac_bc = ac.val

    W_("tensor", ac, ac_bc)
    E_("tensor", "matmul", pp[0][:, :], onez32_s[:, :], fb2hi_row[0:1, :],
       start=True, stop=True, inc=pe)
    W_("scalar", pe, pe.val)
    E_("scalar", "copy", fb2_bc[:, 512:1024], pp[0][:, :], inc=ac)
    ac_bc = ac.val

    W_("tensor", ac, ac_bc)
    for (srcap, rows, dstap, bscale) in (
            (stgb(1).bitcast(F32)[0:8, 0:128], 8, bq_s[:, 0:8], 0.125),
            (stgb(1).bitcast(F32)[0:8, 128:256], 8, bk_s[:, 0:8], 1.0),
            (A(OF_XTC, F32, 256)[0:32, 0:128], 32, b1_s[:, 0:32], 1.0),
            (A(OF_XTC, F32, 256)[0:32, 128:256], 32, fb1_s[:, 0:32], 1.0)):
        E_("tensor", "transpose", pp[3][:, 0:rows], srcap,
           idn32_s[0:rows, 0:rows], inc=pe)
        W_("scalar", pe, pe.val)
        E_("scalar", "activation", dstap, pp[3][:, 0:rows], AF.Identity,
           scale=bscale, inc=ac)
        W_("tensor", ac, ac.val)

    W_("vector", k0)
    E_("vector", "reduce_sum", small_s[0:1, 56:57], small_s[0:1, 40:44],
       axis=AX)
    E_("vector", "reduce_sum", small_s[0:1, 57:58], small_s[0:1, 44:48],
       axis=AX, inc=dv)
    W_("tensor", dv, dv.val)
    E_("tensor", "matmul", pp[2][:, 0:2], onez32_s[:, :],
       small_s[0:1, 56:58], start=True, stop=True, inc=pe)
    W_("scalar", pe, pe.val)
    W_("scalar", dv, dv_gba0)
    E_("scalar", "copy", gba[0:16, 0:1], pp[2][0:16, 0:1])
    E_("scalar", "copy", gba[32:48, 0:1], pp[2][32:48, 1:2], inc=ac)

    E_("vector", "tensor_add", stg(0)[0:2, 256:320], stg(0)[0:2, 0:64],
       stg(0)[0:2, 64:128])
    E_("vector", "tensor_add", stg(0)[0:2, 320:384], stg(0)[0:2, 128:192],
       stg(0)[0:2, 192:256])
    E_("vector", "tensor_add", stg(0)[0:2, 384:448], stg(0)[0:2, 256:320],
       stg(0)[0:2, 320:384])
    E_("vector", "tensor_copy", stgb(0)[0:2, 0:64], stg(0)[0:2, 384:448],
       inc=dv)
    W_("tensor", dv, dv.val)
    W_("tensor", ac, ac.val)
    E_("tensor", "transpose", ppb(3)[0:64, 0:2], stgb(0)[0:2, 0:64],
       idn_s[0:2, 0:2], inc=pe)
    W_("scalar", pe, pe.val)
    E_("scalar", "copy", stgb(0)[0:64, 64:66], ppb(3)[0:64, 0:2], inc=ac)
    ac_g2 = ac.val
    E_("vector", "memset", gwa[:, :], 0.0)
    W_("vector", ac, ac_g2)
    for h in range(H):
        kt_i, r0 = h // 2, (h % 2) * 64
        E_("vector", "tensor_copy",
           gwa[r0:r0 + 64, kt_i * 48 + h:kt_i * 48 + h + 1],
           stgb(0)[0:64, 64:65])
        E_("vector", "tensor_copy",
           gwa[r0:r0 + 64, kt_i * 48 + 32 + h:kt_i * 48 + 32 + h + 1],
           stgb(0)[0:64, 65:66], inc=(dv if h == H - 1 else None))

    conv_done = [dv.val, dv.val]
    tpose_done = [ac.val, ac.val]
    bank_free = [pe.val] * 8
    for _bi in range(8):
        bank_free[_bi] = ac.val
    xa_guard_pe = pe.val
    wst_ch = [0, 0]
    for wi, (wn, scale) in enumerate(
            (("wq", 0.125), ("wk", 1.0), ("wv", 1.0), ("wo", 1.0))):
        for rc in range(8):
            slot = (wi * 8 + rc) % 2
            W_("sync", dv, conv_done[slot])
            if wi == 0 and rc < 2:
                W_("sync", pe, xa_guard_pe)
            ldv = dma("sync", "wx%d" % slot, xa(slot)[:, :],
                      w_in[wn][rc * 128:(rc + 1) * 128, :])
            W_("vector", ch("wx%d" % slot), ldv)
            W_("vector", ac, tpose_done[slot])
            E_("vector", "tensor_scalar", stgb(slot)[:, :], xa(slot)[:, :],
               scale, None, ALU.mult, inc=dv)
            conv_done[slot] = dv.val
            W_("tensor", dv, conv_done[slot])
            for ccc in range(8):
                bk2 = 4 + (ccc % 2)
                W_("tensor", ac, bank_free[bk2])
                E_("tensor", "transpose", ppb(bk2)[:, 0:128],
                   stgb(slot)[:, ccc * 128:(ccc + 1) * 128], idn_s[:, :],
                   inc=pe)
                dstc = A(OF_WSTR + (ccc % 2) * 8 * 128 * 2, BF16, 128)
                W_("scalar", pe, pe.val)
                W_("scalar", ch("ws%d" % (ccc % 2)), wst_ch[ccc % 2])
                E_("scalar", "copy", dstc[:, :], ppb(bk2)[:, 0:128], inc=ac)
                bank_free[bk2] = ac.val
                if ccc == 7:
                    tpose_done[slot] = ac.val
                W_("sync", ac, ac.val)
                wst_ch[ccc % 2] = dma(
                    "sync", "ws%d" % (ccc % 2),
                    wT_h[wn][ccc * 128:(ccc + 1) * 128,
                             rc * 128:(rc + 1) * 128], dstc[:, :])
    W_("sync", dv, dv.val)
    W_("sync", pe, pe.val)
    wrld = dma("sync", "k0", stg(0)[0:8, 0:D], wr_i[:, :])
    W_("tensor", ch("k0"), wrld)
    for ccc in range(8):
        bk2 = 4 + (ccc % 2)
        W_("tensor", ac, bank_free[bk2])
        E_("tensor", "transpose", pp[bk2][:, 0:8],
           stg(0)[0:8, ccc * 128:(ccc + 1) * 128], idn32_s[0:8, 0:8], inc=pe)
        W_("scalar", pe, pe.val)
        E_("scalar", "copy", wrT_s[:, ccc * 8:(ccc + 1) * 8], pp[bk2][:, 0:8],
           inc=ac)
        bank_free[bk2] = ac.val

    g1hv = A(OF_G1H, BF16, NREL)
    for half in range(2):
        for kc3 in range(3):
            W_("sync", pe, pe.val)
            g1ld = dma("sync", "g1", g1hv[:, :],
                       g1h_i[kc3 * 128:(kc3 + 1) * 128, :])
            rows = 128 if kc3 < 2 else 64
            W_("sync", dv, dv.val)
            rld = dma("sync", "rl", stg(1)[0:rows, 0:16],
                      rel_e[kc3 * 128:kc3 * 128 + rows, :])
            W_("vector", ch("rl"), rld)
            W_("vector", pe, pe.val)
            if kc3 == 2:
                E_("vector", "memset", rel_s[:, :], 0.0)
            E_("vector", "tensor_copy", rel_s[0:rows, 0:16],
               stg(1)[0:rows, 0:16], inc=dv)
            W_("tensor", dv, dv.val)
            W_("tensor", ch("g1"), g1ld)
            if kc3 == 0:
                for _b2 in range(8):
                    W_("tensor", ac, bank_free[_b2])
            for nch in range(8):
                ncol = half * 8 + nch
                n0 = ncol * 512
                nn = min(512, NREL - n0)
                if nn <= 0:
                    continue
                E_("tensor", "matmul", pp[nch][0:16, 0:nn], rel_s[:, 0:16],
                   g1hv[:, n0:n0 + nn], start=(kc3 == 0), stop=(kc3 == 2),
                   inc=(pe if nch == 7 else None))
        W_("scalar", pe, pe.val)
        for nch in range(8):
            ncol = half * 8 + nch
            n0 = ncol * 512
            nn = min(512, NREL - n0)
            if nn <= 0:
                continue
            W_("scalar", ch("br%d" % (nch % 2)), CH.get("br%d" % (nch % 2))
               and CH["br%d" % (nch % 2)].val or 0)
            E_("scalar", "copy", stgb(nch % 2)[0:16, 0:nn],
               pp[nch][0:16, 0:nn], inc=ac)
            bank_free[nch] = ac.val
            W_("sync", ac, ac.val)
            dma("sync", "br%d" % (nch % 2),
                AP(biasrel_h.tensor, n0, [[BRP, 16], [1, nn]]),
                stgb(nch % 2)[0:16, 0:nn])
    if "biasrel" in dbg:
        W_("gpsimd", ch("br0"))
        W_("gpsimd", ch("br1"))
        dma("gpsimd", "dbg", dbg["biasrel"][:, :], biasrel_h[:, 0:NREL])

    b.tag = "P1"
    vaugv = A(OF_VAUG, BF16, NT * H * 65)
    gcolv = A(OF_GCOL, F32, NT * H)
    gcolrv = A(OF_GCOLR, F32, NT * H)
    wpmv = A(OF_WPM, F32, NT * H * 2)
    xtcv = A(OF_XTC, BF16, 8 * 512)

    def wstr(slot):
        return A(OF_WSTR + slot * 8 * 128 * 2, BF16, 8 * 128)

    OF_PSTG = OF_G1H + 16 * 512 * 2 + 8 * 512 * 2 + 2 * 16 * 1024

    def pstg(slot):
        return A(OF_PSTG + slot * 512 * 2, BF16, 512)

    vstr = A(OF_G1H + 4 * 512 * 2, BF16, 8 * 512)

    E_("vector", "memset", vaugv, 1.0)
    W_("vector", ch("br0"))
    W_("vector", ch("br1"))
    W_("sync", ch("ws0"))
    W_("sync", ch("ws1"))
    W_("sync", ch("g1"))
    OF_XTC2 = P01_END
    OF_WQK = OF_XTC2 + 8 * 512 * 2
    OF_WKK = OF_WQK + 8 * 1024 * 2
    assert OF_WKK + 8 * 1024 * 2 <= ARENA_BYTES
    xtc2v = A(OF_XTC2, BF16, 8 * 512)
    wqk = A(OF_WQK, BF16, 8 * 1024)
    wkk = A(OF_WKK, BF16, 8 * 1024)
    vstr2 = A(OF_G1H, BF16, 16 * 512)
    W_("sync", pe, pe.val)
    wq_ld = dma("sync", "pw0", wqk[:, :],
                AP(wT_h["wq"].tensor, 0, [[D, 128], [128 * D, 8], [1, 1024]]))
    wk_ld = dma("sync", "pw1", wkk[:, :],
                AP(wT_h["wk"].tensor, 0, [[D, 128], [128 * D, 8], [1, 1024]]))
    vld = 0
    for oc in range(2):
        vld = dma("sync", "pv", vstr2[:, oc * 4096:(oc + 1) * 4096],
                  AP(wT_h["wv"].tensor, oc * 512,
                     [[D, 128], [128 * D, 8], [1, 512]]))
    x_conv = [(dv, dv.val), (dv, dv.val)]
    stgb_free = [(ac, ac.val), (ac, ac.val)]
    pstg_free = [None, None, None, None]
    stg_free = [(dv, dv.val), (dv, dv.val)]
    xt_ready = [0, 0]
    xtc_free = [(pe, pe.val), (pe, pe.val)]
    WT = b.WT

    def xtc(p):
        return xtcv if p == 0 else xtc2v

    def emit_xt(tc):
        p = tc % 2
        first_copy = True
        for sub in range(4):
            slot = sub % 2
            WT("sync", x_conv[slot])
            ldv = dma("sync", "px%d" % slot, xa(slot)[:, :],
                      x_in[(tc * 4 + sub) * 128:(tc * 4 + sub + 1) * 128, :])
            W_("vector", ch("px%d" % slot), ldv)
            WT("vector", stgb_free[slot])
            E_("vector", "tensor_copy", stgb(slot)[:, :], xa(slot)[:, :],
               inc=dv)
            x_conv[slot] = (dv, dv.val)
            W_("tensor", dv, dv.val)
            for dt_ in range(8):
                bk2 = 4 + (dt_ % 2)
                W_("tensor", ac, bank_free[bk2])
                E_("tensor", "transpose", ppb(bk2)[:, 0:128],
                   stgb(slot)[:, dt_ * 128:(dt_ + 1) * 128], idn_s[:, :],
                   inc=pe)
                W_("scalar", pe, pe.val)
                if first_copy:
                    WT("scalar", xtc_free[p])
                    first_copy = False
                E_("scalar", "copy",
                   xtc(p)[:, dt_ * 512 + sub * 128:dt_ * 512 + sub * 128 + 128],
                   ppb(bk2)[:, 0:128], drain=False, inc=ac)
                bank_free[bk2] = ac.val
            stgb_free[slot] = (ac, ac.val)
        xt_ready[p] = ac.val

    emit_xt(0)
    for tc in range(8):
        p = tc % 2
        for pi, (wres, bias_col, dsth) in enumerate(
                ((wqk, bq_s, qt_h), (wkk, bk_s, kt_h))):
            for ot in range(8):
                bk0 = ot % 2
                if tc == 0 and pi == 0 and ot == 0:
                    W_("tensor", ch("pw0"), wq_ld)
                    W_("tensor", ch("pw1"), wk_ld)
                    W_("tensor", ch("pv"), vld)
                W_("tensor", ac, max(xt_ready[p], bank_free[bk0]))
                for kc in range(8):
                    E_("tensor", "matmul", pp[bk0][:, :],
                       wres[:, kc * 1024 + ot * 128:kc * 1024 + (ot + 1) * 128],
                       xtc(p)[:, kc * 512:(kc + 1) * 512],
                       start=(kc == 0), stop=(kc == 7),
                       inc=(pe if kc == 7 else None))
                pslot = pi * 2 + (ot % 2)
                W_("scalar", pe, pe.val)
                WT("scalar", pstg_free[pslot])
                E_("scalar", "activation", pstg(pslot)[:, :], pp[bk0][:, :],
                   AF.Identity, bias=bias_col[:, ot:ot + 1], drain=False,
                   inc=ac)
                bank_free[bk0] = ac.val
                W_("sync", ac, ac.val)
                _v = dma(
                    "sync", "ps%d" % pslot,
                    dsth[ot * 128:(ot + 1) * 128, tc * 512:(tc + 1) * 512],
                    pstg(pslot)[:, :])
                pstg_free[pslot] = (ch("ps%d" % pslot), _v)
        if tc + 1 < 8:
            emit_xt(tc + 1)
        for oc in range(2):
            for sub in range(4):
                bkv = 2 + (sub % 2)
                W_("tensor", ac, bank_free[bkv])
                for kc in range(8):
                    E_("tensor", "matmul", pp[bkv][:, :],
                       xtc(p)[:, kc * 512 + sub * 128:
                              kc * 512 + sub * 128 + 128],
                       vstr2[:, oc * 4096 + kc * 512:
                             oc * 4096 + (kc + 1) * 512],
                       start=(kc == 0), stop=(kc == 7),
                       inc=(pe if kc == 7 else None))
                sslot = sub % 2
                W_("scalar", pe, pe.val)
                WT("scalar", stg_free[sslot])
                E_("scalar", "copy", stg(sslot)[:, 0:512], pp[bkv][:, :],
                   drain=False, inc=ac)
                bank_free[bkv] = ac.val
                W_("vector", ac, ac.val)
                it = tc * 4 + sub
                for hh in range(8):
                    h = oc * 8 + hh
                    E_("vector", "tensor_tensor",
                       vaugv[:, (it * H + h) * 65:(it * H + h) * 65 + 64],
                       stg(sslot)[:, hh * 64:hh * 64 + 64],
                       bv_bc[:, oc * 512 + hh * 64:oc * 512 + hh * 64 + 64],
                       ALU.add, drain=False, inc=(dv if hh == 7 else None))
                stg_free[sslot] = (dv, dv.val)
        W_("tensor", ac, max(xt_ready[p], bank_free[6]))
        for kc in range(8):
            E_("tensor", "matmul", pp[6][0:48, :],
               gwa[:, kc * 48:(kc + 1) * 48],
               xtc(p)[:, kc * 512:(kc + 1) * 512],
               start=(kc == 0), stop=(kc == 7), inc=(pe if kc == 7 else None))
        W_("scalar", pe, pe.val)
        WT("scalar", stg_free[0])
        E_("scalar", "activation", stg(0)[0:48, 0:512], pp[6][0:48, :],
           AF.Sigmoid, bias=gba[:, 0:1], inc=ac)
        bank_free[6] = ac.val
        W_("vector", ac, ac.val)
        WT("vector", stg_free[1])
        E_("vector", "tensor_scalar", stg(1)[0:16, 512:1024],
           stg(0)[32:48, 0:512], gruc_s[32:48, 0:1], -1.0, ALU.mult, ALU.add)
        E_("vector", "tensor_mul", stg(1)[64:80, 0:512],
           stg(1)[0:16, 512:1024], stg(0)[0:16, 0:512])
        E_("vector", "tensor_scalar", stg(1)[0:16, 0:512],
           stg(1)[64:80, 0:512], 2.0, None, ALU.add)
        E_("vector", "tensor_copy", stg(1)[32:48, 0:512],
           stg(1)[0:16, 511::-1], inc=dv)
        g_ready = dv.val
        stg_free[0] = (dv, dv.val)
        W_("tensor", dv, g_ready)
        for sub in range(4):
            bk2 = 4 + (sub % 2)
            W_("tensor", ac, bank_free[bk2])
            E_("tensor", "transpose", pp[bk2][:, 0:16],
               stg(1)[0:16, sub * 128:(sub + 1) * 128],
               idn32_s[0:16, 0:16], inc=pe)
            W_("scalar", pe, pe.val)
            it = tc * 4 + sub
            E_("scalar", "copy", gcolv[:, it * H:(it + 1) * H],
               pp[bk2][:, 0:16], drain=False, inc=ac)
            bank_free[bk2] = ac.val
            E_("tensor", "transpose", pp[bk2][:, 16:32],
               stg(1)[32:48, sub * 128:(sub + 1) * 128],
               idn32_s[32:48, 32:48], inc=pe)
            W_("scalar", pe, pe.val)
            itr = tc * 4 + (3 - sub)
            E_("scalar", "copy", gcolrv[:, itr * H:(itr + 1) * H],
               pp[bk2][:, 16:32], drain=False, inc=ac)
            bank_free[bk2] = ac.val
        gcol_ready = ac.val
        W_("vector", ac, gcol_ready)
        for sub in range(4):
            it = tc * 4 + sub
            E_("vector", "tensor_tensor", stg(1)[:, sub * 32:sub * 32 + 16],
               gcolv[:, it * H:(it + 1) * H], bpm_bc[:, 0:16], ALU.mult,
               drain=False)
            E_("vector", "tensor_tensor",
               stg(1)[:, sub * 32 + 16:sub * 32 + 32],
               gcolv[:, it * H:(it + 1) * H], bpm_bc[:, 16:32], ALU.mult,
               inc=(dv if sub == 3 else None))
        W_("scalar", dv, dv.val)
        for sub in range(4):
            it = tc * 4 + sub
            E_("scalar", "activation", wpmv[:, it * 32:(it + 1) * 32],
               stg(1)[:, sub * 32:(sub + 1) * 32], AF.Exp,
               drain=False, inc=(ac if sub == 3 else None))
        stg_free[1] = (ac, ac.val)
        xtc_free[p] = (pe, pe.val)

    p1_ac = ac.val
    p1_dv = dv.val
    if "qt" in dbg:
        W_("gpsimd", ch("ps0"))
        W_("gpsimd", ch("ps1"))
        dma("gpsimd", "dbg", dbg["qt"][:, :], qt_h[:, :])
    if "kt" in dbg:
        W_("gpsimd", ch("ps2"))
        W_("gpsimd", ch("ps3"))
        dma("gpsimd", "dbg", dbg["kt"][:, :], kt_h[:, :])
    if "vaug" in dbg:
        W_("gpsimd", dv, p1_dv)
        dma("gpsimd", "dbg", dbg["vaug"][:, :], vaugv)
    if "gcol" in dbg:
        W_("gpsimd", ac, p1_ac)
        dma("gpsimd", "dbg", dbg["gcol"][:, :], gcolv)
    if "gcolr" in dbg:
        W_("gpsimd", ac, p1_ac)
        dma("gpsimd", "dbg", dbg["gcolr"][:, :], gcolrv)
    if "wpm" in dbg:
        W_("gpsimd", ac, p1_ac)
        dma("gpsimd", "dbg", dbg["wpm"][:, :], wpmv)

    b.tag = "ATT"
    OF_ATTB = OF_ATTS + 2 * 512 * 2
    assert OF_ATTB + 256 * 2 <= ARENA_BYTES

    def ktb(slot):
        return A(OF_KTB + slot * 4096 * 2, BF16, 4096)

    def qpadv(v):
        return A(OF_QPAD + v * 512 * 2, BF16, 512)

    def a1t(hp):
        return A(OF_A1T + hp * 21 * 128 * 2, BF16, 21 * 128)

    def diagv(s):
        return A(OF_DIAG + s * 4 * 128 * 2, BF16, 4 * 128)

    def expsv(e4):
        return A(OF_EXPS + e4 * 512 * 2, BF16, 512)

    rcmb = A(OF_RCMB, F32, 1024)
    attbf = A(OF_ATTB, BF16, 256)
    OF_RC2 = OF_ATTB + 256 * 2
    OF_RCMB2 = OF_RC2 + 1024 * 4
    assert OF_RCMB2 + 1024 * 4 <= ARENA_BYTES
    rc2 = A(OF_RC2, F32, 1024)
    rcmbs = [rcmb, A(OF_RCMB2, F32, 1024)]

    def attsv(s):
        return A(OF_ATTS + s * 512 * 2, BF16, 512)

    for cn in ("ps0", "ps1", "ps2", "ps3"):
        W_("sync", ch(cn))
    W_("sync", pe, pe.val)
    W_("sync", dv, dv.val)
    W_("sync", ac, ac.val)
    W_("vector", pe, pe.val)
    W_("vector", ac, ac.val)
    for v4 in range(4):
        E_("vector", "memset", qpadv(v4)[:, :], 0.0)
    E_("vector", "memset", rcmb[:, :], 0.0, inc=dv)
    qpad_zero = dv.val
    W_("sync", dv, qpad_zero)

    regcol = {}
    for ic in range(8):
        present = [0]
        if any(_tile_region(jt, ic) == 1 for jt in range(32)):
            present.append(1)
        if any(_tile_region(jt, ic) == 2 for jt in range(32)):
            present.append(2)
        regcol[ic] = {r: i * 65 for i, r in enumerate(present)}

    kt_ld = [None, None]
    kt_free = [None, None]
    a1_ld = [None, None]
    a1_free = [None, None]
    q_ld = [None] * 4
    q_free = [None] * 4
    diag_rdy = [None, None]
    diag_free = [None, None]
    scores_free = [None, None, None]
    SBK = (0, 1, 6)
    exps_free = [None] * 4
    atts_free = [None, None]
    WT = b.WT

    def load_kt(h):
        slot = h % 2
        WT("sync", kt_free[slot])
        v = dma("sync", "kt%d" % slot, ktb(slot)[:, :],
                kt_h[(h // 2) * 128:(h // 2) * 128 + 128, :])
        kt_ld[slot] = (CH["kt%d" % slot], v)

    def load_a1(h):
        slot = h % 2
        WT("sync", a1_free[slot])
        for di in range(21):
            off = h * BRP + 128 * (di - 10) + 3968
            v = dma("sync", "a1%d" % slot,
                    a1t(slot)[:, di * 128:(di + 1) * 128],
                    AP(biasrel_h.tensor, off, [[1, 128], [1, 128]]))
        a1_ld[slot] = (CH["a1%d" % slot], v)

    def load_q(h, ic):
        v4 = (h % 2) * 2 + (ic % 2)
        WT("sync", q_free[v4])
        r0 = (h % 2) * 64
        v = dma("sync", "q%d" % v4,
                qpadv(v4)[r0:r0 + 64, :] if r0 else qpadv(v4)[0:64, :],
                qt_h[h * 64:(h + 1) * 64, ic * 512:(ic + 1) * 512])
        q_ld[v4] = (CH["q%d" % v4], v)

    load_kt(0)
    load_a1(0)
    load_q(0, 0)
    pending = [None]
    attbf_free = [None]
    rcmb_free = [None, None]
    for h in range(H):
        hp = h % 2
        for ic in range(8):
            it0 = ic * 4
            ds = (h * 8 + ic) % 2
            v4 = hp * 2 + (ic % 2)
            if ic + 1 < 8:
                load_q(h, ic + 1)
            elif h + 1 < H:
                load_q(h + 1, 0)
            if ic == 0 and h + 1 < H:
                load_kt(h + 1)
                load_a1(h + 1)
            WT("vector", diag_free[ds])
            for sub in range(4):
                it = it0 + sub
                E_("vector", "tensor_scalar",
                   diagv(ds)[:, sub * 128:(sub + 1) * 128], anti_s[:, :],
                   gcolrv[:, it * H + h:it * H + h + 1], None, ALU.mult,
                   drain=False, inc=(dv if sub == 3 else None))
            diag_rdy[ds] = (dv, dv.val)
            WT("tensor", kt_ld[hp])
            WT("tensor", a1_ld[hp])
            WT("tensor", q_ld[v4])
            WT("tensor", diag_rdy[ds])
            av_first = {0: (0, 0), 1: (0, 2)}
            av_last = {0: (31, 1), 1: (31, 3)}
            qk_done = [None] * 32

            def emit_qk(jt):
                sb = jt % 3
                WT("tensor", scores_free[sb])
                nearj = _near_pair(jt, it0) or _near_pair(jt, it0 + 1) or \
                    _near_pair(jt, it0 + 2) or _near_pair(jt, it0 + 3)
                nbias = [sub for sub in range(4)
                         if abs(jt - (it0 + sub)) <= 7] if nearj else []
                E_("tensor", "matmul", pp[SBK[sb]][:, :],
                   ktb(hp)[:, jt * 128:(jt + 1) * 128], qpadv(v4)[:, :],
                   start=True, stop=(not nbias),
                   inc=(pe if not nbias else None))
                for bi, sub in enumerate(nbias):
                    dd = jt - (it0 + sub)
                    E_("tensor", "matmul",
                       pp[SBK[sb]][:, sub * 128:(sub + 1) * 128],
                       a1t(hp)[:, (dd + 10) * 128:(dd + 11) * 128],
                       diagv(ds)[:, sub * 128:(sub + 1) * 128],
                       start=False, stop=(bi == len(nbias) - 1),
                       inc=(pe if bi == len(nbias) - 1 else None))
                qk_done[jt] = pe.val

            emit_qk(0)
            emit_qk(1)
            for jt in range(32):
                sb = jt % 3
                if jt + 2 < 32:
                    emit_qk(jt + 2)
                e4 = jt % 4
                W_("scalar", pe, qk_done[jt])
                WT("scalar", exps_free[e4])
                E_("scalar", "activation", expsv(e4)[:, :], pp[SBK[sb]][:, :],
                   AF.Exp, drain=False, inc=ac)
                exp_done = ac.val
                scores_free[sb] = (ac, exp_done)
                W_("tensor", ac, exp_done)
                for sub in range(4):
                    r = 0 if abs(jt - (it0 + sub)) <= 7 else \
                        (1 if jt > it0 + sub else 2)
                    bank = pp[2 + sub // 2]
                    coff = (sub % 2) * 256 + regcol[ic][r]
                    E_("tensor", "matmul", bank[:, coff:coff + 65],
                       expsv(e4)[:, sub * 128:(sub + 1) * 128],
                       vaugv[:, ((jt * H) + h) * 65:((jt * H) + h) * 65 + 65],
                       start=(av_first[sub // 2] == (jt, sub)),
                       stop=(av_last[sub // 2] == (jt, sub)),
                       inc=(pe if (jt == 31 and sub == 3) else None))
                exps_free[e4] = (pe, pe.val)
                if jt == 5 and pending[0] is not None:
                    pending[0]()
                    pending[0] = None
            kt_free[hp] = (pe, pe.val)
            a1_free[hp] = (pe, pe.val)
            q_free[v4] = (pe, pe.val)
            diag_free[ds] = (pe, pe.val)
            av_done = pe.val
            ncols = 65 * len(regcol[ic])
            ip = (h * 8 + ic) % 2
            rcmb_c = rcmbs[ip]
            W_("scalar", pe, av_done)
            WT("scalar", rcmb_free[ip])
            for bk_i in range(2):
                E_("scalar", "copy",
                   rcmb_c[:, bk_i * 512:bk_i * 512 + 256 + ncols],
                   pp[2 + bk_i][:, 0:256 + ncols],
                   drain=False, inc=(ac if bk_i == 1 else None))
            rc_done = ac.val

            def make_epilogue(h=h, ic=ic, it0=it0, rc_done=rc_done,
                              rcmb_c=rcmb_c, ip=ip):
                def emit():
                    W_("vector", ac, rc_done)
                    WT("vector", attbf_free[0])
                    has1 = 1 in regcol[ic]
                    has2 = 2 in regcol[ic]

                    def _cf(sub):
                        if has2:
                            return rc2, sub * 130 + 65
                        if has1:
                            return rc2, sub * 130
                        return rcmb_c, sub * 256
                    if has1:
                        for sub in range(4):
                            it = it0 + sub
                            base = sub * 256
                            E_("vector", "scalar_tensor_tensor",
                               rc2[:, sub * 130:sub * 130 + 65],
                               rcmb_c[:, base + regcol[ic][1]:
                                      base + regcol[ic][1] + 65],
                               wpmv[:, it * 32 + 16 + h:it * 32 + 16 + h + 1],
                               rcmb_c[:, base:base + 65], ALU.mult, ALU.add,
                               drain=(sub == 3))
                    if has2:
                        for sub in range(4):
                            it = it0 + sub
                            base = sub * 256
                            src1 = rc2[:, sub * 130:sub * 130 + 65] \
                                if has1 else rcmb_c[:, base:base + 65]
                            E_("vector", "scalar_tensor_tensor",
                               rc2[:, sub * 130 + 65:sub * 130 + 130],
                               rcmb_c[:, base + regcol[ic][2]:
                                      base + regcol[ic][2] + 65],
                               wpmv[:, it * 32 + h:it * 32 + h + 1],
                               src1, ALU.mult, ALU.add, drain=(sub == 3))
                    for sub in range(4):
                        bufc, c0 = _cf(sub)
                        E_("vector", "reciprocal",
                           rc2[:, 520 + sub:521 + sub],
                           bufc[:, c0 + 64:c0 + 65], drain=(sub == 3))
                    for sub in range(4):
                        bufc, c0 = _cf(sub)
                        E_("vector", "tensor_scalar",
                           attbf[:, sub * 64:sub * 64 + 64],
                           bufc[:, c0:c0 + 64], rc2[:, 520 + sub:521 + sub],
                           None, ALU.mult, drain=False,
                           inc=(dv if sub == 3 else None))
                    cmb_done = dv.val
                    rcmb_free[ip] = (dv, cmb_done)
                    W_("tensor", dv, cmb_done)
                    asl = ic % 2
                    for sub in range(4):
                        bk2 = 4 + (sub % 2)
                        W_("tensor", ac, bank_free[bk2])
                        E_("tensor", "transpose", ppb(bk2)[0:64, 0:128],
                           attbf[:, sub * 64:(sub + 1) * 64], idn_s[:, :],
                           inc=pe)
                        W_("scalar", pe, pe.val)
                        if sub < 2:
                            WT("scalar", atts_free[asl])
                        E_("scalar", "copy", attsv(asl)[0:64, sub * 128:
                                                        (sub + 1) * 128],
                           ppb(bk2)[0:64, 0:128], drain=False, inc=ac)
                        bank_free[bk2] = ac.val
                    attbf_free[0] = (pe, pe.val)
                    W_("sync", ac, ac.val)
                    v = dma("sync", "at%d" % asl,
                            attnT_h[h * 64:(h + 1) * 64,
                                    ic * 512:(ic + 1) * 512],
                            attsv(asl)[0:64, :])
                    atts_free[asl] = (CH["at%d" % asl], v)
                return emit
            pending[0] = make_epilogue()
    if pending[0] is not None:
        pending[0]()
        pending[0] = None
    bank_free[6] = ac.val
    if "attnT" in dbg:
        W_("gpsimd", ch("at0"))
        W_("gpsimd", ch("at1"))
        dma("gpsimd", "dbg", dbg["attnT"][:, :], attnT_h[:, :])

    b.tag = "LN1"

    def attld(slot):
        return A(OF_ATTLD + slot * 8 * 128 * 2, BF16, 8 * 128)

    x1t32 = A(OF_X1T32, F32, D)
    OF_X1T32B = 137 * 1024

    def x1t32p(p):
        return x1t32 if p == 0 else A(OF_X1T32B, F32, D)
    wmst = A(OF_WMST, BF16, 512)
    rtr = A(OF_RTR, F32, 256)
    W_("sync", ch("at0"))
    W_("sync", ch("at1"))
    W_("sync", pe, pe.val)
    W_("sync", dv, dv.val)
    W_("sync", ac, ac.val)
    W_("vector", pe, pe.val)
    W_("scalar", dv, dv.val)
    attld_free = [None, None]
    wo_free = [None, None]
    lnst_free = [None, None]
    x1st_free = [None, None]
    x1b_free = [None, None]
    x1t_free = [None, None]
    wm_free = [None]
    xin_free = [None, None]
    lgbank_free = [(ac, ac.val)]
    x1t32_free = [[], []]
    pending_rt = [None]
    for it in range(NT):
        sl = it % 2
        WT("sync", attld_free[sl])
        av_ = dma("sync", "al%d" % sl, attld(sl)[:, :],
                  AP(attnT_h.tensor, it * 128, [[N, 128], [128 * N, 8],
                                                [1, 128]]))
        WT("sync", wo_free[sl])
        wv_ = dma("sync", "wl%d" % sl, wstr(sl)[:, :],
                  AP(wT_h["wo"].tensor, it * 0, [[D, 128], [128 * D, 8],
                                                 [1, 128]])) if False else None
        if it == 0:
            wo_ld = dma("sync", "wo", vstr[:, :],
                        AP(wT_h["wo"].tensor, 0,
                           [[D, 128], [128 * D, 8], [1, 512]]))
            wo_ld2 = dma("sync", "wo2", A(OF_G1H + 12 * 1024, BF16, 8 * 512),
                         AP(wT_h["wo"].tensor, 512,
                            [[D, 128], [128 * D, 8], [1, 512]]))
        WT("sync", xin_free[sl])
        xv_ = dma("sync", "xi%d" % sl, xa(sl)[:, :],
                  x_in[it * 128:(it + 1) * 128, :])
        W_("tensor", ch("al%d" % sl), av_)
        if it == 0:
            W_("tensor", ch("wo"), wo_ld)
            W_("tensor", ch("wo2"), wo_ld2)
        W_("tensor", ac, bank_free[6])
        W_("tensor", ac, bank_free[7])
        for ot in range(8):
            E_("tensor", "matmul", pp[6][:, :],
               attld(sl)[:, ot * 128:(ot + 1) * 128],
               vstr[:, ot * 512:(ot + 1) * 512],
               start=(ot == 0), stop=(ot == 7))
            E_("tensor", "matmul", pp[7][:, :],
               attld(sl)[:, ot * 128:(ot + 1) * 128],
               A(OF_G1H + 12 * 1024, BF16, 8 * 512)[:, ot * 512:
                                                    (ot + 1) * 512],
               start=(ot == 0), stop=(ot == 7),
               inc=(pe if ot == 7 else None))
        attld_free[sl] = (pe, pe.val)
        oproj_done = pe.val
        W_("scalar", pe, oproj_done)
        WT("scalar", lnst_free[sl])
        E_("scalar", "copy", stg(sl)[:, 0:512], pp[6][:, :], drain=False)
        E_("scalar", "copy", stg(sl)[:, 512:1024], pp[7][:, :], drain=False,
           inc=ac)
        bank_free[6] = ac.val
        bank_free[7] = ac.val
        ev_done = ac.val
        W_("vector", ac, ev_done)
        W_("vector", ch("xi%d" % sl), xv_)
        E_("vector", "tensor_add", stgb(sl).bitcast(F32)[:, 0:1024],
           stg(sl)[:, :], bo_bc[:, :]) if False else None
        E_("vector", "tensor_add", stg(sl)[:, :], stg(sl)[:, :],
           bo_bc[:, :]) if False else None
        s2 = A(OF_EXPS, F32, 1024)
        WT("vector", x1st_free[0])
        WT("vector", x1st_free[1])
        E_("vector", "tensor_add", s2[:, 0:1024], stg(sl)[:, :], bo_bc[:, :])
        E_("vector", "tensor_add", stg(sl)[:, :], s2[:, 0:1024], xa(sl)[:, :],
           inc=dv)
        xin_free[sl] = (dv, dv.val)
        E_("vector", "reduce_sum", rcmb[:, 0:1], stg(sl)[:, :], axis=AX,
           inc=dv)
        sum_done = dv.val
        W_("scalar", dv, sum_done)
        E_("scalar", "activation", s2[:, 0:1024], stg(sl)[:, :], AF.Square,
           accum_out=rcmb[:, 1:2], drain=False, inc=ac)
        sq_done = ac.val
        W_("vector", ac, sq_done)
        E_("vector", "tensor_scalar", rcmb[:, 2:3], rcmb[:, 0:1],
           1.0 / D, None, ALU.mult)
        E_("vector", "tensor_scalar", rcmb[:, 3:4], rcmb[:, 1:2],
           1.0 / D, None, ALU.mult)
        E_("vector", "tensor_mul", rcmb[:, 4:5], rcmb[:, 2:3], rcmb[:, 2:3])
        E_("vector", "tensor_sub", rcmb[:, 5:6], rcmb[:, 3:4], rcmb[:, 4:5])
        E_("vector", "tensor_scalar", rcmb[:, 6:7], rcmb[:, 5:6],
           EPS, None, ALU.add, inc=dv)
        var_done = dv.val
        W_("scalar", dv, var_done)
        E_("scalar", "sqrt", rcmb[:, 7:8], rcmb[:, 6:7], drain=False, inc=ac)
        W_("vector", ac, ac.val)
        E_("vector", "reciprocal", rcmb[:, 8:9], rcmb[:, 7:8], inc=dv)
        W_("vector", dv, dv.val)
        E_("vector", "tensor_scalar", s2[:, 0:1024], stg(sl)[:, :],
           rcmb[:, 2:3], rcmb[:, 8:9], ALU.subtract, ALU.mult, inc=dv)
        x1_done = dv.val
        lnst_free[sl] = (dv, x1_done)
        W_("sync", dv, x1_done)
        WT("sync", x1st_free[sl])
        v1 = dma("sync", "x1f%d" % sl, x1_h[it * 128:(it + 1) * 128, :],
                 s2[:, 0:1024])
        x1st_free[sl] = (CH["x1f%d" % sl], v1)
        WT("vector", x1t_free[sl])
        E_("vector", "tensor_copy", stgb(sl)[:, :], s2[:, 0:1024], inc=dv)
        x1b_done = dv.val
        W_("sync", dv, x1b_done)
        WT("sync", x1b_free[sl])
        v2 = dma("sync", "x1b%d" % sl, x1bf_h[it * 128:(it + 1) * 128, :],
                 stgb(sl)[:, :])
        x1b_free[sl] = (CH["x1b%d" % sl], v2)
        W_("tensor", dv, x1_done)
        p = it % 2
        x1t32c = x1t32p(p)
        for tok in x1t32_free[p]:
            WT("scalar", tok)
        for dt_ in range(8):
            bk2 = 4 + (dt_ % 2)
            W_("tensor", ac, bank_free[bk2])
            E_("tensor", "transpose", pp[bk2][:, 0:128],
               s2[:, dt_ * 128:(dt_ + 1) * 128], idn32_s[:, :], inc=pe)
            W_("scalar", pe, pe.val)
            E_("scalar", "copy", x1t32c[:, dt_ * 128:(dt_ + 1) * 128],
               pp[bk2][:, 0:128], drain=False, inc=ac)
            bank_free[bk2] = ac.val
        x1t_done = ac.val
        W_("vector", ac, x1t_done)
        WT("vector", x1b_free[sl])
        E_("vector", "tensor_copy", stgb(sl)[:, :], x1t32c[:, :], inc=dv)
        x1tb_done = dv.val
        W_("sync", dv, x1tb_done)
        WT("sync", x1t_free[sl])
        v3 = dma("sync", "x1t%d" % sl,
                 AP(x1t_h.tensor, it * 128, [[N, 128], [128 * N, 8],
                                             [1, 128]]),
                 stgb(sl)[:, :])
        x1t_free[sl] = (CH["x1t%d" % sl], v3)

        def make_rt(it=it, p=p, x1t32c=x1t32c, x1t_done=x1t_done,
                    x1tb_done=x1tb_done):
            def emit():
                W_("tensor", ac, max(x1t_done, lgbank_free[0][1]))
                for kc in range(8):
                    E_("tensor", "matmul", pp[2][:, 0:8],
                       x1t32c[:, kc * 128:(kc + 1) * 128], wrT_s[:, kc * 8:kc * 8 + 8],
                       start=(kc == 0), stop=(kc == 7),
                       inc=(pe if kc == 7 else None))
                lg_done = pe.val
                W_("scalar", pe, lg_done)
                E_("scalar", "copy", rtr[:, 0:8], pp[2][:, 0:8], drain=False, inc=ac)
                lgbank_free[0] = (ac, ac.val)
                lgc_done = ac.val
                W_("vector", ac, lgc_done)
                E_("vector", "reduce_max", rtr[:, 8:9], rtr[:, 0:8], axis=AX)
                E_("vector", "tensor_scalar", rtr[:, 16:24], rtr[:, 0:8],
                   rtr[:, 8:9], None, ALU.is_equal)
                E_("vector", "tensor_tensor_scan", rtr[:, 24:32], ones8_s[:, :],
                   rtr[:, 16:24], 0.0, ALU.mult, ALU.add)
                E_("vector", "tensor_scalar", rtr[:, 32:40], rtr[:, 24:32],
                   1.0, None, ALU.is_equal)
                E_("vector", "tensor_mul", rtr[:, 40:48], rtr[:, 16:24],
                   rtr[:, 32:40])
                E_("vector", "tensor_scalar", rtr[:, 48:56], rtr[:, 40:48],
                   1e30, None, ALU.mult)
                E_("vector", "tensor_sub", rtr[:, 56:64], rtr[:, 0:8], rtr[:, 48:56])
                E_("vector", "reduce_max", rtr[:, 9:10], rtr[:, 56:64], axis=AX)
                E_("vector", "tensor_scalar", rtr[:, 64:72], rtr[:, 56:64],
                   rtr[:, 9:10], None, ALU.is_equal)
                E_("vector", "tensor_tensor_scan", rtr[:, 72:80], ones8_s[:, :],
                   rtr[:, 64:72], 0.0, ALU.mult, ALU.add)
                E_("vector", "tensor_scalar", rtr[:, 80:88], rtr[:, 72:80],
                   1.0, None, ALU.is_equal)
                E_("vector", "tensor_mul", rtr[:, 88:96], rtr[:, 64:72],
                   rtr[:, 80:88])
                E_("vector", "tensor_scalar", rtr[:, 10:11], rtr[:, 8:9],
                   -1.0, None, ALU.mult, inc=dv)
                t2a = dv.val
                W_("scalar", dv, t2a)
                E_("scalar", "activation", rtr[:, 96:104], rtr[:, 0:8], AF.Exp,
                   bias=rtr[:, 10:11], drain=False, inc=ac)
                e8_done = ac.val
                W_("vector", ac, e8_done)
                E_("vector", "tensor_mul", rtr[:, 104:112], rtr[:, 96:104],
                   rtr[:, 40:48])
                E_("vector", "reduce_sum", rtr[:, 11:12], rtr[:, 104:112], axis=AX)
                E_("vector", "tensor_mul", rtr[:, 112:120], rtr[:, 96:104],
                   rtr[:, 88:96])
                E_("vector", "reduce_sum", rtr[:, 12:13], rtr[:, 112:120], axis=AX)
                E_("vector", "tensor_add", rtr[:, 13:14], rtr[:, 11:12],
                   rtr[:, 12:13])
                E_("vector", "reciprocal", rtr[:, 14:15], rtr[:, 13:14])
                E_("vector", "tensor_mul", rtr[:, 15:16], rtr[:, 11:12],
                   rtr[:, 14:15])
                E_("vector", "tensor_mul", rtr[:, 120:121], rtr[:, 12:13],
                   rtr[:, 14:15])
                E_("vector", "tensor_scalar", rtr[:, 128:136], rtr[:, 40:48],
                   rtr[:, 15:16], None, ALU.mult)
                E_("vector", "scalar_tensor_tensor", rtr[:, 136:144],
                   rtr[:, 88:96], rtr[:, 120:121], rtr[:, 128:136],
                   ALU.mult, ALU.add)
                E_("vector", "tensor_copy", A(OF_ATTB, BF16, 256)[:, 0:8],
                   rtr[:, 136:144], inc=dv)
                wm8_done = dv.val
                W_("tensor", dv, wm8_done)
                bk2 = 4
                W_("tensor", ac, bank_free[bk2])
                E_("tensor", "transpose", ppb(bk2)[0:8, 0:128],
                   A(OF_ATTB, BF16, 256)[:, 0:8], idn_s[:, :], inc=pe)
                W_("scalar", pe, pe.val)
                if it % 4 == 0 and wm_free[0] is not None:
                    WT("scalar", wm_free[0])
                E_("scalar", "copy", wmst[0:8, (it % 4) * 128:(it % 4) * 128 + 128],
                   ppb(bk2)[0:8, 0:128], drain=False, inc=ac)
                bank_free[bk2] = ac.val
                if it % 4 == 3:
                    W_("sync", ac, ac.val)
                    if wm_free[0] is not None:
                        WT("sync", wm_free[0])
                    vw = dma("sync", "wm", wmT_h[:, (it - 3) * 128:(it + 1) * 128],
                             wmst[0:8, :])
                    wm_free[0] = (CH["wm"], vw)
                x1t32_free[p] = [(pe, lg_done),
                                 (dv, x1tb_done)]
            return emit
        if pending_rt[0] is not None:
            pending_rt[0]()
        pending_rt[0] = make_rt()
    if pending_rt[0] is not None:
        pending_rt[0]()
        pending_rt[0] = None
    if "x1" in dbg:
        W_("gpsimd", ch("x1f0"))
        W_("gpsimd", ch("x1f1"))
        dma("gpsimd", "dbg", dbg["x1"][:, :], x1_h[:, :])
    if "wm" in dbg:
        W_("gpsimd", ch("wm"))
        dma("gpsimd", "dbg", dbg["wm"][:, :], wmT_h[:, :])

    b.tag = "SCAN"
    W_("gpsimd", ch("wm"))
    E_("gpsimd", "collective_compute", "AllGather", ALU.bypass,
       [list(range(NCORE))], [wmT_h[:, :].opt()], [wall_h[:, :].opt()],
       inc=cc)
    cc_ag1 = cc.val
    W_("gpsimd", ch("x1b0"))
    W_("gpsimd", ch("x1b1"))
    E_("gpsimd", "collective_compute", "AllGather", ALU.bypass,
       [list(range(NCORE))], [x1bf_h[:, :].opt()], [x1g_h[:, :].opt()],
       inc=cc)
    cc_ag2 = cc.val
    msbv = A(OF_MSB, BF16, 8 * 256)
    posv = A(OF_POSS, F32, 8 * 256)
    keptv = A(OF_KEPT, F32, 8 * 256)
    scr = A(OF_SCR, F32, 8 * 256)
    idxa = A(OF_IDXA, F32, 16 * 256)
    eligb = A(OF_ELIGB, BF16, 256)
    bbf = A(OF_BBF, BF16, 256)
    onesr = A(OF_ONESR, F32, 256)
    csrow = A(OF_CSROW, F32, 512)

    W_("sync", pe, pe.val)
    W_("vector", pe, pe.val)
    W_("vector", ac, ac.val)
    W_("sync", cc, cc_ag1)
    W_("vector", cc, cc_ag1)
    xbv = 0
    for e in range(E):
        for r in range(NCORE):
            xbv = dma("sync", "xb", msbv[:, e * 256 + r * 32:
                                         e * 256 + (r + 1) * 32],
                      AP(wall_h.tensor, (8 * r + e) * N, [[128, 32], [1, 128]]),
                      transpose=True)
    E_("vector", "memset", onesr[0:1, :], 1.0)
    E_("vector", "memset", scr[:, 0:256], 0.0, inc=dv)
    W_("vector", ch("xb"), xbv)
    fb_cur = 0
    for e in range(E):
        mat = idxa[:, (8 + e) * 256:(9 + e) * 256]
        E_("vector", "tensor_copy", mat, msbv[:, e * 256:(e + 1) * 256])
        E_("vector", "tensor_scalar", scr[:, 768:1024], mat, 0.0, None,
           ALU.is_gt)
        E_("vector", "tensor_scalar", scr[:, 1280:1536],
           scr[:, fb_cur:fb_cur + 256], -1.0, 1.0, ALU.mult, ALU.add)
        E_("vector", "tensor_mul", scr[:, 1024:1280], scr[:, 768:1024],
           scr[:, 1280:1536])
        E_("vector", "tensor_copy", eligb[:, :], scr[:, 1024:1280], inc=dv)
        el_done = dv.val
        W_("tensor", dv, el_done)
        W_("tensor", ac, max(bank_free[0], bank_free[1]))
        E_("tensor", "matmul", pp[0][:, 0:256], ltx_s[:, :], eligb[:, :],
           start=True, stop=False)
        E_("tensor", "matmul", pp[1][0:1, 0:256], onecol_s[:, 0:1],
           eligb[:, :], start=True, stop=True, inc=pe)
        cs_done = pe.val
        W_("scalar", pe, cs_done)
        E_("scalar", "copy", csrow[0:1, 0:256], pp[1][0:1, 0:256], inc=ac)
        W_("vector", ac, ac.val)
        E_("vector", "tensor_tensor_scan", csrow[0:1, 256:512],
           onesr[0:1, :], csrow[0:1, 0:256], 0.0, ALU.mult, ALU.add)
        E_("vector", "memset", scr[0:1, 1536:1537], 0.0)
        E_("vector", "tensor_copy", scr[0:1, 1537:1792],
           csrow[0:1, 256:511])
        E_("vector", "tensor_copy", eligb[0:1, :], scr[0:1, 1536:1792])
        E_("vector", "tensor_copy", csrow[0:1, 0:256], eligb[0:1, :])
        E_("vector", "tensor_sub", scr[0:1, 1792:2048],
           scr[0:1, 1536:1792], csrow[0:1, 0:256])
        E_("vector", "tensor_copy", bbf[0:1, :], scr[0:1, 1792:2048],
           inc=dv)
        ex_done = dv.val
        W_("tensor", dv, ex_done)
        E_("tensor", "matmul", pp[0][:, 0:256], onez_s[:, :], eligb[0:1, :],
           start=False, stop=False)
        E_("tensor", "matmul", pp[0][:, 0:256], onez_s[:, :], bbf[0:1, :],
           start=False, stop=True, inc=pe)
        pos_done = pe.val
        W_("scalar", pe, pos_done)
        E_("scalar", "copy", posv[:, e * 256:(e + 1) * 256],
           pp[0][:, 0:256], inc=ac)
        bank_free[0] = ac.val
        bank_free[1] = ac.val
        posc_done = ac.val
        W_("vector", ac, posc_done)
        E_("vector", "tensor_scalar", scr[:, 1792:2048],
           posv[:, e * 256:(e + 1) * 256], float(CAP), None, ALU.is_lt)
        E_("vector", "tensor_mul", keptv[:, e * 256:(e + 1) * 256],
           scr[:, 1792:2048], scr[:, 1024:1280])
        E_("vector", "tensor_sub", scr[:, 1280:1536], scr[:, 1024:1280],
           keptv[:, e * 256:(e + 1) * 256])
        nfb = 256 if fb_cur == 0 else 0
        E_("vector", "tensor_add", scr[:, nfb:nfb + 256],
           scr[:, fb_cur:fb_cur + 256], scr[:, 1280:1536], inc=dv)
        fb_cur = nfb
    for col in range(7):
        E_("vector", "memset", idxa[:, col * 256:(col + 1) * 256], 0.0)
    E_("vector", "memset", idxa[:, 7 * 256:8 * 256], 0.0, inc=dv)
    for e in range(E):
        mat = idxa[:, (8 + e) * 256:(9 + e) * 256]
        E_("vector", "tensor_scalar", scr[:, 768:1024], mat, 0.0, None,
           ALU.is_gt)
        E_("vector", "tensor_scalar", scr[:, 1024:1280],
           idxa[:, 0:256], 0.0, None, ALU.is_equal)
        E_("vector", "tensor_mul", scr[:, 1280:1536], scr[:, 768:1024],
           scr[:, 1024:1280])
        E_("vector", "tensor_scalar", scr[:, 1024:1280],
           idxa[:, 0:256], 1.0, None, ALU.is_equal)
        E_("vector", "tensor_mul", scr[:, 1536:1792], scr[:, 768:1024],
           scr[:, 1024:1280])
        for (facc, base) in ((scr[:, 1280:1536], 1), (scr[:, 1536:1792], 4)):
            E_("vector", "scalar_tensor_tensor",
               idxa[:, base * 256:(base + 1) * 256], facc, float(e),
               idxa[:, base * 256:(base + 1) * 256], ALU.mult, ALU.add) \
                if False else None
            E_("vector", "scalar_tensor_tensor", scr[:, 1024:1280],
               facc, float(e), idxa[:, base * 256:(base + 1) * 256],
               ALU.mult, ALU.add)
            E_("vector", "tensor_copy", idxa[:, base * 256:(base + 1) * 256],
               scr[:, 1024:1280])
            E_("vector", "tensor_mul", scr[:, 1024:1280], facc,
               posv[:, e * 256:(e + 1) * 256])
            E_("vector", "tensor_add", scr[:, 1792:2048], scr[:, 1024:1280],
               idxa[:, (base + 1) * 256:(base + 2) * 256])
            E_("vector", "tensor_copy",
               idxa[:, (base + 1) * 256:(base + 2) * 256],
               scr[:, 1792:2048])
            E_("vector", "tensor_mul", scr[:, 1024:1280], facc, mat)
            E_("vector", "tensor_add", scr[:, 1792:2048], scr[:, 1024:1280],
               idxa[:, (base + 2) * 256:(base + 3) * 256])
            E_("vector", "tensor_copy",
               idxa[:, (base + 2) * 256:(base + 3) * 256],
               scr[:, 1792:2048])
        E_("vector", "tensor_add", scr[:, 1024:1280], idxa[:, 0:256],
           scr[:, 768:1024])
        E_("vector", "tensor_copy", idxa[:, 0:256], scr[:, 1024:1280],
           inc=dv)
    E_("vector", "memset", scr[:, 1792:2048], 0.0)
    for e in range(E):
        E_("vector", "tensor_scalar", scr[:, 1024:1280],
           posv[:, e * 256:(e + 1) * 256], -float(CAP), None, ALU.add)
        E_("vector", "tensor_mul", scr[:, 1280:1536],
           keptv[:, e * 256:(e + 1) * 256], scr[:, 1024:1280])
        E_("vector", "tensor_scalar", scr[:, 1024:1280], scr[:, 1280:1536],
           float(CAP), None, ALU.add)
        E_("vector", "tensor_scalar", scr[:, 1280:1536], scr[:, 1024:1280],
           coreoh_bc[:, e:e + 1], None, ALU.mult)
        E_("vector", "tensor_add", scr[:, 1536:1792], scr[:, 1792:2048],
           scr[:, 1280:1536])
        E_("vector", "tensor_copy", scr[:, 1792:2048], scr[:, 1536:1792])
    E_("vector", "tensor_copy", slotc_s[:, :], scr[:, 1792:2048])
    E_("vector", "tensor_copy", slotc32_s[:, :], slotc_s[:, :],
       drain=True, inc=dv)
    slotc_done = dv.val
    sl32_done = dv.val
    if "slotc" in dbg:
        W_("gpsimd", dv, slotc_done)
        dma("gpsimd", "dbg", dbg["slotc"][:, :], slotc_s[:, :])
    if "pos" in dbg:
        W_("gpsimd", ac, ac.val)
        dma("gpsimd", "dbg", dbg["pos"][:, :], posv)
    if "kept" in dbg:
        W_("gpsimd", dv, dv.val)
        dma("gpsimd", "dbg", dbg["kept"][:, :], keptv)
    iscr = A(OF_SCR + 1536 * 4, I32, 256)
    ihi = A(OF_SCR + 256 * 4, I32, 256)
    ilo = A(OF_SCR + 512 * 4, I32, 256)
    E_("vector", "tensor_scalar", scr[:, 1024:1280],
       scr[:, fb_cur:fb_cur + 256], -1.0, 1.0, ALU.mult, ALU.add)
    E_("vector", "tensor_scalar", scr[:, 1280:1536],
       scr[:, fb_cur:fb_cur + 256], float(FBROW), None, ALU.mult)
    for r_i, (ecol, pcol) in enumerate(((1, 2), (4, 5))):
        E_("vector", "tensor_copy", iscr[:, :],
           idxa[:, pcol * 256:(pcol + 1) * 256])
        E_("vector", "tensor_scalar", ihi[:, :], iscr[:, :], 10, None,
           ALU.arith_shift_right)
        E_("vector", "tensor_scalar", ilo[:, :], iscr[:, :], 1023, None,
           ALU.bitwise_and)
        E_("vector", "tensor_copy", scr[:, 768:1024], ihi[:, :])
        E_("vector", "tensor_copy", scr[:, 1792:2048], ilo[:, :])
        E_("vector", "scalar_tensor_tensor", scr[:, 1536:1792],
           idxa[:, ecol * 256:(ecol + 1) * 256], 1024.0,
           scr[:, 1792:2048], ALU.mult, ALU.add)
        E_("vector", "scalar_tensor_tensor", scr[:, 512:768],
           scr[:, 768:1024], 8192.0, scr[:, 1536:1792],
           ALU.mult, ALU.add)
        E_("vector", "tensor_mul", idxa[:, 7 * 256:8 * 256],
           scr[:, 512:768], scr[:, 1024:1280])
        E_("vector", "tensor_add", idxa[:, r_i * 256:(r_i + 1) * 256],
           idxa[:, 7 * 256:8 * 256], scr[:, 1280:1536], inc=dv)
    rs_arrays = [idxa[:, 0:256], idxa[:, 256:512], idxa[:, 3 * 256:4 * 256],
                 idxa[:, 6 * 256:7 * 256], scr[:, fb_cur:fb_cur + 256]]
    rs_done = dv.val
    W_("tensor", dv, rs_done)
    rsst = 0
    for arr_i, arr in enumerate(rs_arrays):
        for hhf in range(2):
            bk2 = 4 + (hhf % 2)
            W_("tensor", ac, bank_free[bk2])
            E_("tensor", "transpose", pp[bk2][:, 0:128],
               arr[:, hhf * 128:(hhf + 1) * 128], idn32_s[:, :], inc=pe)
            W_("scalar", pe, pe.val)
            W_("scalar", ch("rs"), rsst)
            E_("scalar", "activation", csrow[0:128, 0:128],
               pp[bk2][:, 0:128], AF.Identity, scale=0.125, inc=ac)
            bank_free[bk2] = ac.val
            W_("sync", ac, ac.val)
            rsst = dma("sync", "rs",
                       AP(rsi_h.tensor,
                          ((4 * hhf) * 160 + arr_i * 32) * 128,
                          [[160 * 128, 4], [128, 32], [1, 128]]),
                       csrow[0:128, 0:128])
    W_("gpsimd", ch("rs"))
    E_("gpsimd", "collective_compute", "ReduceScatter", ALU.add,
       [list(range(NCORE))], [rsi_h[:, :].opt()], [rso_h[:, :].opt()],
       inc=cc)
    cc_rs = cc.val
    if "rso" in dbg:
        W_("gpsimd", cc, cc_rs)
        dma("gpsimd", "dbg", dbg["rso"][:, :], rso_h[:, :])

    b.tag = "FB"

    fwa = A(OF_WA, BF16, 8 * 4096)
    fwb = A(OF_WB, BF16, 32 * 1024)
    hts = A(OF_HTS, BF16, 32 * 256)

    def xet(slot):
        return A(OF_XET + slot * 8 * 256 * 2, BF16, 8 * 256)

    def fsb(slot):
        return A(OF_FSB + slot * 1024 * 2, BF16, 1024)

    W_("sync", pe, pe.val)
    W_("sync", dv, dv.val)
    W_("sync", ac, ac.val)
    W_("gpsimd", pe, pe.val)
    W_("gpsimd", dv, dv.val)
    W_("gpsimd", ac, ac.val)
    for cn in ("x1t0", "x1t1", "x1f0", "x1f1", "x1b0", "x1b1", "wm"):
        W_("sync", ch(cn))
        W_("gpsimd", ch(cn))

    def ffn_block(wname1, wname2, w1ap, w2ap, b1col, b2row, xt_src, nchunks,
                  ydst, ych, aux=None, agchunk=None):
        w1ch = ch(wname1)
        w2ch = ch(wname2)
        for cn in ("yf0", "yf1", "ye0", "ye1", "yz",
                   "xg0", "xg1", "xg2", "xg3", "xg4", "xg5"):
            if cn in CH:
                W_("scalar", ch(cn))
                W_("vector", ch(cn))
                W_("sync", ch(cn))
        W_("gpsimd", pe, pe.val)
        W_("gpsimd", ac, ac.val)
        W_("tensor", ch("b2r"))
        for iik in range(32):
            E_("gpsimd", "dma_start",
               out=fwb[:, iik * 1024:(iik + 1) * 1024],
               in_=w2ap[iik * 128:(iik + 1) * 128, :], inc=w2ch)
        w2_ld = w2ch.val
        W_("gpsimd", ch("rs"))
        for kc in range(8):
            for hf in range(2):
                E_("gpsimd", "dma_start",
                   out=fwa[:, kc * 4096 + hf * 2048:kc * 4096 + (hf + 1) * 2048],
                   in_=w1ap[kc * 128:(kc + 1) * 128,
                            hf * 2048:(hf + 1) * 2048], inc=w1ch)
        w1_ld = w1ch.val
        xet_free = [None, None]
        ybank_free = {4: (ac, ac.val), 5: (ac, ac.val),
                      6: (ac, ac.val), 7: (ac, ac.val)}
        yst_free = [None, None]
        w2_done_prev = (pe, pe.val)
        for tc in range(nchunks):
            xsl = tc % 2
            WT("sync", xet_free[xsl])
            _, xtoks = xt_src(tc, xsl)
            if tc == 0:
                W_("tensor", w1ch, w1_ld)
            for xtok in xtoks:
                W_("tensor", xtok[0], xtok[1])
            for iic in range(32):
                if aux is not None:
                    aux()
                bk0 = iic % 4
                W_("tensor", ac, bank_free[bk0])
                for kc in range(8):
                    E_("tensor", "matmul", pp[bk0][:, 0:256],
                       fwa[:, kc * 4096 + iic * 128:kc * 4096 + (iic + 1) * 128],
                       xet(xsl)[:, kc * 256:(kc + 1) * 256],
                       start=(kc == 0), stop=(kc == 7),
                       inc=(pe if kc == 7 else None))
                if iic == 31:
                    xet_free[xsl] = (pe, pe.val)
                W_("scalar", pe, pe.val)
                if iic == 0:
                    WT("scalar", w2_done_prev)
                E_("scalar", "activation", hts[:, iic * 256:(iic + 1) * 256],
                   pp[bk0][:, 0:256], AF.Gelu, bias=b1col[:, iic:iic + 1],
                   drain=False, inc=ac)
                bank_free[bk0] = ac.val
            hts_done = ac.val
            if tc == 0:
                W_("tensor", w2ch, w2_ld)
            for tt4 in range(2):
                tt = tc * 2 + tt4
                ysl = tt % 2
                for oc in range(2):
                    bky = 4 + 2 * tt4 + oc
                    WT("tensor", ybank_free[bky])
                    if tt4 == 0 and oc == 0:
                        W_("tensor", ac, hts_done)
                    for iik in range(32):
                        E_("tensor", "matmul", pp[bky][:, :],
                           hts[:, iik * 256 + tt4 * 128:
                               iik * 256 + tt4 * 128 + 128],
                           fwb[:, iik * 1024 + oc * 512:
                               iik * 1024 + (oc + 1) * 512],
                           start=(iik == 0), stop=False)
                    E_("tensor", "matmul", pp[bky][:, :], onez_s[:, :],
                       b2row[0:1, oc * 512:(oc + 1) * 512],
                       start=False, stop=True, inc=pe)
                    if tt4 == 1 and oc == 1:
                        w2_done_prev = (pe, pe.val)
                    W_("scalar", pe, pe.val)
                    if oc == 0:
                        WT("scalar", yst_free[ysl])
                    E_("scalar", "copy", fsb(ysl)[:, oc * 512:(oc + 1) * 512],
                       pp[bky][:, :], drain=False, inc=ac)
                    ybank_free[bky] = (ac, ac.val)
                W_("sync", ac, ac.val)
                v = dma("sync", ych + str(ysl),
                        ydst[tt * 128:(tt + 1) * 128, :], fsb(ysl)[:, :])
                yst_free[ysl] = (CH[ych + str(ysl)], v)
                if agchunk is not None:
                    agchunk(tt)

    def fb_xt(tc, xsl):
        toks = []
        for q in range(2):
            chn = "xt%d_%d" % (xsl, q)
            v = dma("sync", chn, xet(xsl)[:, q * 1024:(q + 1) * 1024],
                    AP(x1t_h.tensor, tc * 256 + q * 4 * 128 * N,
                       [[N, 128], [128 * N, 4], [1, 256]]))
            toks.append((ch(chn), v))
        return None, toks

    def scst(gsl):
        return A(OF_SCST + gsl * 1024 * 2, BF16, 1024)

    sc_state = {"n": 0, "c": 0}
    xg_free = [None] * 6

    def scatter_aux():
        sc_state["n"] += 1
        if sc_state["n"] <= 128:
            return
        c = sc_state["c"]
        if c >= 256:
            return
        gsl = c % 6
        if c == 0:
            W_("sync", cc, cc_ag2)
            W_("gpsimd", dv, sl32_done)
        WT("sync", xg_free[gsl])
        v = dma("sync", "xg%d" % gsl, scst(gsl)[:, :],
                x1g_h[c * 128:(c + 1) * 128, :])
        W_("gpsimd", ch("xg%d" % gsl), v)
        E_("gpsimd", "indirect_dma_start",
           out=xe_h[:, :],
           out_offset=IndirectOffsetOnAxis(slotc32_s[:, c:c + 1], 0),
           in_=scst(gsl)[:, :], in_offset=None, inc=ch("sc%d" % gsl))
        xg_free[gsl] = (ch("sc%d" % gsl), ch("sc%d" % gsl).val)
        sc_state["c"] = c + 1

    ffn_block("fw1", "fw2", fw1_i, fw2_i, fb1_s, fb2row_s, fb_xt, 16,
              yfb_h, "yf", aux=scatter_aux)
    assert sc_state["c"] == 256, sc_state
    if "yfb" in dbg:
        W_("gpsimd", ch("yf0"))
        W_("gpsimd", ch("yf1"))
        dma("gpsimd", "dbg", dbg["yfb"][:, :], yfb_h[:, :])

    b.tag = "EXP"
    W_("vector", dv, dv.val)
    for cn in ("yf0", "yf1", "hs0", "hs1"):
        if cn in CH:
            W_("vector", ch(cn))
    E_("vector", "memset", fsb(0)[:, :], 0.0, inc=dv)
    zb_done = dv.val
    W_("sync", dv, zb_done)
    for blk in range(8):
        dma("sync", "yz", yall_h[FBROW + blk * 128:FBROW + (blk + 1) * 128, :],
            fsb(0)[:, :])
    for cn in ("sc0", "sc1", "sc2", "sc3", "sc4", "sc5"):
        W_("sync", ch(cn))
    if "xe" in dbg:
        for cn in ("sc0", "sc1", "sc2", "sc3", "sc4", "sc5"):
            W_("gpsimd", ch(cn))
        dma("gpsimd", "dbg", dbg["xe"][:, :], xe_h[:, :])

    def ex_xt(tc, xsl):
        vq = {}
        for dt_ in range(8):
            chn = "xt%d_%d" % (xsl, dt_ % 2)
            vq[chn] = dma("sync", chn,
                          xet(xsl)[:, dt_ * 256:(dt_ + 1) * 256],
                          AP(xe_h.tensor, tc * 256 * D + dt_ * 128,
                             [[D, 256], [1, 128]]),
                          transpose=True)
        return None, [(ch(c), v) for c, v in vq.items()]

    agst = {"k": 0}

    def ag3_chunk(tt):
        if (tt + 1) % 8:
            return
        k = tt // 8
        W_("gpsimd", CH["ye0"])
        W_("gpsimd", CH["ye1"])
        E_("gpsimd", "collective_compute", "AllGather", ALU.bypass,
           [list(range(NCORE))],
           [AP(ye_h.tensor, k * 1024 * D, [[1, 1024 * D]])],
           [AP(yall_h.tensor, k * 8192 * D, [[1, 8192 * D]])],
           inc=cc)
        agst["k"] = k + 1

    ffn_block("w1e", "w2e", w1e_i, w2e_i, b1_s, b2row_s, ex_xt, 40,
              ye_h, "ye", agchunk=ag3_chunk)
    assert agst["k"] == 10, agst
    cc_ag3 = cc.val
    if "ye" in dbg:
        for cn in ("ye0", "ye1"):
            W_("gpsimd", ch(cn))
        dma("gpsimd", "dbg", dbg["ye"][:, :], ye_h[:, :])
        W_("gpsimd", ch("dbg"))

    b.tag = "CMB"
    OF_CHT = 32 * 1024
    locv = A(OF_LOC, F32, 8 * 32)
    li32 = A(OF_LI32, I32, 64)

    def g0b(slot):
        return A(OF_G0 + slot * D * 2, BF16, D)

    def g1b(slot):
        return A(OF_G1R + slot * D * 2, BF16, D)

    def yfbb(slot):
        return A(OF_YFB + slot * D * 2, BF16, D)

    def cx1(slot):
        return A(OF_CX1 + slot * D * 4, F32, D)

    def cst(slot):
        return A(OF_CST + slot * D * 4, F32, D)

    W_("sync", cc, cc_rs)
    W_("sync", pe, pe.val)
    W_("sync", ac, ac.val)
    W_("sync", dv, dv.val)
    W_("vector", pe, pe.val)
    W_("vector", ac, ac.val)
    W_("scalar", pe, pe.val)
    W_("scalar", dv, dv.val)
    for cn in ("ye0", "ye1", "xt0_0", "xt0_1", "xt1_0", "xt1_1"):
        if cn in CH:
            W_("vector", ch(cn))
            W_("sync", ch(cn))
    W_("gpsimd", cc, cc_ag3)
    W_("gpsimd", ch("yz"))
    rlv = 0
    for arr_i in range(5):
        rlv = dma("sync", "rl2", csrow[0:32, 0:128] if False else
                  A(OF_SCR, F32, 128, rows=32)[0:32, :],
                  rso_h[arr_i * 32:(arr_i + 1) * 32, :])
        W_("tensor", ch("rl2"), rlv)
        W_("tensor", ac, bank_free[4])
        E_("tensor", "transpose", pp[4][:, 0:32],
           A(OF_SCR, F32, 128, rows=32)[0:32, :], idn32_s[0:32, 0:32],
           inc=pe)
        W_("scalar", pe, pe.val)
        E_("scalar", "activation", locv[:, arr_i * 32:(arr_i + 1) * 32],
           pp[4][:, 0:32], AF.Identity, scale=1.0, inc=ac)
        bank_free[4] = ac.val
        W_("sync", pe, pe.val)
    loc_done = ac.val
    W_("vector", ac, loc_done)
    E_("vector", "tensor_copy", li32[:, 0:32], locv[:, 0:32])
    E_("vector", "tensor_copy", li32[:, 32:64], locv[:, 32:64], inc=dv)
    li_done = dv.val
    g_free = [None, None, None, None]
    yf_free = [None, None]
    cx_free = [None, None]
    cs_free = [None, None]
    ost_free = [None, None]
    W_("gpsimd", dv, li_done)
    for lt in range(NT):
        sl = lt % 2
        WT("gpsimd", g_free[sl])
        E_("gpsimd", "indirect_dma_start",
           out=g0b(sl)[:, :], out_offset=None, in_=yall_h[:, :],
           in_offset=IndirectOffsetOnAxis(li32[:, lt:lt + 1], 0),
           inc=ch("cg0%d" % sl))
        WT("gpsimd", g_free[2 + sl])
        E_("gpsimd", "indirect_dma_start",
           out=g1b(sl)[:, :], out_offset=None, in_=yall_h[:, :],
           in_offset=IndirectOffsetOnAxis(li32[:, 32 + lt:33 + lt], 0),
           inc=ch("cg1%d" % sl))
        WT("sync", yf_free[sl])
        yfv = dma("sync", "cy%d" % sl, yfbb(sl)[:, :],
                  yfb_h[lt * 128:(lt + 1) * 128, :])
        WT("sync", cx_free[sl])
        cxv = dma("sync", "cx%d" % sl, cx1(sl)[:, :],
                  x1_h[lt * 128:(lt + 1) * 128, :])
        W_("vector", ch("cg0%d" % sl), CH["cg0%d" % sl].val)
        W_("vector", ch("cg1%d" % sl), CH["cg1%d" % sl].val)
        W_("vector", ch("cy%d" % sl), yfv)
        W_("vector", ch("cx%d" % sl), cxv)
        WT("vector", cs_free[sl])
        s2c = cst(sl)
        E_("vector", "tensor_scalar", s2c[:, :], g0b(sl)[:, :],
           locv[:, 64 + lt:65 + lt], None, ALU.mult)
        E_("vector", "scalar_tensor_tensor",
           A(OF_CHT, F32, 1024)[:, :], g1b(sl)[:, :],
           locv[:, 96 + lt:97 + lt], s2c[:, :], ALU.mult, ALU.add)
        E_("vector", "tensor_sub", s2c[:, :], yfbb(sl)[:, :],
           A(OF_CHT, F32, 1024)[:, :], inc=dv)
        g_free[sl] = (dv, dv.val)
        g_free[2 + sl] = (dv, dv.val)
        yf_free[sl] = (dv, dv.val)
        E_("vector", "scalar_tensor_tensor",
           A(OF_CHT + 4096, F32, 1024)[:, :], s2c[:, :],
           locv[:, 128 + lt:129 + lt], A(OF_CHT, F32, 1024)[:, :],
           ALU.mult, ALU.add)
        E_("vector", "tensor_add", s2c[:, :],
           A(OF_CHT + 4096, F32, 1024)[:, :], cx1(sl)[:, :], inc=dv)
        cx_free[sl] = (dv, dv.val)
        E_("vector", "reduce_sum", rcmb[:, 0:1], s2c[:, :], axis=AX, inc=dv)
        W_("scalar", dv, dv.val)
        E_("scalar", "activation", A(OF_CHT, F32, 1024)[:, :], s2c[:, :],
           AF.Square, accum_out=rcmb[:, 1:2], inc=ac)
        W_("vector", ac, ac.val)
        E_("vector", "tensor_scalar", rcmb[:, 2:3], rcmb[:, 0:1],
           1.0 / D, None, ALU.mult)
        E_("vector", "tensor_scalar", rcmb[:, 3:4], rcmb[:, 1:2],
           1.0 / D, None, ALU.mult)
        E_("vector", "tensor_mul", rcmb[:, 4:5], rcmb[:, 2:3], rcmb[:, 2:3])
        E_("vector", "tensor_sub", rcmb[:, 5:6], rcmb[:, 3:4], rcmb[:, 4:5])
        E_("vector", "tensor_scalar", rcmb[:, 6:7], rcmb[:, 5:6],
           EPS, None, ALU.add, inc=dv)
        W_("scalar", dv, dv.val)
        E_("scalar", "sqrt", rcmb[:, 7:8], rcmb[:, 6:7], drain=False, inc=ac)
        W_("vector", ac, ac.val)
        E_("vector", "reciprocal", rcmb[:, 8:9], rcmb[:, 7:8], inc=dv)
        W_("vector", dv, dv.val)
        E_("vector", "tensor_scalar", A(OF_CHT, F32, 1024)[:, :], s2c[:, :],
           rcmb[:, 2:3], rcmb[:, 8:9], ALU.subtract, ALU.mult, inc=dv)
        out_done = dv.val
        cs_free[sl] = (dv, out_done)
        W_("sync", dv, out_done)
        WT("sync", ost_free[sl])
        ov = dma("sync", "o%d" % sl, out[lt * 128:(lt + 1) * 128, :],
                 A(OF_CHT, F32, 1024)[:, :], )
        ost_free[sl] = (CH["o%d" % sl], ov)
        WT("vector", ost_free[sl])
    W_("gpsimd", ch("o0"))
    W_("gpsimd", ch("o1"))


    for name, c in CH.items():
        if name == "dbg":
            W_("gpsimd", c)
    W_("gpsimd", od, od.val)
    with es, nc.Block() as block:
        @block.sync
        def _(h):
            for f in b.q["sync"]:
                f(h)

        @block.gpsimd
        def _(h):
            for f in b.q["gpsimd"]:
                f(h)

        @block.scalar
        def _(h):
            for f in b.q["scalar"]:
                f(h)

        @block.vector
        def _(h):
            for f in b.q["vector"]:
                f(h)

        @block.tensor
        def _(h):
            for f in b.q["tensor"]:
                f(h)

    return nc, dbg



def _host_constants():
    eye = np.eye(128)
    cons = {
        "idn": eye.astype(bfloat16),
        "idn32": eye.astype(np.float32),
        "ltx": np.triu(np.ones((128, 128)), 1).astype(bfloat16),
        "anti": eye[::-1].copy().astype(bfloat16),
        "onez": np.ones((1, 128), dtype=bfloat16),
        "onez32": np.ones((1, 128), dtype=np.float32),
        "eidx": np.broadcast_to(np.arange(8, dtype=np.float32),
                                (128, 8)).copy(),
    }
    bt = _bucket_table()
    g1 = np.zeros((384, NREL), dtype=bfloat16)
    g1[bt, np.arange(NREL)] = bfloat16(1.0)
    cons["g1h"] = g1
    return cons


def make_in_maps(inputs):
    cons = _host_constants()
    gc = np.asarray(inputs["L0_attn_gru_const"], np.float32).reshape(H)
    gru_c = np.zeros((48, 1), np.float32)
    gru_c[0:16, 0] = gc
    gru_c[32:48, 0] = gc
    in_maps = []
    for c in range(NCORE):
        m = {
            "x": np.ascontiguousarray(
                np.asarray(inputs["L0_attn_hidden_states"],
                           np.float32)[c]),
            "wq": np.asarray(inputs["L0_attn_Wq"], np.float32),
            "wk": np.asarray(inputs["L0_attn_Wk"], np.float32),
            "wv": np.asarray(inputs["L0_attn_Wv"], np.float32),
            "wo": np.asarray(inputs["L0_attn_Wo"], np.float32),
            "bq": np.asarray(inputs["L0_attn_bq"], np.float32),
            "bk": np.asarray(inputs["L0_attn_bk"], np.float32),
            "bv": np.asarray(inputs["L0_attn_bv"], np.float32),
            "bo": np.asarray(inputs["L0_attn_bo"], np.float32),
            "gru_w": np.asarray(inputs["L0_attn_gru_W"], np.float32),
            "gru_b": np.asarray(inputs["L0_attn_gru_b"],
                                np.float32).reshape(1, 8),
            "gru_c": gru_c,
            "rel_embed": np.asarray(inputs["L0_attn_rel_embed"], np.float32),
            "wr": np.asarray(inputs["L0_ffn_Wr"], np.float32),
            "w1e": np.ascontiguousarray(
                np.asarray(inputs["L0_ffn_W1"], np.float32)[c]),
            "b1e": np.ascontiguousarray(
                np.asarray(inputs["L0_ffn_b1"], np.float32)[c]),
            "w2e": np.ascontiguousarray(
                np.asarray(inputs["L0_ffn_W2"], np.float32)[c]),
            "b2e": np.ascontiguousarray(
                np.asarray(inputs["L0_ffn_b2"], np.float32)[c]),
            "fw1": np.asarray(inputs["L0_ffn_fw1"], np.float32),
            "fb1": np.asarray(inputs["L0_ffn_fb1"], np.float32),
            "fw2": np.asarray(inputs["L0_ffn_fw2"], np.float32),
            "fb2": np.asarray(inputs["L0_ffn_fb2"], np.float32),
            "coreoh": np.eye(8, dtype=np.float32)[c:c + 1],
        }
        m.update(cons)
        in_maps.append(m)
    return in_maps


def kernel(**inputs):
    nc, dbg = build_program()
    in_maps = make_in_maps(inputs)
    res = run_bass_kernel_spmd(nc, in_maps, core_ids=list(range(NCORE)))
    outs = np.stack([np.asarray(r["out"]) for r in res.results], axis=0)
    if DBG:
        kernel.dbg_results = [
            {k: np.asarray(r["dbg_" + k]) for k in DBG}
            for r in res.results]
    return outs


# version 14 (final)
# speedup vs baseline: 1.0890x; 1.0037x over previous
import math
import os
from contextlib import ExitStack

import numpy as np
import ml_dtypes

import concourse.bass as bass
import concourse.mybir as mybir
from concourse.ap import AP
from concourse.bass_core_rust import IndirectOffsetOnAxis
from concourse.bass_utils import run_bass_kernel_spmd

F32 = mybir.dt.float32
BF16 = mybir.dt.bfloat16
I16 = mybir.dt.int16
I32 = mybir.dt.int32
AF = mybir.ActivationFunctionType
ALU = mybir.AluOpType
AX = mybir.AxisListType.X
bfloat16 = ml_dtypes.bfloat16

NCORE = 8
N = 4096
D = 1024
H = 16
DH = 64
II = 4096
E = 8
T = NCORE * N
CAP = 10240
YROWS = CAP + 128
FBROW = 81920
NB = 320
NREL = 2 * N - 1
BRP = NREL + 257
EPS = 1e-5
NT = N // 128
FAR_ABS = 789
NDELTA = 10
DBG = [x for x in os.environ.get("MK_DEBUG", "").split(",") if x]
DRAIN = os.environ.get("MK_DRAIN", "1") != "0"


def _near_pair(jt, it):
    d = abs(jt - it)
    return d == 0 or (d - 1) * 128 + 1 < FAR_ABS


def _tile_region(jt, ic):
    if any(_near_pair(jt, ic * 4 + s) for s in range(4)):
        return 0
    return 1 if jt > ic * 4 else 2


def _bucket_table():
    r = np.arange(-(N - 1), N, dtype=np.int64)
    nb = NB // 2
    b = (r > 0).astype(np.int64) * nb
    ra = np.abs(r)
    mx = nb // 2
    small = ra < mx
    lg = np.log(np.maximum(ra, 1).astype(np.float64) / mx) \
        / math.log(800.0 / mx) * (nb - mx)
    lg = np.minimum((mx + lg).astype(np.int64), nb - 1)
    return (b + np.where(small, ra, lg)).astype(np.int64)


class Ctr:
    def __init__(self, sem, step):
        self.sem, self.step, self.val = sem, step, 0


EMITTED = {}


class Bld:
    ENG = ("sync", "gpsimd", "scalar", "vector", "tensor")

    def __init__(self, nc, es):
        self.nc, self.es = nc, es
        self.q = {e: [] for e in self.ENG}
        self.ctr = {}
        self.tag = ""

    def counter(self, name, step):
        sem = self.es.enter_context(self.nc.semaphore(name))
        self.ctr[name] = Ctr(sem, step)
        return self.ctr[name]

    def E(self, eng, meth, *args, inc=None, drain=None, **kw):
        pairs = []
        if inc is not None:
            for c in (inc if isinstance(inc, (list, tuple)) else [inc]):
                c.val += c.step
                pairs.append((c.sem, c.step))
        tag = f"{self.tag}|{eng}.{meth}#{len(self.q[eng])}"
        if drain is None:
            drain = (DRAIN and eng in ("vector", "scalar", "gpsimd")
                     and meth not in ("dma_start", "indirect_dma_start",
                                      "collective_compute",
                                      "dma_start_transpose"))
        else:
            drain = drain and DRAIN

        def f(h, meth=meth, args=args, kw=kw, pairs=pairs, tag=tag,
              drain=drain):
            inst = getattr(h, meth)(*args, **kw)
            try:
                EMITTED[inst.ins.name] = tag
            except Exception:
                pass
            for sem, step in pairs:
                inst = inst.then_inc(sem, step)
            if drain:
                h.drain()
        self.q[eng].append(f)

    def W(self, eng, c, val=None):
        v = c.val if val is None else val
        if v <= 0:
            return

        def f(h, sem=c.sem, v=v):
            h.wait_ge(sem, v)
        self.q[eng].append(f)

    def WT(self, eng, tok):
        if tok is not None:
            self.W(eng, tok[0], tok[1])

    def raw(self, eng, fn):
        self.q[eng].append(fn)


ARENA_BYTES = 168 * 1024
OF_VAUG = 0
OF_GCOL = OF_VAUG + NT * H * 65 * 2
OF_GCOLR = OF_GCOL + NT * H * 4
OF_WPM = OF_GCOLR + NT * H * 4
PERS_END = OF_WPM + NT * H * 2 * 4
OF_STG = PERS_END
OF_STGB = OF_STG + 2 * D * 4
OF_XA = OF_STGB + 2 * D * 2
OF_XTC = OF_XA + 2 * D * 4
OF_WSTR = OF_XTC + 8 * 512 * 2
OF_G1H = OF_WSTR + 2 * 8 * 128 * 2
P01_END = OF_G1H + 8192 * 2
OF_KTB = OF_XTC
OF_QPAD = OF_KTB + 2 * 4096 * 2
OF_A1T = OF_QPAD + 4 * 512 * 2
OF_DIAG = OF_A1T + 2 * 21 * 128 * 2
OF_EXPS = OF_DIAG + 2 * 4 * 128 * 2
OF_RCMB = OF_EXPS + 4 * 512 * 2
OF_ATTS = OF_RCMB + 1024 * 4
ATT_END = OF_ATTS + 2 * 512 * 2
assert ATT_END <= ARENA_BYTES and P01_END <= ARENA_BYTES
OF_ATTLD = OF_KTB
OF_RTR = OF_ATTLD + 2 * 8 * 128 * 2
OF_X1T32 = OF_RTR + 256 * 4
OF_WMST = OF_X1T32 + D * 4
OF_MSB = OF_VAUG
OF_POSS = OF_MSB + 8 * 256 * 2
OF_KEPT = OF_POSS + 8 * 256 * 4
OF_SCR = OF_KEPT + 8 * 256 * 4
OF_IDXA = OF_SCR + 8 * 256 * 4
OF_ELIGB = OF_IDXA + 16 * 256 * 4
OF_BBF = OF_ELIGB + 256 * 2
OF_ONESR = OF_BBF + 256 * 2
OF_CSROW = OF_ONESR + 256 * 4
SCAN_END = OF_CSROW + 512 * 4
assert SCAN_END <= OF_STG
OF_WA = 0
OF_WB = OF_WA + 8 * 4096 * 2
OF_HTS = OF_WB + 32 * 1024 * 2
OF_XET = OF_HTS + 32 * 256 * 2
OF_FSB = OF_XET + 2 * 8 * 256 * 2
OF_SCST = OF_FSB + 2 * 1024 * 2
FFN_END = OF_SCST + 6 * 1024 * 2
assert FFN_END <= ARENA_BYTES
OF_LOC = 0
OF_LI32 = OF_LOC + 8 * 32 * 4
OF_G0 = OF_LI32 + 64 * 4
OF_G1R = OF_G0 + 2 * D * 2
OF_YFB = OF_G1R + 2 * D * 2
OF_CX1 = OF_YFB + 2 * D * 2
OF_CST = OF_CX1 + 2 * D * 4
CMB_END = OF_CST + 2 * D * 4


def build_program():
    nc = bass.Bass("TRN2", target_bir_lowering=False, debug=False,
                   detect_race_conditions=False)

    def din(name, shape, dt):
        return nc.dram_tensor(name, list(shape), dt, kind="ExternalInput").ap()

    x_in = din("x", (N, D), F32)
    w_in = {n: din(n, (D, D), F32) for n in ("wq", "wk", "wv", "wo")}
    bq_i = din("bq", (D,), F32)
    bk_i = din("bk", (D,), F32)
    bv_i = din("bv", (D,), F32)
    bo_i = din("bo", (D,), F32)
    gru_w = din("gru_w", (8, DH), F32)
    gru_b = din("gru_b", (1, 8), F32)
    gru_c = din("gru_c", (48, 1), F32)
    rel_e = din("rel_embed", (NB, H), F32)
    wr_i = din("wr", (E, D), F32)
    w1e_i = din("w1e", (D, II), F32)
    b1e_i = din("b1e", (II,), F32)
    w2e_i = din("w2e", (II, D), F32)
    b2e_i = din("b2e", (D,), F32)
    fw1_i = din("fw1", (D, II), F32)
    fb1_i = din("fb1", (II,), F32)
    fw2_i = din("fw2", (II, D), F32)
    fb2_i = din("fb2", (D,), F32)
    idn_i = din("idn", (128, 128), BF16)
    idn32_i = din("idn32", (128, 128), F32)
    ltx_i = din("ltx", (128, 128), BF16)
    anti_i = din("anti", (128, 128), BF16)
    onez_i = din("onez", (1, 128), BF16)
    onez32_i = din("onez32", (1, 128), F32)
    g1h_i = din("g1h", (384, NREL), BF16)
    eidx_i = din("eidx", (128, 8), F32)
    coreoh_i = din("coreoh", (1, 8), F32)

    out = nc.dram_tensor("out", [N, D], F32, kind="ExternalOutput").ap()

    dbg_specs = {
        "qt": ((D, N), BF16), "kt": ((D, N), BF16),
        "vaug": ((128, NT * H * 65), BF16), "gcol": ((128, NT * H), F32),
        "gcolr": ((128, NT * H), F32), "wpm": ((128, NT * H * 2), F32),
        "biasrel": ((H, NREL), BF16), "x1": ((N, D), F32),
        "attnT": ((D, N), BF16), "wm": ((E, N), BF16),
        "pos": ((128, 8 * 256), F32), "kept": ((128, 8 * 256), F32),
        "rso": ((160, 128), F32), "slotc": ((128, 256), F32),
        "yfb": ((N, D), BF16), "ye": ((YROWS, D), BF16),
        "xe": ((CAP + 1, D), BF16), "idxa": ((128, 16 * 256), F32),
    }
    dbg = {}
    for name in DBG:
        shp, dt = dbg_specs[name]
        dbg[name] = nc.dram_tensor("dbg_" + name, list(shp), dt,
                                   kind="ExternalOutput").ap()

    def hbm(name, shape, dt):
        return nc.dram_tensor(name, list(shape), dt, kind="Internal").ap()

    biasrel_h = hbm("biasrel", (H, BRP), BF16)
    wT_h = {n: hbm(n + "T", (D, D), BF16) for n in ("wq", "wk", "wv", "wo")}
    qt_h = hbm("qt", (D, N), BF16)
    kt_h = hbm("kt", (D, N), BF16)
    attnT_h = hbm("attnT", (D, N), BF16)
    x1_h = hbm("x1f", (N, D), F32)
    x1bf_h = hbm("x1bf", (N, D), BF16)
    x1g_h = hbm("x1g", (T, D), BF16)
    x1t_h = hbm("x1t", (D, N), BF16)
    wmT_h = hbm("wmT", (E, N), BF16)
    wall_h = hbm("wall", (NCORE * E, N), BF16)
    rsi_h = hbm("rs_in", (NCORE * 160, 128), F32)
    rso_h = hbm("rs_out", (160, 128), F32)
    xe_h = hbm("xe", (CAP + 1, D), BF16)
    ht_h = hbm("ht", (II, CAP), BF16)
    hft_h = hbm("hft", (II, N), BF16)
    ye_h = hbm("ye", (YROWS, D), BF16)
    yall_h = hbm("yall", (NCORE * YROWS, D), BF16)
    yfb_h = hbm("yfb", (N, D), BF16)

    es = ExitStack()

    def S(name, shape, dt):
        return es.enter_context(nc.sbuf_tensor(name, list(shape), dt))

    bq_s = S("bq_s", (128, 8), F32)
    bk_s = S("bk_s", (128, 8), F32)
    bv_bc = S("bv_bc", (128, D), F32)
    bo_bc = S("bo_bc", (128, D), F32)
    b2_bc = S("b2_bc", (128, D), F32)
    fb2_bc = S("fb2_bc", (128, D), F32)
    b1_s = S("b1_s", (128, II // 128), F32)
    fb1_s = S("fb1_s", (128, II // 128), F32)
    gwa = S("gwa", (128, 8 * 48), BF16)
    gba = S("gba", (48, 1), F32)
    gruc_s = S("gruc_s", (48, 1), F32)
    bpm_bc = S("bpm_bc", (128, 2 * H), F32)
    idn_s = S("idn_s", (128, 128), BF16)
    idn32_s = S("idn32_s", (128, 128), F32)
    ltx_s = S("ltx_s", (128, 128), BF16)
    anti_s = S("anti_s", (128, 128), BF16)
    onez_s = S("onez_s", (1, 128), BF16)
    onez32_s = S("onez32_s", (1, 128), F32)
    onecol_s = S("onecol_s", (128, 8), BF16)
    eidx_s = S("eidx_s", (128, 8), F32)
    coreoh_bc = S("coreoh_bc", (128, 8), F32)
    ones8_s = S("ones8_s", (128, 8), F32)
    zrow_s = S("zrow_s", (1, D), BF16)
    wrT_s = S("wrT_s", (128, 8 * E), F32)
    slotc_s = S("slotc_s", (128, 256), F32)
    slotc32_s = S("slotc32_s", (128, 256), I32)
    rel_s = S("rel_s", (128, 16), BF16)
    b2row_s = S("b2row_s", (1, D), BF16)
    fb2row_s = S("fb2row_s", (1, D), BF16)
    small_s = S("small_s", (128, 64), F32)

    arena = S("arena", (128, ARENA_BYTES), mybir.dt.uint8)

    def A(off, dt, cols, rows=128, row0=0):
        esz = {F32: 4, BF16: 2, I32: 4, I16: 2}[dt]
        v = arena.bitcast(dt)
        return v[row0:row0 + rows, off // esz:off // esz + cols]

    pp = [es.enter_context(nc.psum_tensor(f"pp{i}", [128, 512], F32))
          for i in range(8)]

    def ppb(i):
        return pp[i].bitcast(BF16)

    b = Bld(nc, es)
    qd = b.counter("qd", 16)
    gd = b.counter("gd", 16)
    sd = b.counter("sd", 16)
    pe = b.counter("pe", 1)
    ac = b.counter("ac", 1)
    dv = b.counter("dv", 1)
    cc = b.counter("cc", 1)
    od = b.counter("od", 16)
    E_, W_ = b.E, b.W

    def stg(slot):
        return A(OF_STG + slot * D * 4, F32, D)

    def stgb(slot):
        return A(OF_STGB + slot * D * 2, BF16, D)

    def xa(slot):
        return A(OF_XA + slot * D * 4, F32, D)

    b.tag = "P0"
    CH = {}

    CHAL = {}
    for _s in range(2):
        for _q in range(4):
            for _fam in ("fx", "hl", "ex"):
                CHAL["%s%d_%d" % (_fam, _s, _q)] = "xq%d_%d" % (_s, _q)
    CHAL.update({"hs2": "kt0", "hs3": "kt1"})

    def ch(name):
        name = CHAL.get(name, name)
        if name not in CH:
            CH[name] = b.counter("c_" + name, 16)
        return CH[name]

    def dma(eng, name, out, in_, **kw):
        c = ch(name)
        E_(eng, "dma_start", out=out, in_=in_, inc=c, **kw)
        return c.val

    for dst, srcp in ((idn_s, idn_i), (idn32_s, idn32_i), (ltx_s, ltx_i),
                      (anti_s, anti_i), (onez_s, onez_i),
                      (onez32_s, onez32_i), (eidx_s, eidx_i),
                      (gruc_s, gru_c)):
        dma("sync", "k0", dst[:, :], srcp[:, :])
    dma("sync", "k0", stgb(1).bitcast(F32)[0:8, 0:128],
        AP(bq_i.tensor, 0, [[128, 8], [1, 128]]))
    dma("sync", "k0", stgb(1).bitcast(F32)[0:8, 128:256],
        AP(bk_i.tensor, 0, [[128, 8], [1, 128]]))
    dma("sync", "k0", A(OF_XTC, F32, 256)[0:32, 0:128],
        AP(b1e_i.tensor, 0, [[128, 32], [1, 128]]))
    dma("sync", "k0", A(OF_XTC, F32, 256)[0:32, 128:256],
        AP(fb1_i.tensor, 0, [[128, 32], [1, 128]]))
    dma("sync", "k0", small_s[0:1, 0:16],
        AP(rel_e.tensor, 159 * H, [[H, 1], [1, H]]))
    dma("sync", "k0", small_s[0:1, 16:32],
        AP(rel_e.tensor, 319 * H, [[H, 1], [1, H]]))
    dma("sync", "k0", small_s[0:1, 32:40], coreoh_i[:, :])
    dma("sync", "k0", small_s[0:1, 40:48], gru_b[:, :])
    for g in range(4):
        dma("sync", "k0", stg(0)[0:2, g * DH:(g + 1) * DH],
            AP(gru_w.tensor, g * DH, [[4 * DH, 2], [1, DH]]))
    dma("sync", "k0", stg(1)[0:1, 0:512],
        AP(bv_i.tensor, 0, [[D, 1], [1, 512]]))
    dma("sync", "k0", xa(0)[0:1, 0:512],
        AP(bv_i.tensor, 512, [[D, 1], [1, 512]]))
    dma("sync", "k0", stg(1)[0:1, 512:1024],
        AP(bo_i.tensor, 0, [[D, 1], [1, 512]]))
    dma("sync", "k0", xa(0)[0:1, 512:1024],
        AP(bo_i.tensor, 512, [[D, 1], [1, 512]]))
    dma("gpsimd", "b2r", b2row_s[0:1, :],
        AP(b2e_i.tensor, 0, [[D, 1], [1, 1024]]))
    dma("gpsimd", "b2r", fb2row_s[0:1, :],
        AP(fb2_i.tensor, 0, [[D, 1], [1, 1024]]))
    dma("sync", "k0", xa(1)[0:1, 0:512],
        AP(b2e_i.tensor, 0, [[D, 1], [1, 512]]))
    dma("sync", "k0", xa(1)[0:1, 512:1024],
        AP(b2e_i.tensor, 512, [[D, 1], [1, 512]]))
    dma("sync", "k0", stg(0)[0:1, 512:1024],
        AP(fb2_i.tensor, 0, [[D, 1], [1, 512]]))
    dma("sync", "k0", stg(0)[0:1, 256 + 768:1024 + 256],
        AP(fb2_i.tensor, 512, [[D, 1], [1, 512]])) if False else None
    dma("sync", "k0", xa(0).bitcast(F32)[0:1, 1024 + 0:1024 + 0], None) \
        if False else None
    fb2hi_row = A(OF_XTC + 1024, F32, 512)
    dma("sync", "k0", fb2hi_row[0:1, :],
        AP(fb2_i.tensor, 512, [[D, 1], [1, 512]]))
    k0 = ch("k0")
    E_("vector", "memset", zrow_s[:, :], 0.0)
    E_("vector", "memset", onecol_s[:, :], 1.0)
    E_("vector", "memset", ones8_s[:, :], 1.0)
    E_("vector", "memset", gba[:, :], 0.0, inc=dv)
    dv_gba0 = dv.val

    W_("tensor", k0)
    E_("tensor", "matmul", pp[0][:, :], onez32_s[:, :], stg(1)[0:1, 0:512],
       start=True, stop=True)
    E_("tensor", "matmul", pp[1][:, :], onez32_s[:, :], xa(0)[0:1, 0:512],
       start=True, stop=True)
    E_("tensor", "matmul", pp[2][:, :], onez32_s[:, :], stg(1)[0:1, 512:1024],
       start=True, stop=True)
    E_("tensor", "matmul", pp[3][:, :], onez32_s[:, :], xa(0)[0:1, 512:1024],
       start=True, stop=True)
    E_("tensor", "matmul", pp[4][:, :], onez32_s[:, :], xa(1)[0:1, 0:512],
       start=True, stop=True)
    E_("tensor", "matmul", pp[5][:, :], onez32_s[:, :], xa(1)[0:1, 512:1024],
       start=True, stop=True)
    E_("tensor", "matmul", pp[6][:, :], onez32_s[:, :], stg(0)[0:1, 512:1024],
       start=True, stop=True)
    E_("tensor", "matmul", pp[7][:, :], onez32_s[:, :], fb2hi_row[0:1, :],
       start=True, stop=True)
    E_("tensor", "matmul", pp[0][:, :], onez32_s[:, :], stg(1)[0:1, 0:512],
       start=False, stop=True) if False else None
    E_("tensor", "matmul", pp[7][:, 256:288], onez32_s[:, :],
       small_s[0:1, 0:32], start=True, stop=True)
    E_("tensor", "matmul", pp[7][:, 288:296], onez32_s[:, :],
       small_s[0:1, 32:40], start=True, stop=True, inc=pe)
    W_("scalar", pe, pe.val)
    E_("scalar", "copy", bv_bc[:, 0:512], pp[0][:, :])
    E_("scalar", "copy", bv_bc[:, 512:1024], pp[1][:, :])
    E_("scalar", "copy", bo_bc[:, 0:512], pp[2][:, :])
    E_("scalar", "copy", bo_bc[:, 512:1024], pp[3][:, :])
    E_("scalar", "copy", b2_bc[:, 0:512], pp[4][:, :])
    E_("scalar", "copy", b2_bc[:, 512:1024], pp[5][:, :])
    E_("scalar", "copy", fb2_bc[:, 0:512], pp[6][:, :])
    E_("scalar", "copy", fb2_bc[:, 512:1024], pp[7][:, 0:512])
    E_("scalar", "copy", bpm_bc[:, :], pp[7][:, 256:288])
    E_("scalar", "copy", coreoh_bc[:, :], pp[7][:, 288:296], inc=ac)
    ac_bc = ac.val

    W_("tensor", ac, ac_bc)
    E_("tensor", "matmul", pp[0][:, :], onez32_s[:, :], fb2hi_row[0:1, :],
       start=True, stop=True, inc=pe)
    W_("scalar", pe, pe.val)
    E_("scalar", "copy", fb2_bc[:, 512:1024], pp[0][:, :], inc=ac)
    ac_bc = ac.val

    W_("tensor", ac, ac_bc)
    for (srcap, rows, dstap, bscale) in (
            (stgb(1).bitcast(F32)[0:8, 0:128], 8, bq_s[:, 0:8], 0.125),
            (stgb(1).bitcast(F32)[0:8, 128:256], 8, bk_s[:, 0:8], 1.0),
            (A(OF_XTC, F32, 256)[0:32, 0:128], 32, b1_s[:, 0:32], 1.0),
            (A(OF_XTC, F32, 256)[0:32, 128:256], 32, fb1_s[:, 0:32], 1.0)):
        E_("tensor", "transpose", pp[3][:, 0:rows], srcap,
           idn32_s[0:rows, 0:rows], inc=pe)
        W_("scalar", pe, pe.val)
        E_("scalar", "activation", dstap, pp[3][:, 0:rows], AF.Identity,
           scale=bscale, inc=ac)
        W_("tensor", ac, ac.val)

    W_("vector", k0)
    E_("vector", "reduce_sum", small_s[0:1, 56:57], small_s[0:1, 40:44],
       axis=AX)
    E_("vector", "reduce_sum", small_s[0:1, 57:58], small_s[0:1, 44:48],
       axis=AX, inc=dv)
    W_("tensor", dv, dv.val)
    E_("tensor", "matmul", pp[2][:, 0:2], onez32_s[:, :],
       small_s[0:1, 56:58], start=True, stop=True, inc=pe)
    W_("scalar", pe, pe.val)
    W_("scalar", dv, dv_gba0)
    E_("scalar", "copy", gba[0:16, 0:1], pp[2][0:16, 0:1])
    E_("scalar", "copy", gba[32:48, 0:1], pp[2][32:48, 1:2], inc=ac)

    E_("vector", "tensor_add", stg(0)[0:2, 256:320], stg(0)[0:2, 0:64],
       stg(0)[0:2, 64:128])
    E_("vector", "tensor_add", stg(0)[0:2, 320:384], stg(0)[0:2, 128:192],
       stg(0)[0:2, 192:256])
    E_("vector", "tensor_add", stg(0)[0:2, 384:448], stg(0)[0:2, 256:320],
       stg(0)[0:2, 320:384])
    E_("vector", "tensor_copy", stgb(0)[0:2, 0:64], stg(0)[0:2, 384:448],
       inc=dv)
    W_("tensor", dv, dv.val)
    W_("tensor", ac, ac.val)
    E_("tensor", "transpose", ppb(3)[0:64, 0:2], stgb(0)[0:2, 0:64],
       idn_s[0:2, 0:2], inc=pe)
    W_("scalar", pe, pe.val)
    E_("scalar", "copy", stgb(0)[0:64, 64:66], ppb(3)[0:64, 0:2], inc=ac)
    ac_g2 = ac.val
    E_("vector", "memset", gwa[:, :], 0.0)
    W_("vector", ac, ac_g2)
    for h in range(H):
        kt_i, r0 = h // 2, (h % 2) * 64
        E_("vector", "tensor_copy",
           gwa[r0:r0 + 64, kt_i * 48 + h:kt_i * 48 + h + 1],
           stgb(0)[0:64, 64:65])
        E_("vector", "tensor_copy",
           gwa[r0:r0 + 64, kt_i * 48 + 32 + h:kt_i * 48 + 32 + h + 1],
           stgb(0)[0:64, 65:66], inc=(dv if h == H - 1 else None))

    conv_done = [dv.val, dv.val]
    tpose_done = [ac.val, ac.val]
    bank_free = [pe.val] * 8
    for _bi in range(8):
        bank_free[_bi] = ac.val
    xa_guard_pe = pe.val
    wst_ch = [0, 0]
    for wi, (wn, scale) in enumerate(
            (("wq", 0.125), ("wk", 1.0), ("wv", 1.0), ("wo", 1.0))):
        for rc in range(8):
            slot = (wi * 8 + rc) % 2
            W_("sync", dv, conv_done[slot])
            if wi == 0 and rc < 2:
                W_("sync", pe, xa_guard_pe)
            ldv = dma("sync", "wx%d" % slot, xa(slot)[:, :],
                      w_in[wn][rc * 128:(rc + 1) * 128, :])
            W_("vector", ch("wx%d" % slot), ldv)
            W_("vector", ac, tpose_done[slot])
            E_("vector", "tensor_scalar", stgb(slot)[:, :], xa(slot)[:, :],
               scale, None, ALU.mult, inc=dv)
            conv_done[slot] = dv.val
            W_("tensor", dv, conv_done[slot])
            for ccc in range(8):
                bk2 = 4 + (ccc % 2)
                W_("tensor", ac, bank_free[bk2])
                E_("tensor", "transpose", ppb(bk2)[:, 0:128],
                   stgb(slot)[:, ccc * 128:(ccc + 1) * 128], idn_s[:, :],
                   inc=pe)
                dstc = A(OF_WSTR + (ccc % 2) * 8 * 128 * 2, BF16, 128)
                W_("scalar", pe, pe.val)
                W_("scalar", ch("ws%d" % (ccc % 2)), wst_ch[ccc % 2])
                E_("scalar", "copy", dstc[:, :], ppb(bk2)[:, 0:128], inc=ac)
                bank_free[bk2] = ac.val
                if ccc == 7:
                    tpose_done[slot] = ac.val
                W_("sync", ac, ac.val)
                wst_ch[ccc % 2] = dma(
                    "sync", "ws%d" % (ccc % 2),
                    wT_h[wn][ccc * 128:(ccc + 1) * 128,
                             rc * 128:(rc + 1) * 128], dstc[:, :])
    W_("sync", dv, dv.val)
    W_("sync", pe, pe.val)
    wrld = dma("sync", "k0", stg(0)[0:8, 0:D], wr_i[:, :])
    W_("tensor", ch("k0"), wrld)
    for ccc in range(8):
        bk2 = 4 + (ccc % 2)
        W_("tensor", ac, bank_free[bk2])
        E_("tensor", "transpose", pp[bk2][:, 0:8],
           stg(0)[0:8, ccc * 128:(ccc + 1) * 128], idn32_s[0:8, 0:8], inc=pe)
        W_("scalar", pe, pe.val)
        E_("scalar", "copy", wrT_s[:, ccc * 8:(ccc + 1) * 8], pp[bk2][:, 0:8],
           inc=ac)
        bank_free[bk2] = ac.val

    g1hv = A(OF_G1H, BF16, NREL)
    for half in range(2):
        for kc3 in range(3):
            W_("sync", pe, pe.val)
            g1ld = dma("sync", "g1", g1hv[:, :],
                       g1h_i[kc3 * 128:(kc3 + 1) * 128, :])
            rows = 128 if kc3 < 2 else 64
            W_("sync", dv, dv.val)
            rld = dma("sync", "rl", stg(1)[0:rows, 0:16],
                      rel_e[kc3 * 128:kc3 * 128 + rows, :])
            W_("vector", ch("rl"), rld)
            W_("vector", pe, pe.val)
            if kc3 == 2:
                E_("vector", "memset", rel_s[:, :], 0.0)
            E_("vector", "tensor_copy", rel_s[0:rows, 0:16],
               stg(1)[0:rows, 0:16], inc=dv)
            W_("tensor", dv, dv.val)
            W_("tensor", ch("g1"), g1ld)
            if kc3 == 0:
                for _b2 in range(8):
                    W_("tensor", ac, bank_free[_b2])
            for nch in range(8):
                ncol = half * 8 + nch
                n0 = ncol * 512
                nn = min(512, NREL - n0)
                if nn <= 0:
                    continue
                E_("tensor", "matmul", pp[nch][0:16, 0:nn], rel_s[:, 0:16],
                   g1hv[:, n0:n0 + nn], start=(kc3 == 0), stop=(kc3 == 2),
                   inc=(pe if nch == 7 else None))
        W_("scalar", pe, pe.val)
        for nch in range(8):
            ncol = half * 8 + nch
            n0 = ncol * 512
            nn = min(512, NREL - n0)
            if nn <= 0:
                continue
            W_("scalar", ch("br%d" % (nch % 2)), CH.get("br%d" % (nch % 2))
               and CH["br%d" % (nch % 2)].val or 0)
            E_("scalar", "copy", stgb(nch % 2)[0:16, 0:nn],
               pp[nch][0:16, 0:nn], inc=ac)
            bank_free[nch] = ac.val
            W_("sync", ac, ac.val)
            dma("sync", "br%d" % (nch % 2),
                AP(biasrel_h.tensor, n0, [[BRP, 16], [1, nn]]),
                stgb(nch % 2)[0:16, 0:nn])
    if "biasrel" in dbg:
        W_("gpsimd", ch("br0"))
        W_("gpsimd", ch("br1"))
        dma("gpsimd", "dbg", dbg["biasrel"][:, :], biasrel_h[:, 0:NREL])

    b.tag = "P1"
    vaugv = A(OF_VAUG, BF16, NT * H * 65)
    gcolv = A(OF_GCOL, F32, NT * H)
    gcolrv = A(OF_GCOLR, F32, NT * H)
    wpmv = A(OF_WPM, F32, NT * H * 2)
    xtcv = A(OF_XTC, BF16, 8 * 512)

    def wstr(slot):
        return A(OF_WSTR + slot * 8 * 128 * 2, BF16, 8 * 128)

    OF_PSTG = OF_G1H + 16 * 512 * 2 + 8 * 512 * 2 + 2 * 16 * 1024

    def pstg(slot):
        return A(OF_PSTG + slot * 512 * 2, BF16, 512)

    vstr = A(OF_G1H + 4 * 512 * 2, BF16, 8 * 512)

    E_("vector", "memset", vaugv, 1.0)
    W_("vector", ch("br0"))
    W_("vector", ch("br1"))
    W_("sync", ch("ws0"))
    W_("sync", ch("ws1"))
    W_("sync", ch("g1"))
    OF_XTC2 = P01_END
    OF_WQK = OF_XTC2 + 8 * 512 * 2
    OF_WKK = OF_WQK + 8 * 1024 * 2
    assert OF_WKK + 8 * 1024 * 2 <= ARENA_BYTES
    xtc2v = A(OF_XTC2, BF16, 8 * 512)
    wqk = A(OF_WQK, BF16, 8 * 1024)
    wkk = A(OF_WKK, BF16, 8 * 1024)
    vstr2 = A(OF_G1H, BF16, 16 * 512)
    W_("sync", pe, pe.val)
    wq_ld = dma("sync", "pw0", wqk[:, :],
                AP(wT_h["wq"].tensor, 0, [[D, 128], [128 * D, 8], [1, 1024]]))
    wk_ld = dma("sync", "pw1", wkk[:, :],
                AP(wT_h["wk"].tensor, 0, [[D, 128], [128 * D, 8], [1, 1024]]))
    vld = 0
    for oc in range(2):
        vld = dma("sync", "pv", vstr2[:, oc * 4096:(oc + 1) * 4096],
                  AP(wT_h["wv"].tensor, oc * 512,
                     [[D, 128], [128 * D, 8], [1, 512]]))
    x_conv = [(dv, dv.val), (dv, dv.val)]
    stgb_free = [(ac, ac.val), (ac, ac.val)]
    pstg_free = [None, None, None, None]
    stg_free = [(dv, dv.val), (dv, dv.val)]
    xt_ready = [0, 0]
    xtc_free = [(pe, pe.val), (pe, pe.val)]
    WT = b.WT

    def xtc(p):
        return xtcv if p == 0 else xtc2v

    def emit_xt(tc):
        p = tc % 2
        first_copy = True
        for sub in range(4):
            slot = sub % 2
            WT("sync", x_conv[slot])
            ldv = dma("sync", "px%d" % slot, xa(slot)[:, :],
                      x_in[(tc * 4 + sub) * 128:(tc * 4 + sub + 1) * 128, :])
            W_("vector", ch("px%d" % slot), ldv)
            WT("vector", stgb_free[slot])
            E_("vector", "tensor_copy", stgb(slot)[:, :], xa(slot)[:, :],
               inc=dv)
            x_conv[slot] = (dv, dv.val)
            W_("tensor", dv, dv.val)
            for dt_ in range(8):
                bk2 = 4 + (dt_ % 2)
                W_("tensor", ac, bank_free[bk2])
                E_("tensor", "transpose", ppb(bk2)[:, 0:128],
                   stgb(slot)[:, dt_ * 128:(dt_ + 1) * 128], idn_s[:, :],
                   inc=pe)
                W_("scalar", pe, pe.val)
                if first_copy:
                    WT("scalar", xtc_free[p])
                    first_copy = False
                E_("scalar", "copy",
                   xtc(p)[:, dt_ * 512 + sub * 128:dt_ * 512 + sub * 128 + 128],
                   ppb(bk2)[:, 0:128], drain=False, inc=ac)
                bank_free[bk2] = ac.val
            stgb_free[slot] = (ac, ac.val)
        xt_ready[p] = ac.val

    emit_xt(0)
    for tc in range(8):
        p = tc % 2
        for pi, (wres, bias_col, dsth) in enumerate(
                ((wqk, bq_s, qt_h), (wkk, bk_s, kt_h))):
            for ot in range(8):
                bk0 = ot % 2
                if tc == 0 and pi == 0 and ot == 0:
                    W_("tensor", ch("pw0"), wq_ld)
                    W_("tensor", ch("pw1"), wk_ld)
                    W_("tensor", ch("pv"), vld)
                W_("tensor", ac, max(xt_ready[p], bank_free[bk0]))
                for kc in range(8):
                    E_("tensor", "matmul", pp[bk0][:, :],
                       wres[:, kc * 1024 + ot * 128:kc * 1024 + (ot + 1) * 128],
                       xtc(p)[:, kc * 512:(kc + 1) * 512],
                       start=(kc == 0), stop=(kc == 7),
                       inc=(pe if kc == 7 else None))
                pslot = pi * 2 + (ot % 2)
                W_("scalar", pe, pe.val)
                WT("scalar", pstg_free[pslot])
                E_("scalar", "activation", pstg(pslot)[:, :], pp[bk0][:, :],
                   AF.Identity, bias=bias_col[:, ot:ot + 1], drain=False,
                   inc=ac)
                bank_free[bk0] = ac.val
                W_("sync", ac, ac.val)
                _v = dma(
                    "sync", "ps%d" % pslot,
                    dsth[ot * 128:(ot + 1) * 128, tc * 512:(tc + 1) * 512],
                    pstg(pslot)[:, :])
                pstg_free[pslot] = (ch("ps%d" % pslot), _v)
        if tc + 1 < 8:
            emit_xt(tc + 1)
        for oc in range(2):
            for sub in range(4):
                bkv = 2 + (sub % 2)
                W_("tensor", ac, bank_free[bkv])
                for kc in range(8):
                    E_("tensor", "matmul", pp[bkv][:, :],
                       xtc(p)[:, kc * 512 + sub * 128:
                              kc * 512 + sub * 128 + 128],
                       vstr2[:, oc * 4096 + kc * 512:
                             oc * 4096 + (kc + 1) * 512],
                       start=(kc == 0), stop=(kc == 7),
                       inc=(pe if kc == 7 else None))
                sslot = sub % 2
                W_("scalar", pe, pe.val)
                WT("scalar", stg_free[sslot])
                E_("scalar", "copy", stg(sslot)[:, 0:512], pp[bkv][:, :],
                   drain=False, inc=ac)
                bank_free[bkv] = ac.val
                W_("vector", ac, ac.val)
                it = tc * 4 + sub
                for hh in range(8):
                    h = oc * 8 + hh
                    E_("vector", "tensor_tensor",
                       vaugv[:, (it * H + h) * 65:(it * H + h) * 65 + 64],
                       stg(sslot)[:, hh * 64:hh * 64 + 64],
                       bv_bc[:, oc * 512 + hh * 64:oc * 512 + hh * 64 + 64],
                       ALU.add, drain=False, inc=(dv if hh == 7 else None))
                stg_free[sslot] = (dv, dv.val)
        W_("tensor", ac, max(xt_ready[p], bank_free[6]))
        for kc in range(8):
            E_("tensor", "matmul", pp[6][0:48, :],
               gwa[:, kc * 48:(kc + 1) * 48],
               xtc(p)[:, kc * 512:(kc + 1) * 512],
               start=(kc == 0), stop=(kc == 7), inc=(pe if kc == 7 else None))
        W_("scalar", pe, pe.val)
        WT("scalar", stg_free[0])
        E_("scalar", "activation", stg(0)[0:48, 0:512], pp[6][0:48, :],
           AF.Sigmoid, bias=gba[:, 0:1], inc=ac)
        bank_free[6] = ac.val
        W_("vector", ac, ac.val)
        WT("vector", stg_free[1])
        E_("vector", "tensor_scalar", stg(1)[0:16, 512:1024],
           stg(0)[32:48, 0:512], gruc_s[32:48, 0:1], -1.0, ALU.mult, ALU.add)
        E_("vector", "tensor_mul", stg(1)[64:80, 0:512],
           stg(1)[0:16, 512:1024], stg(0)[0:16, 0:512])
        E_("vector", "tensor_scalar", stg(1)[0:16, 0:512],
           stg(1)[64:80, 0:512], 2.0, None, ALU.add)
        E_("vector", "tensor_copy", stg(1)[32:48, 0:512],
           stg(1)[0:16, 511::-1], inc=dv)
        g_ready = dv.val
        stg_free[0] = (dv, dv.val)
        W_("tensor", dv, g_ready)
        for sub in range(4):
            bk2 = 4 + (sub % 2)
            W_("tensor", ac, bank_free[bk2])
            E_("tensor", "transpose", pp[bk2][:, 0:16],
               stg(1)[0:16, sub * 128:(sub + 1) * 128],
               idn32_s[0:16, 0:16], inc=pe)
            W_("scalar", pe, pe.val)
            it = tc * 4 + sub
            E_("scalar", "copy", gcolv[:, it * H:(it + 1) * H],
               pp[bk2][:, 0:16], drain=False, inc=ac)
            bank_free[bk2] = ac.val
            E_("tensor", "transpose", pp[bk2][:, 16:32],
               stg(1)[32:48, sub * 128:(sub + 1) * 128],
               idn32_s[32:48, 32:48], inc=pe)
            W_("scalar", pe, pe.val)
            itr = tc * 4 + (3 - sub)
            E_("scalar", "copy", gcolrv[:, itr * H:(itr + 1) * H],
               pp[bk2][:, 16:32], drain=False, inc=ac)
            bank_free[bk2] = ac.val
        gcol_ready = ac.val
        W_("vector", ac, gcol_ready)
        for sub in range(4):
            it = tc * 4 + sub
            E_("vector", "tensor_tensor", stg(1)[:, sub * 32:sub * 32 + 16],
               gcolv[:, it * H:(it + 1) * H], bpm_bc[:, 0:16], ALU.mult,
               drain=False)
            E_("vector", "tensor_tensor",
               stg(1)[:, sub * 32 + 16:sub * 32 + 32],
               gcolv[:, it * H:(it + 1) * H], bpm_bc[:, 16:32], ALU.mult,
               inc=(dv if sub == 3 else None))
        W_("scalar", dv, dv.val)
        for sub in range(4):
            it = tc * 4 + sub
            E_("scalar", "activation", wpmv[:, it * 32:(it + 1) * 32],
               stg(1)[:, sub * 32:(sub + 1) * 32], AF.Exp,
               drain=False, inc=(ac if sub == 3 else None))
        stg_free[1] = (ac, ac.val)
        xtc_free[p] = (pe, pe.val)

    p1_ac = ac.val
    p1_dv = dv.val
    if "qt" in dbg:
        W_("gpsimd", ch("ps0"))
        W_("gpsimd", ch("ps1"))
        dma("gpsimd", "dbg", dbg["qt"][:, :], qt_h[:, :])
    if "kt" in dbg:
        W_("gpsimd", ch("ps2"))
        W_("gpsimd", ch("ps3"))
        dma("gpsimd", "dbg", dbg["kt"][:, :], kt_h[:, :])
    if "vaug" in dbg:
        W_("gpsimd", dv, p1_dv)
        dma("gpsimd", "dbg", dbg["vaug"][:, :], vaugv)
    if "gcol" in dbg:
        W_("gpsimd", ac, p1_ac)
        dma("gpsimd", "dbg", dbg["gcol"][:, :], gcolv)
    if "gcolr" in dbg:
        W_("gpsimd", ac, p1_ac)
        dma("gpsimd", "dbg", dbg["gcolr"][:, :], gcolrv)
    if "wpm" in dbg:
        W_("gpsimd", ac, p1_ac)
        dma("gpsimd", "dbg", dbg["wpm"][:, :], wpmv)

    b.tag = "ATT"
    OF_ATTB = OF_ATTS + 2 * 512 * 2
    assert OF_ATTB + 256 * 2 <= ARENA_BYTES

    def ktb(slot):
        return A(OF_KTB + slot * 4096 * 2, BF16, 4096)

    def qpadv(v):
        return A(OF_QPAD + v * 512 * 2, BF16, 512)

    def a1t(hp):
        return A(OF_A1T + hp * 21 * 128 * 2, BF16, 21 * 128)

    def diagv(s):
        return A(OF_DIAG + s * 4 * 128 * 2, BF16, 4 * 128)

    def expsv(e4):
        return A(OF_EXPS + e4 * 512 * 2, BF16, 512)

    rcmb = A(OF_RCMB, F32, 1024)
    attbf = A(OF_ATTB, BF16, 256)
    OF_RC2 = OF_ATTB + 256 * 2
    OF_RCMB2 = OF_RC2 + 1024 * 4
    assert OF_RCMB2 + 1024 * 4 <= ARENA_BYTES
    rc2 = A(OF_RC2, F32, 1024)
    rcmbs = [rcmb, A(OF_RCMB2, F32, 1024)]

    def attsv(s):
        return A(OF_ATTS + s * 512 * 2, BF16, 512)

    for cn in ("ps0", "ps1", "ps2", "ps3"):
        W_("sync", ch(cn))
    W_("sync", pe, pe.val)
    W_("sync", dv, dv.val)
    W_("sync", ac, ac.val)
    W_("vector", pe, pe.val)
    W_("vector", ac, ac.val)
    for v4 in range(4):
        E_("vector", "memset", qpadv(v4)[:, :], 0.0)
    E_("vector", "memset", rcmb[:, :], 0.0, inc=dv)
    qpad_zero = dv.val
    W_("sync", dv, qpad_zero)

    regcol = {}
    for ic in range(8):
        present = [0]
        if any(_tile_region(jt, ic) == 1 for jt in range(32)):
            present.append(1)
        if any(_tile_region(jt, ic) == 2 for jt in range(32)):
            present.append(2)
        regcol[ic] = {r: i * 65 for i, r in enumerate(present)}

    kt_ld = [None, None]
    kt_free = [None, None]
    a1_ld = [None, None]
    a1_free = [None, None]
    q_ld = [None] * 4
    q_free = [None] * 4
    diag_rdy = [None, None]
    diag_free = [None, None]
    scores_free = [None, None, None, None]
    SBK = (0, 1, 6, 7)
    exps_free = [None] * 4
    atts_free = [None, None]
    WT = b.WT

    def load_kt(h):
        slot = h % 2
        WT("sync", kt_free[slot])
        v = dma("sync", "kt%d" % slot, ktb(slot)[:, :],
                kt_h[(h // 2) * 128:(h // 2) * 128 + 128, :])
        kt_ld[slot] = (CH["kt%d" % slot], v)

    def load_a1(h):
        slot = h % 2
        WT("sync", a1_free[slot])
        for di in range(21):
            off = h * BRP + 128 * (di - 10) + 3968
            v = dma("sync", "a1%d" % slot,
                    a1t(slot)[:, di * 128:(di + 1) * 128],
                    AP(biasrel_h.tensor, off, [[1, 128], [1, 128]]))
        a1_ld[slot] = (CH["a1%d" % slot], v)

    def load_q(h, ic):
        v4 = (h % 2) * 2 + (ic % 2)
        WT("sync", q_free[v4])
        r0 = (h % 2) * 64
        v = dma("sync", "q%d" % v4,
                qpadv(v4)[r0:r0 + 64, :] if r0 else qpadv(v4)[0:64, :],
                qt_h[h * 64:(h + 1) * 64, ic * 512:(ic + 1) * 512])
        q_ld[v4] = (CH["q%d" % v4], v)

    load_kt(0)
    load_a1(0)
    load_q(0, 0)
    pending = [None]
    attbf_free = [None]
    rcmb_free = [None, None]
    for h in range(H):
        hp = h % 2
        for ic in range(8):
            it0 = ic * 4
            ds = (h * 8 + ic) % 2
            v4 = hp * 2 + (ic % 2)
            if ic + 1 < 8:
                load_q(h, ic + 1)
            elif h + 1 < H:
                load_q(h + 1, 0)
            if ic == 0 and h + 1 < H:
                load_kt(h + 1)
                load_a1(h + 1)
            WT("vector", diag_free[ds])
            for sub in range(4):
                it = it0 + sub
                E_("vector", "tensor_scalar",
                   diagv(ds)[:, sub * 128:(sub + 1) * 128], anti_s[:, :],
                   gcolrv[:, it * H + h:it * H + h + 1], None, ALU.mult,
                   drain=False, inc=(dv if sub == 3 else None))
            diag_rdy[ds] = (dv, dv.val)
            WT("tensor", kt_ld[hp])
            WT("tensor", a1_ld[hp])
            WT("tensor", q_ld[v4])
            WT("tensor", diag_rdy[ds])
            av_first = {0: (0, 0), 1: (0, 2)}
            av_last = {0: (31, 1), 1: (31, 3)}
            qk_done = [None] * 32

            def emit_qk(jt):
                sb = jt % 4
                WT("tensor", scores_free[sb])
                nearj = _near_pair(jt, it0) or _near_pair(jt, it0 + 1) or \
                    _near_pair(jt, it0 + 2) or _near_pair(jt, it0 + 3)
                nbias = [sub for sub in range(4)
                         if abs(jt - (it0 + sub)) <= 7] if nearj else []
                E_("tensor", "matmul", pp[SBK[sb]][:, :],
                   ktb(hp)[:, jt * 128:(jt + 1) * 128], qpadv(v4)[:, :],
                   start=True, stop=(not nbias),
                   inc=(pe if not nbias else None))
                for bi, sub in enumerate(nbias):
                    dd = jt - (it0 + sub)
                    E_("tensor", "matmul",
                       pp[SBK[sb]][:, sub * 128:(sub + 1) * 128],
                       a1t(hp)[:, (dd + 10) * 128:(dd + 11) * 128],
                       diagv(ds)[:, sub * 128:(sub + 1) * 128],
                       start=False, stop=(bi == len(nbias) - 1),
                       inc=(pe if bi == len(nbias) - 1 else None))
                qk_done[jt] = pe.val

            emit_qk(0)
            emit_qk(1)
            emit_qk(2)
            for jt in range(32):
                sb = jt % 4
                if jt + 3 < 32:
                    emit_qk(jt + 3)
                e4 = jt % 4
                W_("scalar", pe, qk_done[jt])
                WT("scalar", exps_free[e4])
                E_("scalar", "activation", expsv(e4)[:, :], pp[SBK[sb]][:, :],
                   AF.Exp, drain=False, inc=ac)
                exp_done = ac.val
                scores_free[sb] = (ac, exp_done)
                W_("tensor", ac, exp_done)
                for sub in range(4):
                    r = 0 if abs(jt - (it0 + sub)) <= 7 else \
                        (1 if jt > it0 + sub else 2)
                    bank = pp[2 + sub // 2]
                    coff = (sub % 2) * 256 + regcol[ic][r]
                    E_("tensor", "matmul", bank[:, coff:coff + 65],
                       expsv(e4)[:, sub * 128:(sub + 1) * 128],
                       vaugv[:, ((jt * H) + h) * 65:((jt * H) + h) * 65 + 65],
                       start=(av_first[sub // 2] == (jt, sub)),
                       stop=(av_last[sub // 2] == (jt, sub)),
                       inc=(pe if (jt == 31 and sub == 3) else None))
                exps_free[e4] = (pe, pe.val)
                if jt == 5 and pending[0] is not None:
                    pending[0]()
                    pending[0] = None
            kt_free[hp] = (pe, pe.val)
            a1_free[hp] = (pe, pe.val)
            q_free[v4] = (pe, pe.val)
            diag_free[ds] = (pe, pe.val)
            av_done = pe.val
            ncols = 65 * len(regcol[ic])
            ip = (h * 8 + ic) % 2
            rcmb_c = rcmbs[ip]
            W_("scalar", pe, av_done)
            WT("scalar", rcmb_free[ip])
            for bk_i in range(2):
                E_("scalar", "copy",
                   rcmb_c[:, bk_i * 512:bk_i * 512 + 256 + ncols],
                   pp[2 + bk_i][:, 0:256 + ncols],
                   drain=False, inc=(ac if bk_i == 1 else None))
            rc_done = ac.val

            def make_epilogue(h=h, ic=ic, it0=it0, rc_done=rc_done,
                              rcmb_c=rcmb_c, ip=ip):
                def emit():
                    W_("vector", ac, rc_done)
                    WT("vector", attbf_free[0])
                    has1 = 1 in regcol[ic]
                    has2 = 2 in regcol[ic]

                    def _cf(sub):
                        if has2:
                            return rc2, sub * 130 + 65
                        if has1:
                            return rc2, sub * 130
                        return rcmb_c, sub * 256
                    if has1:
                        for sub in range(4):
                            it = it0 + sub
                            base = sub * 256
                            E_("vector", "scalar_tensor_tensor",
                               rc2[:, sub * 130:sub * 130 + 65],
                               rcmb_c[:, base + regcol[ic][1]:
                                      base + regcol[ic][1] + 65],
                               wpmv[:, it * 32 + 16 + h:it * 32 + 16 + h + 1],
                               rcmb_c[:, base:base + 65], ALU.mult, ALU.add,
                               drain=(sub == 3))
                    if has2:
                        for sub in range(4):
                            it = it0 + sub
                            base = sub * 256
                            src1 = rc2[:, sub * 130:sub * 130 + 65] \
                                if has1 else rcmb_c[:, base:base + 65]
                            E_("vector", "scalar_tensor_tensor",
                               rc2[:, sub * 130 + 65:sub * 130 + 130],
                               rcmb_c[:, base + regcol[ic][2]:
                                      base + regcol[ic][2] + 65],
                               wpmv[:, it * 32 + h:it * 32 + h + 1],
                               src1, ALU.mult, ALU.add, drain=(sub == 3))
                    for sub in range(4):
                        bufc, c0 = _cf(sub)
                        E_("vector", "reciprocal",
                           rc2[:, 520 + sub:521 + sub],
                           bufc[:, c0 + 64:c0 + 65], drain=(sub == 3))
                    for sub in range(4):
                        bufc, c0 = _cf(sub)
                        E_("vector", "tensor_scalar",
                           attbf[:, sub * 64:sub * 64 + 64],
                           bufc[:, c0:c0 + 64], rc2[:, 520 + sub:521 + sub],
                           None, ALU.mult, drain=False,
                           inc=(dv if sub == 3 else None))
                    cmb_done = dv.val
                    rcmb_free[ip] = (dv, cmb_done)
                    W_("tensor", dv, cmb_done)
                    asl = ic % 2
                    for sub in range(4):
                        bk2 = 4 + (sub % 2)
                        W_("tensor", ac, bank_free[bk2])
                        E_("tensor", "transpose", ppb(bk2)[0:64, 0:128],
                           attbf[:, sub * 64:(sub + 1) * 64], idn_s[:, :],
                           inc=pe)
                        W_("scalar", pe, pe.val)
                        if sub < 2:
                            WT("scalar", atts_free[asl])
                        E_("scalar", "copy", attsv(asl)[0:64, sub * 128:
                                                        (sub + 1) * 128],
                           ppb(bk2)[0:64, 0:128], drain=False, inc=ac)
                        bank_free[bk2] = ac.val
                    attbf_free[0] = (pe, pe.val)
                    W_("sync", ac, ac.val)
                    v = dma("sync", "at%d" % asl,
                            attnT_h[h * 64:(h + 1) * 64,
                                    ic * 512:(ic + 1) * 512],
                            attsv(asl)[0:64, :])
                    atts_free[asl] = (CH["at%d" % asl], v)
                return emit
            pending[0] = make_epilogue()
    if pending[0] is not None:
        pending[0]()
        pending[0] = None
    bank_free[6] = ac.val
    bank_free[7] = ac.val
    if "attnT" in dbg:
        W_("gpsimd", ch("at0"))
        W_("gpsimd", ch("at1"))
        dma("gpsimd", "dbg", dbg["attnT"][:, :], attnT_h[:, :])

    b.tag = "LN1"

    def attld(slot):
        return A(OF_ATTLD + slot * 8 * 128 * 2, BF16, 8 * 128)

    x1t32 = A(OF_X1T32, F32, D)
    OF_X1T32B = 137 * 1024

    def x1t32p(p):
        return x1t32 if p == 0 else A(OF_X1T32B, F32, D)
    wmst = A(OF_WMST, BF16, 512)
    rtr = A(OF_RTR, F32, 256)
    W_("sync", ch("at0"))
    W_("sync", ch("at1"))
    W_("sync", pe, pe.val)
    W_("sync", dv, dv.val)
    W_("sync", ac, ac.val)
    W_("vector", pe, pe.val)
    W_("scalar", dv, dv.val)
    attld_free = [None, None]
    wo_free = [None, None]
    lnst_free = [None, None]
    x1st_free = [None, None]
    x1b_free = [None, None]
    x1t_free = [None, None]
    wm_free = [None]
    xin_free = [None, None]
    lgbank_free = [(ac, ac.val)]
    x1t32_free = [[], []]
    pending_rt = [None]
    for it in range(NT):
        sl = it % 2
        WT("sync", attld_free[sl])
        av_ = dma("sync", "al%d" % sl, attld(sl)[:, :],
                  AP(attnT_h.tensor, it * 128, [[N, 128], [128 * N, 8],
                                                [1, 128]]))
        WT("sync", wo_free[sl])
        wv_ = dma("sync", "wl%d" % sl, wstr(sl)[:, :],
                  AP(wT_h["wo"].tensor, it * 0, [[D, 128], [128 * D, 8],
                                                 [1, 128]])) if False else None
        if it == 0:
            wo_ld = dma("sync", "wo", vstr[:, :],
                        AP(wT_h["wo"].tensor, 0,
                           [[D, 128], [128 * D, 8], [1, 512]]))
            wo_ld2 = dma("sync", "wo2", A(OF_G1H + 12 * 1024, BF16, 8 * 512),
                         AP(wT_h["wo"].tensor, 512,
                            [[D, 128], [128 * D, 8], [1, 512]]))
        WT("sync", xin_free[sl])
        xv_ = dma("sync", "xi%d" % sl, xa(sl)[:, :],
                  x_in[it * 128:(it + 1) * 128, :])
        W_("tensor", ch("al%d" % sl), av_)
        if it == 0:
            W_("tensor", ch("wo"), wo_ld)
            W_("tensor", ch("wo2"), wo_ld2)
        W_("tensor", ac, bank_free[6])
        W_("tensor", ac, bank_free[7])
        for ot in range(8):
            E_("tensor", "matmul", pp[6][:, :],
               attld(sl)[:, ot * 128:(ot + 1) * 128],
               vstr[:, ot * 512:(ot + 1) * 512],
               start=(ot == 0), stop=(ot == 7))
            E_("tensor", "matmul", pp[7][:, :],
               attld(sl)[:, ot * 128:(ot + 1) * 128],
               A(OF_G1H + 12 * 1024, BF16, 8 * 512)[:, ot * 512:
                                                    (ot + 1) * 512],
               start=(ot == 0), stop=(ot == 7),
               inc=(pe if ot == 7 else None))
        attld_free[sl] = (pe, pe.val)
        oproj_done = pe.val
        W_("scalar", pe, oproj_done)
        WT("scalar", lnst_free[sl])
        E_("scalar", "copy", stg(sl)[:, 0:512], pp[6][:, :], drain=False)
        E_("scalar", "copy", stg(sl)[:, 512:1024], pp[7][:, :], drain=False,
           inc=ac)
        bank_free[6] = ac.val
        bank_free[7] = ac.val
        ev_done = ac.val
        W_("vector", ac, ev_done)
        W_("vector", ch("xi%d" % sl), xv_)
        E_("vector", "tensor_add", stgb(sl).bitcast(F32)[:, 0:1024],
           stg(sl)[:, :], bo_bc[:, :]) if False else None
        E_("vector", "tensor_add", stg(sl)[:, :], stg(sl)[:, :],
           bo_bc[:, :]) if False else None
        s2 = A(OF_EXPS, F32, 1024)
        WT("vector", x1st_free[0])
        WT("vector", x1st_free[1])
        E_("vector", "tensor_add", s2[:, 0:1024], stg(sl)[:, :], bo_bc[:, :])
        E_("vector", "tensor_add", stg(sl)[:, :], s2[:, 0:1024], xa(sl)[:, :],
           inc=dv)
        xin_free[sl] = (dv, dv.val)
        E_("vector", "reduce_sum", rcmb[:, 0:1], stg(sl)[:, :], axis=AX,
           inc=dv)
        sum_done = dv.val
        W_("scalar", dv, sum_done)
        E_("scalar", "activation", s2[:, 0:1024], stg(sl)[:, :], AF.Square,
           accum_out=rcmb[:, 1:2], drain=False, inc=ac)
        sq_done = ac.val
        W_("vector", ac, sq_done)
        E_("vector", "tensor_scalar", rcmb[:, 2:3], rcmb[:, 0:1],
           1.0 / D, None, ALU.mult)
        E_("vector", "tensor_scalar", rcmb[:, 3:4], rcmb[:, 1:2],
           1.0 / D, None, ALU.mult)
        E_("vector", "tensor_mul", rcmb[:, 4:5], rcmb[:, 2:3], rcmb[:, 2:3])
        E_("vector", "tensor_sub", rcmb[:, 5:6], rcmb[:, 3:4], rcmb[:, 4:5])
        E_("vector", "tensor_scalar", rcmb[:, 6:7], rcmb[:, 5:6],
           EPS, None, ALU.add, inc=dv)
        var_done = dv.val
        W_("scalar", dv, var_done)
        E_("scalar", "sqrt", rcmb[:, 7:8], rcmb[:, 6:7], drain=False, inc=ac)
        W_("vector", ac, ac.val)
        E_("vector", "reciprocal", rcmb[:, 8:9], rcmb[:, 7:8], inc=dv)
        W_("vector", dv, dv.val)
        E_("vector", "tensor_scalar", s2[:, 0:1024], stg(sl)[:, :],
           rcmb[:, 2:3], rcmb[:, 8:9], ALU.subtract, ALU.mult, inc=dv)
        x1_done = dv.val
        lnst_free[sl] = (dv, x1_done)
        W_("sync", dv, x1_done)
        WT("sync", x1st_free[sl])
        v1 = dma("sync", "x1f%d" % sl, x1_h[it * 128:(it + 1) * 128, :],
                 s2[:, 0:1024])
        x1st_free[sl] = (CH["x1f%d" % sl], v1)
        WT("vector", x1t_free[sl])
        E_("vector", "tensor_copy", stgb(sl)[:, :], s2[:, 0:1024], inc=dv)
        x1b_done = dv.val
        W_("sync", dv, x1b_done)
        WT("sync", x1b_free[sl])
        v2 = dma("sync", "x1b%d" % sl, x1bf_h[it * 128:(it + 1) * 128, :],
                 stgb(sl)[:, :])
        x1b_free[sl] = (CH["x1b%d" % sl], v2)
        W_("tensor", dv, x1_done)
        p = it % 2
        x1t32c = x1t32p(p)
        for tok in x1t32_free[p]:
            WT("scalar", tok)
        for dt_ in range(8):
            bk2 = 4 + (dt_ % 2)
            W_("tensor", ac, bank_free[bk2])
            E_("tensor", "transpose", pp[bk2][:, 0:128],
               s2[:, dt_ * 128:(dt_ + 1) * 128], idn32_s[:, :], inc=pe)
            W_("scalar", pe, pe.val)
            E_("scalar", "copy", x1t32c[:, dt_ * 128:(dt_ + 1) * 128],
               pp[bk2][:, 0:128], drain=False, inc=ac)
            bank_free[bk2] = ac.val
        x1t_done = ac.val
        W_("vector", ac, x1t_done)
        WT("vector", x1b_free[sl])
        E_("vector", "tensor_copy", stgb(sl)[:, :], x1t32c[:, :], inc=dv)
        x1tb_done = dv.val
        W_("sync", dv, x1tb_done)
        WT("sync", x1t_free[sl])
        v3 = dma("sync", "x1t%d" % sl,
                 AP(x1t_h.tensor, it * 128, [[N, 128], [128 * N, 8],
                                             [1, 128]]),
                 stgb(sl)[:, :])
        x1t_free[sl] = (CH["x1t%d" % sl], v3)

        def make_rt(it=it, p=p, x1t32c=x1t32c, x1t_done=x1t_done,
                    x1tb_done=x1tb_done):
            def emit():
                W_("tensor", ac, max(x1t_done, lgbank_free[0][1]))
                for kc in range(8):
                    E_("tensor", "matmul", pp[2][:, 0:8],
                       x1t32c[:, kc * 128:(kc + 1) * 128], wrT_s[:, kc * 8:kc * 8 + 8],
                       start=(kc == 0), stop=(kc == 7),
                       inc=(pe if kc == 7 else None))
                lg_done = pe.val
                W_("scalar", pe, lg_done)
                E_("scalar", "copy", rtr[:, 0:8], pp[2][:, 0:8], drain=False, inc=ac)
                lgbank_free[0] = (ac, ac.val)
                lgc_done = ac.val
                W_("vector", ac, lgc_done)
                E_("vector", "reduce_max", rtr[:, 8:9], rtr[:, 0:8], axis=AX)
                E_("vector", "tensor_scalar", rtr[:, 16:24], rtr[:, 0:8],
                   rtr[:, 8:9], None, ALU.is_equal)
                E_("vector", "tensor_tensor_scan", rtr[:, 24:32], ones8_s[:, :],
                   rtr[:, 16:24], 0.0, ALU.mult, ALU.add)
                E_("vector", "tensor_scalar", rtr[:, 32:40], rtr[:, 24:32],
                   1.0, None, ALU.is_equal)
                E_("vector", "tensor_mul", rtr[:, 40:48], rtr[:, 16:24],
                   rtr[:, 32:40])
                E_("vector", "tensor_scalar", rtr[:, 48:56], rtr[:, 40:48],
                   1e30, None, ALU.mult)
                E_("vector", "tensor_sub", rtr[:, 56:64], rtr[:, 0:8], rtr[:, 48:56])
                E_("vector", "reduce_max", rtr[:, 9:10], rtr[:, 56:64], axis=AX)
                E_("vector", "tensor_scalar", rtr[:, 64:72], rtr[:, 56:64],
                   rtr[:, 9:10], None, ALU.is_equal)
                E_("vector", "tensor_tensor_scan", rtr[:, 72:80], ones8_s[:, :],
                   rtr[:, 64:72], 0.0, ALU.mult, ALU.add)
                E_("vector", "tensor_scalar", rtr[:, 80:88], rtr[:, 72:80],
                   1.0, None, ALU.is_equal)
                E_("vector", "tensor_mul", rtr[:, 88:96], rtr[:, 64:72],
                   rtr[:, 80:88])
                E_("vector", "tensor_scalar", rtr[:, 10:11], rtr[:, 8:9],
                   -1.0, None, ALU.mult, inc=dv)
                t2a = dv.val
                W_("scalar", dv, t2a)
                E_("scalar", "activation", rtr[:, 96:104], rtr[:, 0:8], AF.Exp,
                   bias=rtr[:, 10:11], drain=False, inc=ac)
                e8_done = ac.val
                W_("vector", ac, e8_done)
                E_("vector", "tensor_mul", rtr[:, 104:112], rtr[:, 96:104],
                   rtr[:, 40:48])
                E_("vector", "reduce_sum", rtr[:, 11:12], rtr[:, 104:112], axis=AX)
                E_("vector", "tensor_mul", rtr[:, 112:120], rtr[:, 96:104],
                   rtr[:, 88:96])
                E_("vector", "reduce_sum", rtr[:, 12:13], rtr[:, 112:120], axis=AX)
                E_("vector", "tensor_add", rtr[:, 13:14], rtr[:, 11:12],
                   rtr[:, 12:13])
                E_("vector", "reciprocal", rtr[:, 14:15], rtr[:, 13:14])
                E_("vector", "tensor_mul", rtr[:, 15:16], rtr[:, 11:12],
                   rtr[:, 14:15])
                E_("vector", "tensor_mul", rtr[:, 120:121], rtr[:, 12:13],
                   rtr[:, 14:15])
                E_("vector", "tensor_scalar", rtr[:, 128:136], rtr[:, 40:48],
                   rtr[:, 15:16], None, ALU.mult)
                E_("vector", "scalar_tensor_tensor", rtr[:, 136:144],
                   rtr[:, 88:96], rtr[:, 120:121], rtr[:, 128:136],
                   ALU.mult, ALU.add)
                E_("vector", "tensor_copy", A(OF_ATTB, BF16, 256)[:, 0:8],
                   rtr[:, 136:144], inc=dv)
                wm8_done = dv.val
                W_("tensor", dv, wm8_done)
                bk2 = 4
                W_("tensor", ac, bank_free[bk2])
                E_("tensor", "transpose", ppb(bk2)[0:8, 0:128],
                   A(OF_ATTB, BF16, 256)[:, 0:8], idn_s[:, :], inc=pe)
                W_("scalar", pe, pe.val)
                if it % 4 == 0 and wm_free[0] is not None:
                    WT("scalar", wm_free[0])
                E_("scalar", "copy", wmst[0:8, (it % 4) * 128:(it % 4) * 128 + 128],
                   ppb(bk2)[0:8, 0:128], drain=False, inc=ac)
                bank_free[bk2] = ac.val
                if it % 4 == 3:
                    W_("sync", ac, ac.val)
                    if wm_free[0] is not None:
                        WT("sync", wm_free[0])
                    vw = dma("sync", "wm", wmT_h[:, (it - 3) * 128:(it + 1) * 128],
                             wmst[0:8, :])
                    wm_free[0] = (CH["wm"], vw)
                x1t32_free[p] = [(pe, lg_done),
                                 (dv, x1tb_done)]
            return emit
        if pending_rt[0] is not None:
            pending_rt[0]()
        pending_rt[0] = make_rt()
    if pending_rt[0] is not None:
        pending_rt[0]()
        pending_rt[0] = None
    if "x1" in dbg:
        W_("gpsimd", ch("x1f0"))
        W_("gpsimd", ch("x1f1"))
        dma("gpsimd", "dbg", dbg["x1"][:, :], x1_h[:, :])
    if "wm" in dbg:
        W_("gpsimd", ch("wm"))
        dma("gpsimd", "dbg", dbg["wm"][:, :], wmT_h[:, :])

    b.tag = "SCAN"
    W_("gpsimd", ch("wm"))
    E_("gpsimd", "collective_compute", "AllGather", ALU.bypass,
       [list(range(NCORE))], [wmT_h[:, :].opt()], [wall_h[:, :].opt()],
       inc=cc)
    cc_ag1 = cc.val
    W_("gpsimd", ch("x1b0"))
    W_("gpsimd", ch("x1b1"))
    E_("gpsimd", "collective_compute", "AllGather", ALU.bypass,
       [list(range(NCORE))], [x1bf_h[:, :].opt()], [x1g_h[:, :].opt()],
       inc=cc)
    cc_ag2 = cc.val
    msbv = A(OF_MSB, BF16, 8 * 256)
    posv = A(OF_POSS, F32, 8 * 256)
    keptv = A(OF_KEPT, F32, 8 * 256)
    scr = A(OF_SCR, F32, 8 * 256)
    idxa = A(OF_IDXA, F32, 16 * 256)
    eligb = A(OF_ELIGB, BF16, 256)
    bbf = A(OF_BBF, BF16, 256)
    onesr = A(OF_ONESR, F32, 256)
    csrow = A(OF_CSROW, F32, 512)

    W_("sync", pe, pe.val)
    W_("vector", pe, pe.val)
    W_("vector", ac, ac.val)
    W_("sync", cc, cc_ag1)
    W_("vector", cc, cc_ag1)
    xbv = 0
    for e in range(E):
        for r in range(NCORE):
            xbv = dma("sync", "xb", msbv[:, e * 256 + r * 32:
                                         e * 256 + (r + 1) * 32],
                      AP(wall_h.tensor, (8 * r + e) * N, [[128, 32], [1, 128]]),
                      transpose=True)
    E_("vector", "memset", onesr[0:1, :], 1.0)
    E_("vector", "memset", scr[:, 0:256], 0.0, inc=dv)
    W_("vector", ch("xb"), xbv)
    fb_cur = 0
    for e in range(E):
        mat = idxa[:, (8 + e) * 256:(9 + e) * 256]
        E_("vector", "tensor_copy", mat, msbv[:, e * 256:(e + 1) * 256])
        E_("vector", "tensor_scalar", scr[:, 768:1024], mat, 0.0, None,
           ALU.is_gt)
        E_("vector", "tensor_scalar", scr[:, 1280:1536],
           scr[:, fb_cur:fb_cur + 256], -1.0, 1.0, ALU.mult, ALU.add)
        E_("vector", "tensor_mul", scr[:, 1024:1280], scr[:, 768:1024],
           scr[:, 1280:1536])
        E_("vector", "tensor_copy", eligb[:, :], scr[:, 1024:1280], inc=dv)
        el_done = dv.val
        W_("tensor", dv, el_done)
        W_("tensor", ac, max(bank_free[0], bank_free[1]))
        E_("tensor", "matmul", pp[0][:, 0:256], ltx_s[:, :], eligb[:, :],
           start=True, stop=False)
        E_("tensor", "matmul", pp[1][0:1, 0:256], onecol_s[:, 0:1],
           eligb[:, :], start=True, stop=True, inc=pe)
        cs_done = pe.val
        W_("scalar", pe, cs_done)
        E_("scalar", "copy", csrow[0:1, 0:256], pp[1][0:1, 0:256], inc=ac)
        W_("vector", ac, ac.val)
        E_("vector", "tensor_tensor_scan", csrow[0:1, 256:512],
           onesr[0:1, :], csrow[0:1, 0:256], 0.0, ALU.mult, ALU.add)
        E_("vector", "memset", scr[0:1, 1536:1537], 0.0)
        E_("vector", "tensor_copy", scr[0:1, 1537:1792],
           csrow[0:1, 256:511])
        E_("vector", "tensor_copy", eligb[0:1, :], scr[0:1, 1536:1792])
        E_("vector", "tensor_copy", csrow[0:1, 0:256], eligb[0:1, :])
        E_("vector", "tensor_sub", scr[0:1, 1792:2048],
           scr[0:1, 1536:1792], csrow[0:1, 0:256])
        E_("vector", "tensor_copy", bbf[0:1, :], scr[0:1, 1792:2048],
           inc=dv)
        ex_done = dv.val
        W_("tensor", dv, ex_done)
        E_("tensor", "matmul", pp[0][:, 0:256], onez_s[:, :], eligb[0:1, :],
           start=False, stop=False)
        E_("tensor", "matmul", pp[0][:, 0:256], onez_s[:, :], bbf[0:1, :],
           start=False, stop=True, inc=pe)
        pos_done = pe.val
        W_("scalar", pe, pos_done)
        E_("scalar", "copy", posv[:, e * 256:(e + 1) * 256],
           pp[0][:, 0:256], inc=ac)
        bank_free[0] = ac.val
        bank_free[1] = ac.val
        posc_done = ac.val
        W_("vector", ac, posc_done)
        E_("vector", "tensor_scalar", scr[:, 1792:2048],
           posv[:, e * 256:(e + 1) * 256], float(CAP), None, ALU.is_lt)
        E_("vector", "tensor_mul", keptv[:, e * 256:(e + 1) * 256],
           scr[:, 1792:2048], scr[:, 1024:1280])
        E_("vector", "tensor_sub", scr[:, 1280:1536], scr[:, 1024:1280],
           keptv[:, e * 256:(e + 1) * 256])
        nfb = 256 if fb_cur == 0 else 0
        E_("vector", "tensor_add", scr[:, nfb:nfb + 256],
           scr[:, fb_cur:fb_cur + 256], scr[:, 1280:1536], inc=dv)
        fb_cur = nfb
    for col in range(7):
        E_("vector", "memset", idxa[:, col * 256:(col + 1) * 256], 0.0)
    E_("vector", "memset", idxa[:, 7 * 256:8 * 256], 0.0, inc=dv)
    for e in range(E):
        mat = idxa[:, (8 + e) * 256:(9 + e) * 256]
        E_("vector", "tensor_scalar", scr[:, 768:1024], mat, 0.0, None,
           ALU.is_gt)
        E_("vector", "tensor_scalar", scr[:, 1024:1280],
           idxa[:, 0:256], 0.0, None, ALU.is_equal)
        E_("vector", "tensor_mul", scr[:, 1280:1536], scr[:, 768:1024],
           scr[:, 1024:1280])
        E_("vector", "tensor_scalar", scr[:, 1024:1280],
           idxa[:, 0:256], 1.0, None, ALU.is_equal)
        E_("vector", "tensor_mul", scr[:, 1536:1792], scr[:, 768:1024],
           scr[:, 1024:1280])
        for (facc, base) in ((scr[:, 1280:1536], 1), (scr[:, 1536:1792], 4)):
            E_("vector", "scalar_tensor_tensor",
               idxa[:, base * 256:(base + 1) * 256], facc, float(e),
               idxa[:, base * 256:(base + 1) * 256], ALU.mult, ALU.add) \
                if False else None
            E_("vector", "scalar_tensor_tensor", scr[:, 1024:1280],
               facc, float(e), idxa[:, base * 256:(base + 1) * 256],
               ALU.mult, ALU.add)
            E_("vector", "tensor_copy", idxa[:, base * 256:(base + 1) * 256],
               scr[:, 1024:1280])
            E_("vector", "tensor_mul", scr[:, 1024:1280], facc,
               posv[:, e * 256:(e + 1) * 256])
            E_("vector", "tensor_add", scr[:, 1792:2048], scr[:, 1024:1280],
               idxa[:, (base + 1) * 256:(base + 2) * 256])
            E_("vector", "tensor_copy",
               idxa[:, (base + 1) * 256:(base + 2) * 256],
               scr[:, 1792:2048])
            E_("vector", "tensor_mul", scr[:, 1024:1280], facc, mat)
            E_("vector", "tensor_add", scr[:, 1792:2048], scr[:, 1024:1280],
               idxa[:, (base + 2) * 256:(base + 3) * 256])
            E_("vector", "tensor_copy",
               idxa[:, (base + 2) * 256:(base + 3) * 256],
               scr[:, 1792:2048])
        E_("vector", "tensor_add", scr[:, 1024:1280], idxa[:, 0:256],
           scr[:, 768:1024])
        E_("vector", "tensor_copy", idxa[:, 0:256], scr[:, 1024:1280],
           inc=dv)
    E_("vector", "memset", scr[:, 1792:2048], 0.0)
    for e in range(E):
        E_("vector", "tensor_scalar", scr[:, 1024:1280],
           posv[:, e * 256:(e + 1) * 256], -float(CAP), None, ALU.add)
        E_("vector", "tensor_mul", scr[:, 1280:1536],
           keptv[:, e * 256:(e + 1) * 256], scr[:, 1024:1280])
        E_("vector", "tensor_scalar", scr[:, 1024:1280], scr[:, 1280:1536],
           float(CAP), None, ALU.add)
        E_("vector", "tensor_scalar", scr[:, 1280:1536], scr[:, 1024:1280],
           coreoh_bc[:, e:e + 1], None, ALU.mult)
        E_("vector", "tensor_add", scr[:, 1536:1792], scr[:, 1792:2048],
           scr[:, 1280:1536])
        E_("vector", "tensor_copy", scr[:, 1792:2048], scr[:, 1536:1792])
    E_("vector", "tensor_copy", slotc_s[:, :], scr[:, 1792:2048])
    E_("vector", "tensor_copy", slotc32_s[:, :], slotc_s[:, :],
       drain=True, inc=dv)
    slotc_done = dv.val
    sl32_done = dv.val
    if "slotc" in dbg:
        W_("gpsimd", dv, slotc_done)
        dma("gpsimd", "dbg", dbg["slotc"][:, :], slotc_s[:, :])
    if "pos" in dbg:
        W_("gpsimd", ac, ac.val)
        dma("gpsimd", "dbg", dbg["pos"][:, :], posv)
    if "kept" in dbg:
        W_("gpsimd", dv, dv.val)
        dma("gpsimd", "dbg", dbg["kept"][:, :], keptv)
    iscr = A(OF_SCR + 1536 * 4, I32, 256)
    ihi = A(OF_SCR + 256 * 4, I32, 256)
    ilo = A(OF_SCR + 512 * 4, I32, 256)
    E_("vector", "tensor_scalar", scr[:, 1024:1280],
       scr[:, fb_cur:fb_cur + 256], -1.0, 1.0, ALU.mult, ALU.add)
    E_("vector", "tensor_scalar", scr[:, 1280:1536],
       scr[:, fb_cur:fb_cur + 256], float(FBROW), None, ALU.mult)
    for r_i, (ecol, pcol) in enumerate(((1, 2), (4, 5))):
        E_("vector", "tensor_copy", iscr[:, :],
           idxa[:, pcol * 256:(pcol + 1) * 256])
        E_("vector", "tensor_scalar", ihi[:, :], iscr[:, :], 10, None,
           ALU.arith_shift_right)
        E_("vector", "tensor_scalar", ilo[:, :], iscr[:, :], 1023, None,
           ALU.bitwise_and)
        E_("vector", "tensor_copy", scr[:, 768:1024], ihi[:, :])
        E_("vector", "tensor_copy", scr[:, 1792:2048], ilo[:, :])
        E_("vector", "scalar_tensor_tensor", scr[:, 1536:1792],
           idxa[:, ecol * 256:(ecol + 1) * 256], 1024.0,
           scr[:, 1792:2048], ALU.mult, ALU.add)
        E_("vector", "scalar_tensor_tensor", scr[:, 512:768],
           scr[:, 768:1024], 8192.0, scr[:, 1536:1792],
           ALU.mult, ALU.add)
        E_("vector", "tensor_mul", idxa[:, 7 * 256:8 * 256],
           scr[:, 512:768], scr[:, 1024:1280])
        E_("vector", "tensor_add", idxa[:, r_i * 256:(r_i + 1) * 256],
           idxa[:, 7 * 256:8 * 256], scr[:, 1280:1536], inc=dv)
    rs_arrays = [idxa[:, 0:256], idxa[:, 256:512], idxa[:, 3 * 256:4 * 256],
                 idxa[:, 6 * 256:7 * 256], scr[:, fb_cur:fb_cur + 256]]
    rs_done = dv.val
    W_("tensor", dv, rs_done)
    rsst = 0
    for arr_i, arr in enumerate(rs_arrays):
        for hhf in range(2):
            bk2 = 4 + (hhf % 2)
            W_("tensor", ac, bank_free[bk2])
            E_("tensor", "transpose", pp[bk2][:, 0:128],
               arr[:, hhf * 128:(hhf + 1) * 128], idn32_s[:, :], inc=pe)
            W_("scalar", pe, pe.val)
            W_("scalar", ch("rs"), rsst)
            E_("scalar", "activation", csrow[0:128, 0:128],
               pp[bk2][:, 0:128], AF.Identity, scale=0.125, inc=ac)
            bank_free[bk2] = ac.val
            W_("sync", ac, ac.val)
            rsst = dma("sync", "rs",
                       AP(rsi_h.tensor,
                          ((4 * hhf) * 160 + arr_i * 32) * 128,
                          [[160 * 128, 4], [128, 32], [1, 128]]),
                       csrow[0:128, 0:128])
    W_("gpsimd", ch("rs"))
    E_("gpsimd", "collective_compute", "ReduceScatter", ALU.add,
       [list(range(NCORE))], [rsi_h[:, :].opt()], [rso_h[:, :].opt()],
       inc=cc)
    cc_rs = cc.val
    if "rso" in dbg:
        W_("gpsimd", cc, cc_rs)
        dma("gpsimd", "dbg", dbg["rso"][:, :], rso_h[:, :])

    b.tag = "FB"

    fwa = A(OF_WA, BF16, 8 * 4096)
    fwb = A(OF_WB, BF16, 32 * 1024)
    hts = A(OF_HTS, BF16, 32 * 256)

    def xet(slot):
        return A(OF_XET + slot * 8 * 256 * 2, BF16, 8 * 256)

    def fsb(slot):
        return A(OF_FSB + slot * 1024 * 2, BF16, 1024)

    W_("sync", pe, pe.val)
    W_("sync", dv, dv.val)
    W_("sync", ac, ac.val)
    W_("gpsimd", pe, pe.val)
    W_("gpsimd", dv, dv.val)
    W_("gpsimd", ac, ac.val)
    for cn in ("x1t0", "x1t1", "x1f0", "x1f1", "x1b0", "x1b1", "wm"):
        W_("sync", ch(cn))
        W_("gpsimd", ch(cn))

    def ffn_block(wname1, wname2, w1ap, w2ap, b1col, b2row, xt_src, nchunks,
                  ydst, ych, aux=None, agchunk=None):
        w1ch = ch(wname1)
        w2ch = ch(wname2)
        for cn in ("yf0", "yf1", "ye0", "ye1", "yz",
                   "xg0", "xg1", "xg2", "xg3", "xg4", "xg5"):
            if cn in CH:
                W_("scalar", ch(cn))
                W_("vector", ch(cn))
                W_("sync", ch(cn))
        W_("gpsimd", pe, pe.val)
        W_("gpsimd", ac, ac.val)
        W_("tensor", ch("b2r"))
        for iik in range(32):
            E_("gpsimd", "dma_start",
               out=fwb[:, iik * 1024:(iik + 1) * 1024],
               in_=w2ap[iik * 128:(iik + 1) * 128, :], inc=w2ch)
        w2_ld = w2ch.val
        W_("gpsimd", ch("rs"))
        for kc in range(8):
            for hf in range(2):
                E_("gpsimd", "dma_start",
                   out=fwa[:, kc * 4096 + hf * 2048:kc * 4096 + (hf + 1) * 2048],
                   in_=w1ap[kc * 128:(kc + 1) * 128,
                            hf * 2048:(hf + 1) * 2048], inc=w1ch)
        w1_ld = w1ch.val
        xet_free = [None, None]
        ybank_free = {4: (ac, ac.val), 5: (ac, ac.val),
                      6: (ac, ac.val), 7: (ac, ac.val)}
        yst_free = [None, None]
        w2_done_prev = (pe, pe.val)
        for tc in range(nchunks):
            xsl = tc % 2
            WT("sync", xet_free[xsl])
            _, xtoks = xt_src(tc, xsl)
            if tc == 0:
                W_("tensor", w1ch, w1_ld)
            for xtok in xtoks:
                W_("tensor", xtok[0], xtok[1])
            for iic in range(32):
                if aux is not None:
                    aux()
                bk0 = iic % 4
                W_("tensor", ac, bank_free[bk0])
                for kc in range(8):
                    E_("tensor", "matmul", pp[bk0][:, 0:256],
                       fwa[:, kc * 4096 + iic * 128:kc * 4096 + (iic + 1) * 128],
                       xet(xsl)[:, kc * 256:(kc + 1) * 256],
                       start=(kc == 0), stop=(kc == 7),
                       inc=(pe if kc == 7 else None))
                if iic == 31:
                    xet_free[xsl] = (pe, pe.val)
                W_("scalar", pe, pe.val)
                if iic == 0:
                    WT("scalar", w2_done_prev)
                E_("scalar", "activation", hts[:, iic * 256:(iic + 1) * 256],
                   pp[bk0][:, 0:256], AF.Gelu, bias=b1col[:, iic:iic + 1],
                   drain=False, inc=ac)
                bank_free[bk0] = ac.val
            hts_done = ac.val
            if tc == 0:
                W_("tensor", w2ch, w2_ld)
            for tt4 in range(2):
                tt = tc * 2 + tt4
                ysl = tt % 2
                for oc in range(2):
                    bky = 4 + 2 * tt4 + oc
                    WT("tensor", ybank_free[bky])
                    if tt4 == 0 and oc == 0:
                        W_("tensor", ac, hts_done)
                    for iik in range(32):
                        E_("tensor", "matmul", pp[bky][:, :],
                           hts[:, iik * 256 + tt4 * 128:
                               iik * 256 + tt4 * 128 + 128],
                           fwb[:, iik * 1024 + oc * 512:
                               iik * 1024 + (oc + 1) * 512],
                           start=(iik == 0), stop=False)
                    E_("tensor", "matmul", pp[bky][:, :], onez_s[:, :],
                       b2row[0:1, oc * 512:(oc + 1) * 512],
                       start=False, stop=True, inc=pe)
                    if tt4 == 1 and oc == 1:
                        w2_done_prev = (pe, pe.val)
                    W_("scalar", pe, pe.val)
                    if oc == 0:
                        WT("scalar", yst_free[ysl])
                    E_("scalar", "copy", fsb(ysl)[:, oc * 512:(oc + 1) * 512],
                       pp[bky][:, :], drain=False, inc=ac)
                    ybank_free[bky] = (ac, ac.val)
                W_("sync", ac, ac.val)
                v = dma("sync", ych + str(ysl),
                        ydst[tt * 128:(tt + 1) * 128, :], fsb(ysl)[:, :])
                yst_free[ysl] = (CH[ych + str(ysl)], v)
                if agchunk is not None:
                    agchunk(tt)

    def fb_xt(tc, xsl):
        toks = []
        for q in range(2):
            chn = "xt%d_%d" % (xsl, q)
            v = dma("sync", chn, xet(xsl)[:, q * 1024:(q + 1) * 1024],
                    AP(x1t_h.tensor, tc * 256 + q * 4 * 128 * N,
                       [[N, 128], [128 * N, 4], [1, 256]]))
            toks.append((ch(chn), v))
        return None, toks

    def scst(gsl):
        return A(OF_SCST + gsl * 1024 * 2, BF16, 1024)

    sc_state = {"n": 0, "c": 0}
    xg_free = [None] * 6

    def scatter_aux():
        sc_state["n"] += 1
        if sc_state["n"] <= 128:
            return
        c = sc_state["c"]
        if c >= 256:
            return
        gsl = c % 6
        if c == 0:
            W_("sync", cc, cc_ag2)
            W_("gpsimd", dv, sl32_done)
        WT("sync", xg_free[gsl])
        v = dma("sync", "xg%d" % gsl, scst(gsl)[:, :],
                x1g_h[c * 128:(c + 1) * 128, :])
        W_("gpsimd", ch("xg%d" % gsl), v)
        E_("gpsimd", "indirect_dma_start",
           out=xe_h[:, :],
           out_offset=IndirectOffsetOnAxis(slotc32_s[:, c:c + 1], 0),
           in_=scst(gsl)[:, :], in_offset=None, inc=ch("sc%d" % gsl))
        xg_free[gsl] = (ch("sc%d" % gsl), ch("sc%d" % gsl).val)
        sc_state["c"] = c + 1

    ffn_block("fw1", "fw2", fw1_i, fw2_i, fb1_s, fb2row_s, fb_xt, 16,
              yfb_h, "yf", aux=scatter_aux)
    assert sc_state["c"] == 256, sc_state
    if "yfb" in dbg:
        W_("gpsimd", ch("yf0"))
        W_("gpsimd", ch("yf1"))
        dma("gpsimd", "dbg", dbg["yfb"][:, :], yfb_h[:, :])

    b.tag = "EXP"
    W_("vector", dv, dv.val)
    for cn in ("yf0", "yf1", "hs0", "hs1"):
        if cn in CH:
            W_("vector", ch(cn))
    E_("vector", "memset", fsb(0)[:, :], 0.0, inc=dv)
    zb_done = dv.val
    W_("sync", dv, zb_done)
    for blk in range(8):
        dma("sync", "yz", yall_h[FBROW + blk * 128:FBROW + (blk + 1) * 128, :],
            fsb(0)[:, :])
    for cn in ("sc0", "sc1", "sc2", "sc3", "sc4", "sc5"):
        W_("sync", ch(cn))
    if "xe" in dbg:
        for cn in ("sc0", "sc1", "sc2", "sc3", "sc4", "sc5"):
            W_("gpsimd", ch(cn))
        dma("gpsimd", "dbg", dbg["xe"][:, :], xe_h[:, :])

    def ex_xt(tc, xsl):
        vq = {}
        for dt_ in range(8):
            chn = "xt%d_%d" % (xsl, dt_ % 2)
            vq[chn] = dma("sync", chn,
                          xet(xsl)[:, dt_ * 256:(dt_ + 1) * 256],
                          AP(xe_h.tensor, tc * 256 * D + dt_ * 128,
                             [[D, 256], [1, 128]]),
                          transpose=True)
        return None, [(ch(c), v) for c, v in vq.items()]

    agst = {"k": 0}

    def ag3_chunk(tt):
        if (tt + 1) % 8:
            return
        k = tt // 8
        W_("gpsimd", CH["ye0"])
        W_("gpsimd", CH["ye1"])
        E_("gpsimd", "collective_compute", "AllGather", ALU.bypass,
           [list(range(NCORE))],
           [AP(ye_h.tensor, k * 1024 * D, [[1, 1024 * D]])],
           [AP(yall_h.tensor, k * 8192 * D, [[1, 8192 * D]])],
           inc=cc)
        agst["k"] = k + 1

    ffn_block("w1e", "w2e", w1e_i, w2e_i, b1_s, b2row_s, ex_xt, 40,
              ye_h, "ye", agchunk=ag3_chunk)
    assert agst["k"] == 10, agst
    cc_ag3 = cc.val
    if "ye" in dbg:
        for cn in ("ye0", "ye1"):
            W_("gpsimd", ch(cn))
        dma("gpsimd", "dbg", dbg["ye"][:, :], ye_h[:, :])
        W_("gpsimd", ch("dbg"))

    b.tag = "CMB"
    OF_CHT = 32 * 1024
    locv = A(OF_LOC, F32, 8 * 32)
    li32 = A(OF_LI32, I32, 64)

    def g0b(slot):
        return A(OF_G0 + slot * D * 2, BF16, D)

    def g1b(slot):
        return A(OF_G1R + slot * D * 2, BF16, D)

    def yfbb(slot):
        return A(OF_YFB + slot * D * 2, BF16, D)

    def cx1(slot):
        return A(OF_CX1 + slot * D * 4, F32, D)

    def cst(slot):
        return A(OF_CST + slot * D * 4, F32, D)

    W_("sync", cc, cc_rs)
    W_("sync", pe, pe.val)
    W_("sync", ac, ac.val)
    W_("sync", dv, dv.val)
    W_("vector", pe, pe.val)
    W_("vector", ac, ac.val)
    W_("scalar", pe, pe.val)
    W_("scalar", dv, dv.val)
    for cn in ("ye0", "ye1", "xt0_0", "xt0_1", "xt1_0", "xt1_1"):
        if cn in CH:
            W_("vector", ch(cn))
            W_("sync", ch(cn))
    W_("gpsimd", cc, cc_ag3)
    W_("gpsimd", ch("yz"))
    rlv = 0
    for arr_i in range(5):
        rlv = dma("sync", "rl2", csrow[0:32, 0:128] if False else
                  A(OF_SCR, F32, 128, rows=32)[0:32, :],
                  rso_h[arr_i * 32:(arr_i + 1) * 32, :])
        W_("tensor", ch("rl2"), rlv)
        W_("tensor", ac, bank_free[4])
        E_("tensor", "transpose", pp[4][:, 0:32],
           A(OF_SCR, F32, 128, rows=32)[0:32, :], idn32_s[0:32, 0:32],
           inc=pe)
        W_("scalar", pe, pe.val)
        E_("scalar", "activation", locv[:, arr_i * 32:(arr_i + 1) * 32],
           pp[4][:, 0:32], AF.Identity, scale=1.0, inc=ac)
        bank_free[4] = ac.val
        W_("sync", pe, pe.val)
    loc_done = ac.val
    W_("vector", ac, loc_done)
    E_("vector", "tensor_copy", li32[:, 0:32], locv[:, 0:32])
    E_("vector", "tensor_copy", li32[:, 32:64], locv[:, 32:64], inc=dv)
    li_done = dv.val
    g_free = [None, None, None, None]
    yf_free = [None, None]
    cx_free = [None, None]
    cs_free = [None, None]
    ost_free = [None, None]
    W_("gpsimd", dv, li_done)
    for lt in range(NT):
        sl = lt % 2
        WT("gpsimd", g_free[sl])
        E_("gpsimd", "indirect_dma_start",
           out=g0b(sl)[:, :], out_offset=None, in_=yall_h[:, :],
           in_offset=IndirectOffsetOnAxis(li32[:, lt:lt + 1], 0),
           inc=ch("cg0%d" % sl))
        WT("gpsimd", g_free[2 + sl])
        E_("gpsimd", "indirect_dma_start",
           out=g1b(sl)[:, :], out_offset=None, in_=yall_h[:, :],
           in_offset=IndirectOffsetOnAxis(li32[:, 32 + lt:33 + lt], 0),
           inc=ch("cg1%d" % sl))
        WT("sync", yf_free[sl])
        yfv = dma("sync", "cy%d" % sl, yfbb(sl)[:, :],
                  yfb_h[lt * 128:(lt + 1) * 128, :])
        WT("sync", cx_free[sl])
        cxv = dma("sync", "cx%d" % sl, cx1(sl)[:, :],
                  x1_h[lt * 128:(lt + 1) * 128, :])
        W_("vector", ch("cg0%d" % sl), CH["cg0%d" % sl].val)
        W_("vector", ch("cg1%d" % sl), CH["cg1%d" % sl].val)
        W_("vector", ch("cy%d" % sl), yfv)
        W_("vector", ch("cx%d" % sl), cxv)
        WT("vector", cs_free[sl])
        s2c = cst(sl)
        E_("vector", "tensor_scalar", s2c[:, :], g0b(sl)[:, :],
           locv[:, 64 + lt:65 + lt], None, ALU.mult)
        E_("vector", "scalar_tensor_tensor",
           A(OF_CHT, F32, 1024)[:, :], g1b(sl)[:, :],
           locv[:, 96 + lt:97 + lt], s2c[:, :], ALU.mult, ALU.add)
        E_("vector", "tensor_sub", s2c[:, :], yfbb(sl)[:, :],
           A(OF_CHT, F32, 1024)[:, :], inc=dv)
        g_free[sl] = (dv, dv.val)
        g_free[2 + sl] = (dv, dv.val)
        yf_free[sl] = (dv, dv.val)
        E_("vector", "scalar_tensor_tensor",
           A(OF_CHT + 4096, F32, 1024)[:, :], s2c[:, :],
           locv[:, 128 + lt:129 + lt], A(OF_CHT, F32, 1024)[:, :],
           ALU.mult, ALU.add)
        E_("vector", "tensor_add", s2c[:, :],
           A(OF_CHT + 4096, F32, 1024)[:, :], cx1(sl)[:, :], inc=dv)
        cx_free[sl] = (dv, dv.val)
        E_("vector", "reduce_sum", rcmb[:, 0:1], s2c[:, :], axis=AX, inc=dv)
        W_("scalar", dv, dv.val)
        E_("scalar", "activation", A(OF_CHT, F32, 1024)[:, :], s2c[:, :],
           AF.Square, accum_out=rcmb[:, 1:2], inc=ac)
        W_("vector", ac, ac.val)
        E_("vector", "tensor_scalar", rcmb[:, 2:3], rcmb[:, 0:1],
           1.0 / D, None, ALU.mult)
        E_("vector", "tensor_scalar", rcmb[:, 3:4], rcmb[:, 1:2],
           1.0 / D, None, ALU.mult)
        E_("vector", "tensor_mul", rcmb[:, 4:5], rcmb[:, 2:3], rcmb[:, 2:3])
        E_("vector", "tensor_sub", rcmb[:, 5:6], rcmb[:, 3:4], rcmb[:, 4:5])
        E_("vector", "tensor_scalar", rcmb[:, 6:7], rcmb[:, 5:6],
           EPS, None, ALU.add, inc=dv)
        W_("scalar", dv, dv.val)
        E_("scalar", "sqrt", rcmb[:, 7:8], rcmb[:, 6:7], drain=False, inc=ac)
        W_("vector", ac, ac.val)
        E_("vector", "reciprocal", rcmb[:, 8:9], rcmb[:, 7:8], inc=dv)
        W_("vector", dv, dv.val)
        E_("vector", "tensor_scalar", A(OF_CHT, F32, 1024)[:, :], s2c[:, :],
           rcmb[:, 2:3], rcmb[:, 8:9], ALU.subtract, ALU.mult, inc=dv)
        out_done = dv.val
        cs_free[sl] = (dv, out_done)
        W_("sync", dv, out_done)
        WT("sync", ost_free[sl])
        ov = dma("sync", "o%d" % sl, out[lt * 128:(lt + 1) * 128, :],
                 A(OF_CHT, F32, 1024)[:, :], )
        ost_free[sl] = (CH["o%d" % sl], ov)
        WT("vector", ost_free[sl])
    W_("gpsimd", ch("o0"))
    W_("gpsimd", ch("o1"))


    for name, c in CH.items():
        if name == "dbg":
            W_("gpsimd", c)
    W_("gpsimd", od, od.val)
    with es, nc.Block() as block:
        @block.sync
        def _(h):
            for f in b.q["sync"]:
                f(h)

        @block.gpsimd
        def _(h):
            for f in b.q["gpsimd"]:
                f(h)

        @block.scalar
        def _(h):
            for f in b.q["scalar"]:
                f(h)

        @block.vector
        def _(h):
            for f in b.q["vector"]:
                f(h)

        @block.tensor
        def _(h):
            for f in b.q["tensor"]:
                f(h)

    return nc, dbg



def _host_constants():
    eye = np.eye(128)
    cons = {
        "idn": eye.astype(bfloat16),
        "idn32": eye.astype(np.float32),
        "ltx": np.triu(np.ones((128, 128)), 1).astype(bfloat16),
        "anti": eye[::-1].copy().astype(bfloat16),
        "onez": np.ones((1, 128), dtype=bfloat16),
        "onez32": np.ones((1, 128), dtype=np.float32),
        "eidx": np.broadcast_to(np.arange(8, dtype=np.float32),
                                (128, 8)).copy(),
    }
    bt = _bucket_table()
    g1 = np.zeros((384, NREL), dtype=bfloat16)
    g1[bt, np.arange(NREL)] = bfloat16(1.0)
    cons["g1h"] = g1
    return cons


def make_in_maps(inputs):
    cons = _host_constants()
    gc = np.asarray(inputs["L0_attn_gru_const"], np.float32).reshape(H)
    gru_c = np.zeros((48, 1), np.float32)
    gru_c[0:16, 0] = gc
    gru_c[32:48, 0] = gc
    in_maps = []
    for c in range(NCORE):
        m = {
            "x": np.ascontiguousarray(
                np.asarray(inputs["L0_attn_hidden_states"],
                           np.float32)[c]),
            "wq": np.asarray(inputs["L0_attn_Wq"], np.float32),
            "wk": np.asarray(inputs["L0_attn_Wk"], np.float32),
            "wv": np.asarray(inputs["L0_attn_Wv"], np.float32),
            "wo": np.asarray(inputs["L0_attn_Wo"], np.float32),
            "bq": np.asarray(inputs["L0_attn_bq"], np.float32),
            "bk": np.asarray(inputs["L0_attn_bk"], np.float32),
            "bv": np.asarray(inputs["L0_attn_bv"], np.float32),
            "bo": np.asarray(inputs["L0_attn_bo"], np.float32),
            "gru_w": np.asarray(inputs["L0_attn_gru_W"], np.float32),
            "gru_b": np.asarray(inputs["L0_attn_gru_b"],
                                np.float32).reshape(1, 8),
            "gru_c": gru_c,
            "rel_embed": np.asarray(inputs["L0_attn_rel_embed"], np.float32),
            "wr": np.asarray(inputs["L0_ffn_Wr"], np.float32),
            "w1e": np.ascontiguousarray(
                np.asarray(inputs["L0_ffn_W1"], np.float32)[c]),
            "b1e": np.ascontiguousarray(
                np.asarray(inputs["L0_ffn_b1"], np.float32)[c]),
            "w2e": np.ascontiguousarray(
                np.asarray(inputs["L0_ffn_W2"], np.float32)[c]),
            "b2e": np.ascontiguousarray(
                np.asarray(inputs["L0_ffn_b2"], np.float32)[c]),
            "fw1": np.asarray(inputs["L0_ffn_fw1"], np.float32),
            "fb1": np.asarray(inputs["L0_ffn_fb1"], np.float32),
            "fw2": np.asarray(inputs["L0_ffn_fw2"], np.float32),
            "fb2": np.asarray(inputs["L0_ffn_fb2"], np.float32),
            "coreoh": np.eye(8, dtype=np.float32)[c:c + 1],
        }
        m.update(cons)
        in_maps.append(m)
    return in_maps


def kernel(**inputs):
    nc, dbg = build_program()
    in_maps = make_in_maps(inputs)
    res = run_bass_kernel_spmd(nc, in_maps, core_ids=list(range(NCORE)))
    outs = np.stack([np.asarray(r["out"]) for r in res.results], axis=0)
    if DBG:
        kernel.dbg_results = [
            {k: np.asarray(r["dbg_" + k]) for k in DBG}
            for r in res.results]
    return outs
